# version 12
# baseline (speedup 1.0000x reference)
import numpy as np
import ml_dtypes
import concourse.bass as bass
import concourse.mybir as mybir
from concourse.bass_utils import run_bass_kernel_spmd

F32 = mybir.dt.float32
BF16 = mybir.dt.bfloat16
AF = mybir.ActivationFunctionType
ALU = mybir.AluOpType
AX = mybir.AxisListType

NCORES = 8
BPC = 2
S = 2048
D = 1024
DC = D // 128
TT = S // 128
TW = S // 512
LN_EPS = 1e-5
RMS_EPS = 1e-5
FFN_F = 2048
FC = FFN_F // 128

bf16 = ml_dtypes.bfloat16


def host_constants():
    c = {}
    c["ident_bf"] = np.eye(128, dtype=np.float32).astype(bf16)
    c["ident_f"] = np.eye(128, dtype=np.float32)
    P = np.zeros((128, 128), np.float32)
    for r in range(128):
        P[r, r ^ 32] = 1.0
    c["perm32"] = P.astype(bf16)
    hd = 64
    inv_freq = 1.0 / (10000.0 ** (np.arange(0, hd, 2, dtype=np.float64) / hd))
    freqs = np.arange(S, dtype=np.float64)[:, None] * inv_freq[None, :]
    ridx = np.arange(128) % 32
    sign = np.where((np.arange(128) % 64) < 32, -1.0, 1.0)
    c["rope_c"] = np.ascontiguousarray(np.cos(freqs)[:, ridx].T).astype(np.float32)
    c["rope_s"] = np.ascontiguousarray((sign[:, None] * np.sin(freqs)[:, ridx].T)).astype(np.float32)
    r = np.arange(128)[:, None]
    cc = np.arange(512)[None, :]
    c["cmask"] = np.stack([(r + 128 * jj <= cc) for jj in range(4)]).astype(bf16)
    N, F = 1024, 513
    d = np.arange(N)[:, None].astype(np.float64)
    f = np.arange(F)[None, :].astype(np.float64)
    ang = 2.0 * np.pi * d * f / N
    c["dft_c"] = np.cos(ang).astype(bf16)
    c["dft_s"] = (-np.sin(ang)).astype(bf16)
    w = 2.0 - (np.arange(F) == 0) - (np.arange(F) == 512)
    irfa = np.zeros((640, 1024), np.float64)
    irfb = np.zeros((640, 1024), np.float64)
    irfa[:513] = w[:, None] * np.cos(ang.T) / N
    irfb[:513] = -w[:, None] * np.sin(ang.T) / N
    c["irf_a"] = irfa.astype(bf16)
    c["irf_b"] = irfb.astype(bf16)
    c["ones64"] = np.ones((64, 1), np.float32).astype(bf16)
    c["ones_f"] = np.ones((1, 128), np.float32)
    return c


class Sem:
    def __init__(self, nc, name):
        self.h = nc.alloc_semaphore(name)
        self.n = 0


class KB:
    ENGINES = ("sync", "scalar", "vector", "tensor", "gpsimd")

    def __init__(self, nc):
        self.nc = nc
        self.q = {e: [] for e in self.ENGINES}
        self.sem = {}
        for name in ("dW", "dA", "dO", "T", "V", "Sc", "G", "Bs"):
            self.sem[name] = Sem(nc, "sem_" + name)
        self.dsems = {}

    def op(self, eng, fn, waits=(), inc=None, inc_by=1):
        if inc is not None:
            inc.n += inc_by
        ws = tuple((s.h, v) for (s, v) in waits if s is not None)

        def th(e, fn=fn, ws=ws, inc=inc, inc_by=inc_by):
            for h, v in ws:
                e.wait_ge(h, v)
            ins = fn(e)
            if inc is not None:
                ins.then_inc(inc.h, inc_by)
        self.q[eng].append(th)
        return (inc, inc.n) if inc is not None else None

    def wait(self, eng, waits):
        ws = tuple((s.h, v) for (s, v) in waits if s is not None)

        def th(e, ws=ws):
            for h, v in ws:
                e.wait_ge(h, v)
        self.q[eng].append(th)

    def drain(self, eng):
        self.q[eng].append(lambda e: e.drain())

    def dstream(self, name):
        if name not in self.dsems:
            self.dsems[name] = Sem(self.nc, "d_" + name)
        return self.dsems[name]

    def barrier(self):
        toks = []
        pairs = (("sync", "Bs"), ("scalar", "Sc"), ("vector", "V"),
                 ("tensor", "T"), ("gpsimd", "G"))
        for eng, sname in pairs:
            if eng == "sync":
                toks.append(self.op(eng, lambda e: e.nop(), inc=self.sem[sname]))
            else:
                toks.append(self.op(eng, lambda e: e.drain(), inc=self.sem[sname]))
        toks.append((self.sem["dO"], self.sem["dO"].n))
        toks.append((self.sem["dW"], self.sem["dW"].n))
        for nm, ds in self.dsems.items():
            if not nm.startswith("prep"):
                toks.append((ds, ds.n))
        for eng, _ in pairs:
            self.wait(eng, toks)

    def finish(self, final_waits=()):
        self.wait("sync", list(final_waits))
        with self.nc.Block() as blk:
            for eng in self.ENGINES:
                thunks = self.q[eng]

                def body(e, thunks=thunks):
                    for th in thunks:
                        th(e)
                getattr(blk, eng)(body)


class Alloc:
    def __init__(self, nc):
        self.nc = nc
        self.base = ((nc.sbuf_base + 31) // 32) * 32
        self.PART_BYTES = nc.sbuf_top - 64
        self.per_off = self.base
        self.stage_off = None
        self.uid = 0

    def _alloc(self, name, shape, dtype, off):
        self.uid += 1
        nbytes = int(np.prod(shape[1:])) * (2 if dtype == BF16 else 4)
        t = self.nc.alloc_sbuf_tensor_at(f"{name}_{self.uid}", list(shape), dtype, offset=off)
        return t, off + ((nbytes + 31) // 32) * 32

    def perm(self, name, shape, dtype):
        t, self.per_off = self._alloc(name, shape, dtype, self.per_off)
        assert self.per_off <= self.PART_BYTES, f"persistent SBUF overflow {self.per_off}"
        return t

    def begin_stage(self):
        self.stage_off = self.per_off

    def stage(self, name, shape, dtype):
        t, self.stage_off = self._alloc(name, shape, dtype, self.stage_off)
        assert self.stage_off <= self.PART_BYTES, \
            f"stage SBUF overflow {self.stage_off/1024:.1f}KB at {name}"
        return t

    def stage_at(self, name, shape, dtype, offset):
        t, end = self._alloc(name, shape, dtype, offset)
        assert end <= self.PART_BYTES
        return t


class Psum:
    def __init__(self, nc, name, shape=(128, 512), dtype=F32):
        self.t = nc.alloc_psum_tensor(name, list(shape), dtype)
        self.rd = None
        self.wr = None


def emit_ln_stats(kb, x_ap, negmean_ap, rstd_ap, tmp_ap, tmp2_ap, sq_ap, eps, xwaits=(), width=D):
    V, Sc = kb.sem["V"], kb.sem["Sc"]
    sc1 = kb.op("scalar", lambda e: e.activation(sq_ap, x_ap, AF.Square, accum_out=tmp_ap),
                waits=xwaits, inc=Sc)
    kb.op("vector", lambda e: e.reduce_sum(negmean_ap, x_ap, axis=AX.X), waits=xwaits)
    kb.drain("vector")
    kb.op("vector", lambda e: e.tensor_single_scalar(negmean_ap, negmean_ap, -1.0 / width, ALU.mult))
    kb.drain("vector")
    kb.op("vector", lambda e: e.tensor_single_scalar(tmp2_ap, tmp_ap, 1.0 / width, ALU.mult),
          waits=[sc1])
    kb.drain("vector")
    v4 = kb.op("vector", lambda e: e.scalar_tensor_tensor(
        tmp_ap, negmean_ap, negmean_ap, tmp2_ap, ALU.mult, ALU.subtract), inc=V)
    kb.op("scalar", lambda e: e.activation(tmp2_ap, tmp_ap, AF.Ln, bias=eps[0], scale=-1.0),
          waits=[v4, eps[1]])
    kb.drain("scalar")
    sc2 = kb.op("scalar", lambda e: e.activation(rstd_ap, tmp2_ap, AF.Exp, scale=-0.5), inc=Sc)
    return sc2


def emit_ln_stats2(kb, tiles, sq_ap, eps, width=D):
    V, Sc = kb.sem["V"], kb.sem["Sc"]
    sc1 = []
    for x_ap, negm, rstd, tmp, tmp2, xw in tiles:
        w = list(xw) + ([sc1[-1]] if sc1 else [])
        sc1.append(kb.op("scalar", lambda e, x_ap=x_ap, sq_ap=sq_ap, tmp=tmp: e.activation(
            sq_ap, x_ap, AF.Square, accum_out=tmp), waits=w, inc=Sc))
    for x_ap, negm, rstd, tmp, tmp2, xw in tiles:
        kb.op("vector", lambda e, x_ap=x_ap, negm=negm: e.reduce_sum(negm, x_ap, axis=AX.X),
              waits=xw)
    kb.drain("vector")
    for x_ap, negm, rstd, tmp, tmp2, xw in tiles:
        kb.op("vector", lambda e, negm=negm: e.tensor_single_scalar(
            negm, negm, -1.0 / width, ALU.mult))
    kb.drain("vector")
    for i, (x_ap, negm, rstd, tmp, tmp2, xw) in enumerate(tiles):
        kb.op("vector", lambda e, tmp=tmp, tmp2=tmp2: e.tensor_single_scalar(
            tmp2, tmp, 1.0 / width, ALU.mult), waits=[sc1[i]])
    kb.drain("vector")
    v4 = None
    for x_ap, negm, rstd, tmp, tmp2, xw in tiles:
        v4 = kb.op("vector", lambda e, negm=negm, tmp=tmp, tmp2=tmp2: e.scalar_tensor_tensor(
            tmp, negm, negm, tmp2, ALU.mult, ALU.subtract), inc=V)
    toks = []
    for x_ap, negm, rstd, tmp, tmp2, xw in tiles:
        kb.op("scalar", lambda e, tmp=tmp, tmp2=tmp2: e.activation(
            tmp2, tmp, AF.Ln, bias=eps[0], scale=-1.0), waits=[v4, eps[1]])
    kb.drain("scalar")
    for x_ap, negm, rstd, tmp, tmp2, xw in tiles:
        toks.append(kb.op("scalar", lambda e, rstd=rstd, tmp2=tmp2: e.activation(
            rstd, tmp2, AF.Exp, scale=-0.5), inc=Sc))
    return toks



def emit_rms_stats2(kb, tiles, sq_ap, eps, width=D):
    V, Sc = kb.sem["V"], kb.sem["Sc"]
    sc1 = []
    for x_ap, rstd, tmp, tmp2, xw in tiles:
        w = list(xw) + ([sc1[-1]] if sc1 else [])
        sc1.append(kb.op("scalar", lambda e, x_ap=x_ap, sq_ap=sq_ap, tmp=tmp: e.activation(
            sq_ap, x_ap, AF.Square, accum_out=tmp), waits=w, inc=Sc))
    v1 = None
    for i, (x_ap, rstd, tmp, tmp2, xw) in enumerate(tiles):
        v1 = kb.op("vector", lambda e, tmp=tmp, tmp2=tmp2: e.tensor_single_scalar(
            tmp2, tmp, 1.0 / width, ALU.mult), waits=[sc1[i]], inc=V)
    for x_ap, rstd, tmp, tmp2, xw in tiles:
        kb.op("scalar", lambda e, tmp=tmp, tmp2=tmp2: e.activation(
            tmp, tmp2, AF.Ln, bias=eps[0]), waits=[v1, eps[1]])
    kb.drain("scalar")
    toks = []
    for x_ap, rstd, tmp, tmp2, xw in tiles:
        toks.append(kb.op("scalar", lambda e, rstd=rstd, tmp=tmp: e.activation(
            rstd, tmp, AF.Exp, scale=-0.5), inc=Sc))
    return toks


def emit_rms_stats(kb, x_ap, rstd_ap, tmp_ap, tmp2_ap, sq_ap, eps, xwaits=(), width=D):
    V, Sc = kb.sem["V"], kb.sem["Sc"]
    sc1 = kb.op("scalar", lambda e: e.activation(sq_ap, x_ap, AF.Square, accum_out=tmp_ap),
                waits=xwaits, inc=Sc)
    v1 = kb.op("vector", lambda e: e.tensor_single_scalar(tmp2_ap, tmp_ap, 1.0 / width, ALU.mult),
               waits=[sc1], inc=V)
    kb.op("scalar", lambda e: e.activation(tmp_ap, tmp2_ap, AF.Ln, bias=eps[0]),
          waits=[v1, eps[1]])
    kb.drain("scalar")
    sc2 = kb.op("scalar", lambda e: e.activation(rstd_ap, tmp_ap, AF.Exp, scale=-0.5), inc=Sc)
    return sc2


def bcast_rows(kb, psA, psB, dst_list, waits=()):
    T, V = kb.sem["T"], kb.sem["V"]
    ones_f = kb.ones_f
    banks = [psA, psB]
    i = 0
    for dst_ap, row_ap in dst_list:
        for j in range(D // 512):
            ps = banks[i % 2]
            i += 1
            sl = slice(j * 512, (j + 1) * 512)
            w = list(waits) + ([ps.rd] if ps.rd else [])
            t = kb.op("tensor", lambda e, sl=sl, ps=ps, row_ap=row_ap: e.matmul(
                ps.t[:, :], ones_f[0:1, :], row_ap[0:1, sl], start=True, stop=True),
                waits=w, inc=T)
            ps.wr = t
            v = kb.op("vector", lambda e, sl=sl, ps=ps, dst_ap=dst_ap: e.tensor_copy(
                dst_ap[:, sl], ps.t[:, :]), waits=[t], inc=V)
            ps.rd = v


STAGE_IO = {
    "s1": (("x_in",), "r1"),
    "s2": (("r1",), "rh2"),
    "s3": (("rh2", "r1"), "r2"),
    "s4": (("r2",), "r3"),
    "s5": (("r3",), "out"),
}
ALL_STAGES = ("s1", "s2", "s3", "s4", "s5")

WSPEC = {
    "wqkv": (D, 3 * D), "wo": (D, D),
    "bq3": (128, DC), "bk3": (128, DC), "bv3": (128, DC),
    "bv_row": (1, D), "bo_row": (1, D), "gamma_row": (1, D), "beta_row": (1, D),
    "ln1_row": (1, D), "ln2_row": (1, D),
    "fwq": (D, D), "fwk": (D, 256), "fwv": (D, 256), "fwo": (D, D),
    "wr3": (128, DC, 8),
    "wg": (8, D, FFN_F), "wu": (8, D, FFN_F), "wd": (8, FFN_F, D),
    "awq": (D, D), "awk": (D, D), "awv": (D, D),
    "rw3": (128, DC, 8), "ebias_row": (1, 8),
    "w1t": (8, D, FFN_F), "w3t": (8, D, FFN_F), "w2t": (8, FFN_F, D),
    "sw1t": (2, D, FFN_F), "sw3t": (2, D, FFN_F), "sw2t": (2, FFN_F, D),
}
CSPEC = {
    "ident_bf": ((128, 128), BF16), "ident_f": ((128, 128), F32),
    "perm32": ((128, 128), BF16),
    "rope_c": ((128, S), F32), "rope_s": ((128, S), F32),
    "cmask": ((4, 128, 512), BF16),
    "dft_c": ((D, 513), BF16), "dft_s": ((D, 513), BF16),
    "irf_a": ((640, D), BF16), "irf_b": ((640, D), BF16),
    "ones64": ((64, 1), BF16), "ones_f": ((1, 128), F32),
}


class Builder:
    def __init__(self, stages=ALL_STAGES, expose=False):
        self.stages = list(stages)
        nc = bass.Bass("TRN2", target_bir_lowering=False)
        self.nc = nc
        self.kb = KB(nc)
        self.al = Alloc(nc)
        kb = self.kb

        reads = set()
        writes = set()
        for st in self.stages:
            rr, w = STAGE_IO[st]
            reads.update(rr)
            writes.add(w)
        self.dram = {}
        for name in ("x_in", "r1", "rh2", "r2", "r3", "out"):
            if name == "out" and "out" in writes:
                kind = "ExternalOutput"
            elif name in writes:
                kind = "ExternalOutput" if expose else "Internal"
            elif name in reads:
                kind = "ExternalInput"
            else:
                continue
            self.dram[name] = nc.dram_tensor(name, [BPC, S, D], F32, kind=kind).ap()
        for name, shp in WSPEC.items():
            self.dram[name] = nc.dram_tensor(name, list(shp), F32, kind="ExternalInput").ap()
        for name, (shp, dt) in CSPEC.items():
            self.dram[name] = nc.dram_tensor(name, list(shp), dt, kind="ExternalInput").ap()

        self.pA = Psum(nc, "pA")
        self.pB = Psum(nc, "pB")
        self.pS0 = Psum(nc, "pS0")
        self.pS1 = Psum(nc, "pS1")
        self.pC = Psum(nc, "pC")
        self.pX = Psum(nc, "pX", (64, 512))
        self.pT0 = Psum(nc, "pT0", (128, 512), BF16)
        self.pT1 = Psum(nc, "pT1", (128, 512), BF16)

        al = self.al
        self.sb = {}
        pl = []
        for name, (shp, dt) in CSPEC.items():
            if name in ("rope_c", "rope_s", "cmask", "dft_c", "dft_s", "irf_a", "irf_b"):
                continue
            t = al.perm(name, shp, dt)
            self.sb[name] = t
            pl.append((t, self.dram[name]))
        for name in ("bq3", "bk3", "bv3", "bv_row", "bo_row", "gamma_row", "beta_row",
                     "ln1_row", "ln2_row", "ebias_row", "wr3", "rw3"):
            shp = WSPEC[name]
            t = al.perm(name, shp, F32)
            self.sb[name] = t
            pl.append((t, self.dram[name]))
        dW = kb.sem["dW"]
        for t, ap in pl:
            kb.op("sync", lambda e, t=t, ap=ap: e.dma_start(out=t[:], in_=ap[:]),
                  inc=dW, inc_by=16)
        self.persist_tok = (dW, dW.n)
        kb.ones_f = self.sb["ones_f"]
        self.wbf = {}
        self.prep = {}
        self.psem = [kb.dstream(f"prep{i}") for i in range(4)]
        self._pi = 0
        for name in ("wqkv", "wo", "fwq", "fwk", "fwv", "fwo", "awk", "awv", "awq",
                     "wg", "wu", "wd", "sw1t", "sw3t", "sw2t", "w1t", "w3t", "w2t"):
            self.wbf[name] = nc.dram_tensor("bf_" + name, list(WSPEC[name]), BF16,
                                            kind="Internal").ap()
        epst = al.perm("epsv", (128, 1), F32)
        etok = kb.op("vector", lambda e: e.memset(epst[:], LN_EPS), inc=kb.sem["V"])
        self.eps = (epst[:, 0:1], etok)
        ep30 = al.perm("eps30", (128, 1), F32)
        etok30 = kb.op("vector", lambda e: e.memset(ep30[:], 1e-30), inc=kb.sem["V"])
        self.eps30 = (ep30, etok30)
        self.record_prep(("wqkv", "wo", "fwq", "fwk", "fwv", "fwo", "awk", "awv", "awq"))

    def record_prep(self, names, gate=None):
        kb = self.kb
        for name in names:
            shp = WSPEC[name]
            nsl = shp[0] if len(shp) == 3 else 1
            for j in range(nsl):
                sm = self.psem[self._pi % 4]
                self._pi += 1
                src = self.dram[name][j] if nsl > 1 else self.dram[name][:]
                dst = self.wbf[name][j] if nsl > 1 else self.wbf[name][:]
                w = [(sm, sm.n)] + ([gate] if gate else [])
                gate = None
                kb.op("gpsimd", lambda e, src=src, dst=dst: e.dma_start(out=dst, in_=src),
                      waits=w, inc=sm, inc_by=16)
            self.prep[name] = [(sm_, sm_.n) for sm_ in self.psem]

    def mm_ff(self, ps, lhsT_slices, rhs_slices, waits):
        kb = self.kb
        T = kb.sem["T"]
        n = len(lhsT_slices)
        tok = None
        for i, (lt, rh) in enumerate(zip(lhsT_slices, rhs_slices)):
            w = list(waits) + ([ps.rd] if ps.rd else []) if i == 0 else []
            nw = rh.shape[-1]
            mw = lt.shape[-1]
            tok = kb.op("tensor", lambda e, lt=lt, rh=rh, i=i, n=n, nw=nw, mw=mw: e.matmul(
                ps.t[0:mw, 0:nw], lt, rh, start=(i == 0), stop=(i == n - 1)),
                waits=w, inc=(T if i == n - 1 else None))
        ps.wr = tok
        return tok

    def build(self):
        kb = self.kb
        for eng in ("tensor", "scalar", "vector", "gpsimd"):
            kb.wait(eng, [self.persist_tok, self.eps[1]])
        big = ("wg", "wu", "wd", "sw1t", "sw3t", "sw2t", "w1t", "w3t", "w2t")
        if self.stages[0] == "s1":
            first = True
            for b in range(BPC):
                for st in self.stages:
                    getattr(self, "stage_" + st)(b)
                    if first:
                        gate = (kb.sem["T"], max(1, kb.sem["T"].n * 3 // 4))
                        self.prep_gate = gate
                        self.record_prep(big, gate=gate)
                        first = False
                    kb.barrier()
        else:
            self.record_prep(big)
            for b in range(BPC):
                for st in self.stages:
                    getattr(self, "stage_" + st)(b)
                    kb.barrier()
        kb.finish(final_waits=[(kb.sem["dO"], kb.sem["dO"].n)])
        return self.nc

    def stage_s1(self, b):
        kb, al, nc = self.kb, self.al, self.nc
        T, V, Sc, dW, dA, dO, G = (kb.sem[k] for k in ("T", "V", "Sc", "dW", "dA", "dO", "G"))
        dx = [kb.dstream("x0"), kb.dstream("x1")]
        dwc = [kb.dstream("wa"), kb.dstream("wb")]
        dbig = kb.dstream("wbig")
        al.begin_stage()
        x_ap = self.dram["x_in"]
        r1_ap = self.dram["r1"]

        lnT = al.stage("lnT", (128, DC, S), BF16)
        ctxT = al.stage("ctxT", (128, DC, S), BF16)
        esb = al.stage("esb", (128, TT, 512), BF16)
        qx = [al.stage(f"qx{i}", (65, S), BF16) for i in range(2)]
        kx = [al.stage(f"kx{i}", (65, S), BF16) for i in range(2)]
        vpair = al.stage("vpair", (128, TT, 2, 65), BF16)
        wcol = al.stage("wcol", (128, 2, 3, DC, 128), BF16)
        wo_sb = al.stage("wo_sb", (128, DC, D), BF16)
        x0t = al.stage("x0t", (128, 2, D), F32)
        lntile_off = al.stage_off
        lntile = al.stage("lntile", (128, 2, D), BF16)
        osb = al.stage("osb", (128, D), F32)
        lnf = al.stage("lnf", (128, D), F32)
        a2 = al.stage("a2", (128, D), F32)
        bB = al.stage("bB", (128, 4, D), F32)
        negm = al.stage("negm", (128, TT), F32)
        rstd = al.stage("rstd", (128, TT), F32)
        stA = al.stage("stA", (128, 2 * TT), F32)
        negm2 = al.stage("negm2", (128, TT), F32)
        rstd2 = al.stage("rstd2", (128, TT), F32)
        stC = al.stage("stC", (128, 2 * TT), F32)
        sqscr_off = al.stage_off
        sqscr = al.stage("sqscr", (128, D), F32)
        sqh = al.stage_at("sqh", (64, S), BF16, lntile_off)
        qn_sb = al.stage("qn_sb", (1, S), BF16)
        kn_sb = al.stage_at("kn_sb", (1, S), BF16, sqscr_off)
        kmax = al.stage("kmax", (1, 2), F32)
        rz = al.stage("rz", (1, 512), F32)
        rzB = al.stage("rzB", (64, 512), F32)

        ident_bf = self.sb["ident_bf"]
        ones64 = self.sb["ones64"]
        pA, pB, pS, pC, pX, pT = self.pA, self.pB, [self.pS0, self.pS1], self.pC, self.pX, [self.pT0, self.pT1]

        bcast_rows(kb, pA, pB,
                   [(bB[:, 0, :], self.sb["bv_row"]),
                    (bB[:, 1, :], self.sb["bo_row"]),
                    (bB[:, 2, :], self.sb["gamma_row"]),
                    (bB[:, 3, :], self.sb["beta_row"])],
                   waits=[self.persist_tok])
        wo_tok = kb.op("gpsimd", lambda e: e.dma_start(
            out=wo_sb[:], in_=self.dram["wo"].rearrange("(ic p) d -> p ic d", p=128)),
            waits=[], inc=dbig, inc_by=16)
        vset = kb.op("vector", lambda e: e.memset(vpair[:, :, :, 64:65], 1.0), inc=V)

        x0_tok = [None, None]
        x0_use = [None, None]
        ln_use = [None, None]
        ptA_free = [None, None]
        vA = None
        for tt in range(TT):
            sl = tt % 2
            w = [x0_use[sl]] if x0_use[sl] else []
            x0_tok[sl] = kb.op("sync", lambda e, b=b, tt=tt, sl=sl: e.dma_start(
                out=x0t[:, sl, :], in_=x_ap[b, tt * 128:(tt + 1) * 128, :]),
                waits=w, inc=dx[sl], inc_by=16)
            emit_ln_stats(kb, x0t[:, sl, :], negm[:, tt:tt + 1], rstd[:, tt:tt + 1],
                          stA[:, 2 * tt:2 * tt + 1], stA[:, 2 * tt + 1:2 * tt + 2],
                          sqscr[:], self.eps, xwaits=[x0_tok[sl]])
            w = [ln_use[sl]] if ln_use[sl] else []
            vln = kb.op("vector", lambda e, tt=tt, sl=sl: e.tensor_scalar(
                lntile[:, sl, :], x0t[:, sl, :], negm[:, tt:tt + 1], rstd[:, tt:tt + 1],
                ALU.add, ALU.mult), waits=w, inc=V)
            x0_use[sl] = vln
            for g in range(2):
                tw_ = [vln] + ([ptA_free[g]] if ptA_free[g] else [])
                ttok = None
                for j in range(4):
                    dc = g * 4 + j
                    ttok = kb.op("tensor", lambda e, g=g, j=j, dc=dc, sl=sl: e.transpose(
                        pT[g].t[:, j * 128:(j + 1) * 128], lntile[:, sl, dc * 128:(dc + 1) * 128],
                        ident_bf[:]), waits=(tw_ if j == 0 else []),
                        inc=(T if j == 3 else None))
                vev = None
                for j in range(4):
                    dc = g * 4 + j
                    vev = kb.op("vector", lambda e, g=g, j=j, dc=dc, tt=tt: e.tensor_copy(
                        lnT[:, dc, tt * 128:(tt + 1) * 128], pT[g].t[:, j * 128:(j + 1) * 128]),
                        waits=([ttok] if j == 0 else []), inc=(V if j == 3 else None))
                ptA_free[g] = vev
                ln_use[sl] = vev
            vA = ln_use[sl]

        wcol_tok = [None, None]
        wcol_use = [None, None]
        esb_free = None
        exp_tok = [None] * TT
        bank_exp = [None, None]
        attn_done = None
        sq_reader = None
        for oc in range(8):
            buf = oc % 2
            w = [wcol_use[buf]] if wcol_use[buf] else []
            for im, (wn, wd_) in enumerate((("wq", D), ("wk", D), ("wv", D))):
                wcol_tok[buf] = kb.op("gpsimd", lambda e, wn=wn, oc=oc, buf=buf, im=im: e.dma_start(
                    out=wcol[:, buf, im, :, :],
                    in_=self.dram[wn].rearrange("(ic p) d -> p ic d", p=128)[:, :, oc * 128:(oc + 1) * 128]),
                    waits=(w if im == 0 else []), inc=dwc[buf], inc_by=16)
            wready = wcol_tok[buf]

            evq = {}
            for im, dstpair, brow in ((0, qx, "bq3"), (1, kx, "bk3")):
                for tw in range(TW):
                    ps = pA if (tw % 2 == 0) else pB
                    tok = self.mm_ff(
                        ps,
                        [wcol[:, buf, im, ic, :] for ic in range(DC)],
                        [lnT[:, ic, tw * 512:(tw + 1) * 512] for ic in range(DC)],
                        waits=[wready, vA])
                    s1 = kb.op("scalar", lambda e, ps=ps, dstpair=dstpair, tw=tw, oc=oc, brow=brow: e.activation(
                        dstpair[0][0:64, tw * 512:(tw + 1) * 512], ps.t[0:64, :], AF.Identity,
                        bias=self.sb[brow][0:64, oc:oc + 1]),
                        waits=[tok] + ([attn_done] if attn_done else []), inc=Sc)
                    s2 = kb.op("scalar", lambda e, ps=ps, dstpair=dstpair, tw=tw, oc=oc, brow=brow: e.activation(
                        dstpair[1][0:64, tw * 512:(tw + 1) * 512], ps.t[64:128, :], AF.Identity,
                        bias=self.sb[brow][64:128, oc:oc + 1]), inc=Sc)
                    ps.rd = s2
                    evq[(im, tw)] = s2
            qk_done = evq[(1, TW - 1)]

            vv_done = None
            for tt in range(TT):
                ps = pA if (tt % 2 == 0) else pB
                tok = self.mm_ff(
                    ps,
                    [lnT[:, ic, tt * 128:(tt + 1) * 128] for ic in range(DC)],
                    [wcol[:, buf, 2, ic, :] for ic in range(DC)],
                    waits=[wready, vA, vset])
                vv = None
                for h in range(2):
                    vv = kb.op("vector", lambda e, ps=ps, tt=tt, h=h, oc=oc: e.tensor_tensor(
                        vpair[:, tt, h, 0:64], ps.t[:, h * 64:(h + 1) * 64],
                        bB[:, 0, oc * 128 + h * 64: oc * 128 + (h + 1) * 64], ALU.add),
                        waits=(([tok] + ([attn_done] if attn_done else [])) if h == 0 else []),
                        inc=(V if h == 1 else None))
                ps.rd = vv
                vv_done = vv
            wcol_use[buf] = vv_done

            for h in range(2):
                qh, kh = qx[h], kx[h]
                sq_tok = kb.op("scalar", lambda e, qh=qh: e.activation(
                    sqh[0:64, :], qh[0:64, :], AF.Square),
                    waits=[qk_done] + ([sq_reader] if sq_reader else []), inc=Sc)
                qn_done = None
                for tw in range(TW):
                    w = [sq_tok] + ([pX.rd] if pX.rd else [])
                    mtok = kb.op("tensor", lambda e, tw=tw: e.matmul(
                        pX.t[0:1, :], ones64[0:64, 0:1], sqh[0:64, tw * 512:(tw + 1) * 512],
                        start=True, stop=True), waits=w, inc=T)
                    qn_done = kb.op("scalar", lambda e, tw=tw: e.activation(
                        qn_sb[0:1, tw * 512:(tw + 1) * 512], pX.t[0:1, :], AF.Sqrt),
                        waits=[mtok], inc=Sc)
                    pX.rd = qn_done
                sqk_tok = kb.op("scalar", lambda e, kh=kh: e.activation(
                    sqh[0:64, :], kh[0:64, :], AF.Square), waits=[qn_done, mtok], inc=Sc)
                kn_done = None
                for tw in range(TW):
                    w = [sqk_tok] + ([pX.rd] if pX.rd else [])
                    mtok = kb.op("tensor", lambda e, tw=tw: e.matmul(
                        pX.t[0:1, :], ones64[0:64, 0:1], sqh[0:64, tw * 512:(tw + 1) * 512],
                        start=True, stop=True), waits=w, inc=T)
                    kn_done = kb.op("scalar", lambda e, tw=tw: e.copy(
                        kn_sb[0:1, tw * 512:(tw + 1) * 512], pX.t[0:1, :]),
                        waits=[mtok], inc=Sc)
                    pX.rd = kn_done
                sq_reader = mtok
                vkm = kb.op("vector", lambda e: e.reduce_max(kmax[0:1, 0:1], kn_sb[0:1, :], axis=AX.X),
                            waits=[kn_done], inc=V)
                skm = kb.op("scalar", lambda e: e.activation(kmax[0:1, 1:2], kmax[0:1, 0:1], AF.Sqrt),
                            waits=[vkm], inc=Sc)
                stab = kb.op("vector", lambda e, qh=qh: e.tensor_scalar(
                    qh[64:65, :], qn_sb[0:1, :], kmax[0:1, 1:2], -1.0, ALU.mult, ALU.mult),
                    waits=[skm], inc=V)
                kones = kb.op("vector", lambda e, kh=kh: e.memset(kh[64:65, :], 1.0),
                              waits=[qk_done], inc=V)

                for tw in range(TW):
                    for kc in range(TT):
                        pb_ = pS[kc % 2]
                        w = [stab, kones]
                        if bank_exp[kc % 2]:
                            w.append(bank_exp[kc % 2])
                        if esb_free:
                            w.append(esb_free)
                        mtok = kb.op("tensor", lambda e, pb_=pb_, kh=kh, qh=qh, kc=kc, tw=tw: e.matmul(
                            pb_.t[:, :], kh[:, kc * 128:(kc + 1) * 128], qh[:, tw * 512:(tw + 1) * 512],
                            start=True, stop=True), waits=w, inc=T)
                        etok = kb.op("scalar", lambda e, pb_=pb_, kc=kc: e.activation(
                            esb[:, kc, :], pb_.t[:, :], AF.Exp, scale=0.125),
                            waits=[mtok], inc=Sc)
                        bank_exp[kc % 2] = etok
                        exp_tok[kc] = etok
                    ctx_tok = None
                    for kc in range(TT):
                        w = [exp_tok[kc], vv_done]
                        if kc == 0 and pC.rd:
                            w.append(pC.rd)
                        ctx_tok = kb.op("tensor", lambda e, kc=kc, h=h: e.matmul(
                            pC.t[0:65, :], vpair[:, kc, h, :], esb[:, kc, :],
                            start=(kc == 0), stop=(kc == TT - 1)),
                            waits=w, inc=(T if kc == TT - 1 else None))
                    esb_free = ctx_tok
                    vrz = kb.op("vector", lambda e: e.reciprocal(rz[0:1, :], pC.t[64:65, :]),
                                waits=[ctx_tok], inc=V)
                    w = [vrz] + ([pX.rd] if pX.rd else [])
                    btok = kb.op("tensor", lambda e: e.matmul(
                        pX.t[0:64, :], self.sb["ones_f"][0:1, 0:64], rz[0:1, :],
                        start=True, stop=True), waits=w, inc=T)
                    pX.wr = btok
                    vcp = kb.op("vector", lambda e: e.tensor_copy(rzB[:, :], pX.t[0:64, :]),
                                waits=[btok], inc=V)
                    pX.rd = vcp
                    g_head = 2 * oc + h
                    vn = kb.op("vector", lambda e, g_head=g_head, tw=tw: e.tensor_tensor(
                        ctxT[64 * (g_head % 2):64 * (g_head % 2) + 64, g_head // 2, tw * 512:(tw + 1) * 512],
                        pC.t[0:64, :], rzB[:, :], ALU.mult), waits=[vcp], inc=V)
                    pC.rd = vn
                attn_done = pC.rd
        ctxT_done = pC.rd

        osb_use = None
        x0_use = [x0_use[0], x0_use[1]]
        for tt in range(TT):
            sl = tt % 2
            w = [x0_use[sl]] if x0_use[sl] else []
            x0_tok[sl] = kb.op("sync", lambda e, b=b, tt=tt, sl=sl: e.dma_start(
                out=x0t[:, sl, :], in_=x_ap[b, tt * 128:(tt + 1) * 128, :]),
                waits=w, inc=dx[sl], inc_by=16)
            ev = {}
            for js in range(2):
                ps = pA if js == 0 else pB
                tok = self.mm_ff(
                    ps,
                    [ctxT[:, dc, tt * 128:(tt + 1) * 128] for dc in range(DC)],
                    [wo_sb[:, dc, js * 512:(js + 1) * 512] for dc in range(DC)],
                    waits=[ctxT_done, wo_tok] + ([osb_use] if osb_use else []))
                ev[js] = kb.op("vector", lambda e, ps=ps, js=js: e.tensor_tensor(
                    osb[:, js * 512:(js + 1) * 512], ps.t[:, :], bB[:, 1, js * 512:(js + 1) * 512],
                    ALU.add), waits=[tok], inc=V)
                ps.rd = ev[js]
            vlnf = kb.op("vector", lambda e, tt=tt, sl=sl: e.tensor_scalar(
                lnf[:], x0t[:, sl, :], negm[:, tt:tt + 1], rstd[:, tt:tt + 1], ALU.add, ALU.mult),
                waits=[x0_tok[sl], ev[1]], inc=V)
            kb.drain("vector")
            vsum = kb.op("vector", lambda e: e.tensor_tensor(osb[:], osb[:], lnf[:], ALU.add), inc=V)
            kb.drain("vector")
            emit_ln_stats(kb, osb[:], negm2[:, tt:tt + 1], rstd2[:, tt:tt + 1],
                          stC[:, 2 * tt:2 * tt + 1], stC[:, 2 * tt + 1:2 * tt + 2],
                          sqscr[:], self.eps, xwaits=[vsum])
            kb.op("vector", lambda e, tt=tt: e.tensor_scalar(
                a2[:], osb[:], negm2[:, tt:tt + 1], rstd2[:, tt:tt + 1], ALU.add, ALU.mult))
            kb.drain("vector")
            kb.op("vector", lambda e: e.tensor_tensor(lnf[:], a2[:], bB[:, 2, :], ALU.mult))
            kb.drain("vector")
            kb.op("vector", lambda e: e.tensor_tensor(a2[:], lnf[:], bB[:, 3, :], ALU.add))
            kb.drain("vector")
            vout = kb.op("vector", lambda e, sl=sl: e.tensor_tensor(
                osb[:], a2[:], x0t[:, sl, :], ALU.add), inc=V)
            x0_use[sl] = vout
            kb.drain("vector")
            otok = kb.op("sync", lambda e, b=b, tt=tt: e.dma_start(
                out=r1_ap[b, tt * 128:(tt + 1) * 128, :], in_=osb[:]),
                waits=[vout], inc=dO, inc_by=16)
            osb_use = otok


def prep_weights(i):
    f32 = np.float32
    ca = np.ascontiguousarray
    w = {}
    w["wqkv"] = ca(np.concatenate([np.asarray(i["L0_attn_wq"], f32),
                                   np.asarray(i["L0_attn_wk"], f32),
                                   np.asarray(i["L0_attn_wv"], f32)], axis=1))
    w["wo"] = ca(i["L0_attn_wo"], dtype=f32)
    w["bq3"] = ca(np.asarray(i["L0_attn_bq"], f32).reshape(DC, 128).T)
    w["bk3"] = ca(np.asarray(i["L0_attn_bk"], f32).reshape(DC, 128).T)
    w["bv3"] = ca(np.asarray(i["L0_attn_bv"], f32).reshape(DC, 128).T)
    w["bv_row"] = ca(np.asarray(i["L0_attn_bv"], f32).reshape(1, D))
    w["bo_row"] = ca(np.asarray(i["L0_attn_bo"], f32).reshape(1, D))
    w["gamma_row"] = ca(np.asarray(i["L0_attn_gamma"], f32).reshape(1, D))
    w["beta_row"] = ca(np.asarray(i["L0_attn_beta"], f32).reshape(1, D))
    w["ln1_row"] = ca(np.asarray(i["L0_ffn_ln1_w"], f32).reshape(1, D))
    w["ln2_row"] = ca(np.asarray(i["L0_ffn_ln2_w"], f32).reshape(1, D))
    w["fwq"] = ca(i["L0_ffn_Wq"], dtype=f32)
    w["fwk"] = ca(i["L0_ffn_Wk"], dtype=f32)
    w["fwv"] = ca(i["L0_ffn_Wv"], dtype=f32)
    w["fwo"] = ca(i["L0_ffn_Wo"], dtype=f32)
    w["wr3"] = ca(np.asarray(i["L0_ffn_Wr"], f32).reshape(DC, 128, 8).transpose(1, 0, 2))
    w["wg"] = ca(i["L0_ffn_Wg"], dtype=f32)
    w["wu"] = ca(i["L0_ffn_Wu"], dtype=f32)
    w["wd"] = ca(i["L0_ffn_Wd"], dtype=f32)
    w["awq"] = ca(np.asarray(i["L1_attn_Wq"], f32).T)
    w["awk"] = ca(np.asarray(i["L1_attn_Wk"], f32).T)
    w["awv"] = ca(np.asarray(i["L1_attn_Wv"], f32).T)
    rw = np.asarray(i["L1_ffn_router_w"], f32).T
    w["rw3"] = ca(rw.reshape(DC, 128, 8).transpose(1, 0, 2))
    w["ebias_row"] = ca(np.asarray(i["L1_ffn_expert_bias"], f32).reshape(1, 8))
    w["w1t"] = ca(np.asarray(i["L1_ffn_w1"], f32).transpose(0, 2, 1))
    w["w3t"] = ca(np.asarray(i["L1_ffn_w3"], f32).transpose(0, 2, 1))
    w["w2t"] = ca(np.asarray(i["L1_ffn_w2"], f32).transpose(0, 2, 1))
    w["sw1t"] = ca(np.asarray(i["L1_ffn_sw1"], f32).transpose(0, 2, 1))
    w["sw3t"] = ca(np.asarray(i["L1_ffn_sw3"], f32).transpose(0, 2, 1))
    w["sw2t"] = ca(np.asarray(i["L1_ffn_sw2"], f32).transpose(0, 2, 1))
    w.update(host_constants())
    return w


_PROG_CACHE = {}


def _get_prog(stages, expose):
    key = (tuple(stages), expose)
    if key not in _PROG_CACHE:
        b = Builder(stages, expose=expose)
        b.build()
        _PROG_CACHE[key] = b
    return _PROG_CACHE[key]


def kernel(**inputs):
    w = prep_weights(inputs)
    x = np.ascontiguousarray(np.asarray(inputs["L0_attn_x"], np.float32))
    b = _get_prog(ALL_STAGES, False)
    in_maps = []
    for c in range(NCORES):
        m = dict(w)
        m["x_in"] = x[c * BPC:(c + 1) * BPC]
        in_maps.append(m)
    res = run_bass_kernel_spmd(b.nc, in_maps, core_ids=list(range(NCORES)))
    return np.concatenate([r["out"] for r in res.results], axis=0)


def _stage_s2(self, b):
    kb, al, nc = self.kb, self.al, self.nc
    T, V, Sc, dO = (kb.sem[k] for k in ("T", "V", "Sc", "dO"))
    al.begin_stage()
    x1_ap = self.dram["r1"]
    rh2_ap = self.dram["rh2"]
    dx = [kb.dstream("x0"), kb.dstream("x1")]
    dwc = [kb.dstream("wa"), kb.dstream("wb")]
    dbig = kb.dstream("wbig")
    dwo = kb.dstream("wbig2")

    hT = al.stage("hT", (128, DC, S), BF16)
    ctxT = al.stage("ctxT2", (128, DC, S), BF16)
    esb = al.stage("esb2", (128, TT, 512), BF16)
    qx_off = al.stage_off
    qx = [al.stage(f"q2x{i}", (65, S), BF16) for i in range(2)]
    kx = [al.stage(f"k2x{i}", (65, S), BF16) for i in range(4)]
    fwo_sb = al.stage_at("fwo_sb", (128, DC, D), BF16, qx_off)
    vkv = al.stage("vkv", (128, TT, 4, 65), BF16)
    wqcol = al.stage("wqcol", (128, 2, DC, 128), BF16)
    fwk_sb = al.stage("fwk_sb", (128, DC, 256), BF16)
    fwv_sb = al.stage("fwv_sb", (128, DC, 256), BF16)
    x1t = al.stage("x1t", (128, 2, D), F32)
    htile_off = al.stage_off
    htile = al.stage("htile", (128, 2, D), BF16)
    hsbuf_off = al.stage_off
    hsbuf = al.stage("hsbuf", (128, D), F32)
    t1 = al.stage("t1", (128, 512), F32)
    t2 = al.stage("t2", (128, 512), F32)
    qraw = al.stage("qraw", (128, 512), BF16)
    osb2_off = al.stage_off
    osb = al.stage("osb2", (128, D), F32)
    lnf = al.stage("lnf2", (128, D), F32)
    a2 = al.stage("a2_2", (128, D), F32)
    ln1wB = al.stage("ln1wB", (128, D), F32)
    ropeC = al.stage("ropeC", (128, S), BF16)
    ropeS = al.stage("ropeS", (128, S), BF16)
    cmask_sb = al.stage("cmask_sb", (128, 4, 512), BF16)
    negm = al.stage("negmS2", (128, TT), F32)
    rstd = al.stage("rstdS2", (128, TT), F32)
    rs1 = al.stage("rs1", (128, TT), F32)
    stA = al.stage("stA2", (128, 2 * TT), F32)
    sqscr_off = al.stage_off
    sqscr = al.stage("sqscr2", (128, D), F32)
    sqh = al.stage_at("sqh2", (64, S), BF16, htile_off)
    kn_sb = al.stage_at("kn_sb2", (1, S), BF16, sqscr_off)
    qn_sb = al.stage("qn_sb2", (1, S), BF16)
    kmaxs = al.stage("kmaxs", (1, 8), F32)
    rz = al.stage_at("rz2", (1, 512), F32, hsbuf_off)
    rzB = al.stage_at("rzB2", (64, 512), F32, hsbuf_off + 2048)
    rzl = al.stage_at("rzl2", (1, 512), F32, osb2_off)

    ident_bf = self.sb["ident_bf"]
    perm32 = self.sb["perm32"]
    ones64 = self.sb["ones64"]
    pA, pB, pS, pC, pX, pT = self.pA, self.pB, [self.pS0, self.pS1], self.pC, self.pX, [self.pT0, self.pT1]

    bcast_rows(kb, pA, pB, [(ln1wB[:, :], self.sb["ln1_row"])], waits=[self.persist_tok])
    drp = kb.dstream("ropes")
    kb.op("gpsimd", lambda e: e.dma_start(out=ropeC[:], in_=self.dram["rope_c"][:]),
          waits=[(drp, drp.n)], inc=drp, inc_by=16)
    kb.op("gpsimd", lambda e: e.dma_start(out=ropeS[:], in_=self.dram["rope_s"][:]),
          waits=[(drp, drp.n)], inc=drp, inc_by=16)
    rc_tok = kb.op("gpsimd", lambda e: e.dma_start(
        out=cmask_sb[:], in_=self.dram["cmask"].rearrange("j p c -> p j c")),
        waits=[(drp, drp.n)], inc=drp, inc_by=16)
    dkv = kb.dstream("wkv")
    ck_tok = kb.op("sync", lambda e: e.dma_start(
        out=fwk_sb[:], in_=self.wbf["fwk"].rearrange("(ic p) d -> p ic d", p=128)),
        waits=self.prep["fwk"] + [(dkv, dkv.n)], inc=dkv, inc_by=16)
    ck_tok2 = kb.op("sync", lambda e: e.dma_start(
        out=fwv_sb[:], in_=self.wbf["fwv"].rearrange("(ic p) d -> p ic d", p=128)),
        waits=self.prep["fwv"] + [(dkv, dkv.n)], inc=dkv, inc_by=16)
    consts_tok = [rc_tok, ck_tok, ck_tok2]
    vset = kb.op("vector", lambda e: e.memset(vkv[:, :, :, 64:65], 1.0), inc=V)

    x_tok = [None, None]
    x_use = [None, None]
    h_use = [None, None]
    ptA_free = [None, None]
    vA = None
    for tt in range(TT):
        sl = tt % 2
        w = [x_use[sl]] if x_use[sl] else []
        x_tok[sl] = kb.op("sync", lambda e, b=b, tt=tt, sl=sl: e.dma_start(
            out=x1t[:, sl, :], in_=x1_ap[b, tt * 128:(tt + 1) * 128, :]),
            waits=w, inc=dx[sl], inc_by=16)
        emit_ln_stats(kb, x1t[:, sl, :], negm[:, tt:tt + 1], rstd[:, tt:tt + 1],
                      stA[:, 2 * tt:2 * tt + 1], stA[:, 2 * tt + 1:2 * tt + 2],
                      sqscr[:], self.eps, xwaits=[x_tok[sl]])
        vhs = kb.op("vector", lambda e, tt=tt, sl=sl: e.tensor_scalar(
            hsbuf[:], x1t[:, sl, :], negm[:, tt:tt + 1], rstd[:, tt:tt + 1],
            ALU.add, ALU.mult), waits=[(Sc, Sc.n)], inc=V)
        x_use[sl] = vhs
        kb.drain("vector")
        emit_rms_stats(kb, hsbuf[:], rs1[:, tt:tt + 1], stA[:, 2 * tt:2 * tt + 1],
                       stA[:, 2 * tt + 1:2 * tt + 2], sqscr[:], self.eps, xwaits=[vhs])
        kb.op("vector", lambda e, tt=tt: e.tensor_single_scalar(
            lnf[:], hsbuf[:], rs1[:, tt:tt + 1], ALU.mult), waits=[(Sc, Sc.n)])
        kb.drain("vector")
        w = [h_use[sl]] if h_use[sl] else []
        vln = kb.op("vector", lambda e, sl=sl: e.tensor_tensor(
            htile[:, sl, :], lnf[:], ln1wB[:, :], ALU.mult), waits=w, inc=V)
        kb.drain("vector")
        for g in range(2):
            tw_ = [vln] + ([ptA_free[g]] if ptA_free[g] else [])
            ttok = None
            for j in range(4):
                dc = g * 4 + j
                ttok = kb.op("tensor", lambda e, g=g, j=j, dc=dc, sl=sl: e.transpose(
                    pT[g].t[:, j * 128:(j + 1) * 128], htile[:, sl, dc * 128:(dc + 1) * 128],
                    ident_bf[:]), waits=(tw_ if j == 0 else []),
                    inc=(T if j == 3 else None))
            vev = None
            for j in range(4):
                dc = g * 4 + j
                vev = kb.op("vector", lambda e, g=g, j=j, dc=dc, tt=tt: e.tensor_copy(
                    hT[:, dc, tt * 128:(tt + 1) * 128], pT[g].t[:, j * 128:(j + 1) * 128]),
                    waits=([ttok] if j == 0 else []), inc=(V if j == 3 else None))
            ptA_free[g] = vev
            h_use[sl] = vev
        vA = h_use[sl]

    pP = [pA, pC]
    pW = [pB, pB]
    qraw_free = [None]

    def rope_proj(lhsT_slices_fn, dstA, dstB, wready):
        wready_l = wready if isinstance(wready, list) else [wready]
        for tw in range(TW):
            ps = pP[tw % 2]
            pw = pW[tw % 2]
            tok = self.mm_ff(ps, lhsT_slices_fn(),
                             [hT[:, ic, tw * 512:(tw + 1) * 512] for ic in range(DC)],
                             waits=wready_l + [vA])
            wts = [tok] + ([qraw_free[0]] if qraw_free[0] else [])
            ev = kb.op("scalar", lambda e, ps=ps: e.copy(qraw[:], ps.t[:, :]),
                       waits=wts, inc=Sc)
            ps.rd = ev
            w2 = [ev] + ([pw.rd] if pw.rd else [])
            sw = kb.op("tensor", lambda e, pw=pw: e.matmul(
                pw.t[:, :], perm32[:], qraw[:], start=True, stop=True), waits=w2, inc=T)
            pw.wr = sw
            m1 = kb.op("vector", lambda e, tw=tw: e.tensor_tensor(
                t1[:], qraw[:], ropeC[:, tw * 512:(tw + 1) * 512], ALU.mult),
                waits=[ev] + consts_tok, inc=V)
            m2 = kb.op("vector", lambda e, pw=pw, tw=tw: e.tensor_tensor(
                t2[:], pw.t[:, :], ropeS[:, tw * 512:(tw + 1) * 512], ALU.mult),
                waits=[sw], inc=V)
            pw.rd = m2
            qraw_free[0] = m2
            kb.drain("vector")
            va = kb.op("vector", lambda e, dstA=dstA, tw=tw: e.tensor_tensor(
                dstA[0:64, tw * 512:(tw + 1) * 512], t1[0:64, :], t2[0:64, :], ALU.add))
            vb = kb.op("vector", lambda e, dstB=dstB, tw=tw: e.tensor_tensor(
                dstB[0:64, tw * 512:(tw + 1) * 512], t1[64:128, :], t2[64:128, :], ALU.add),
                inc=V)
            kb.drain("vector")
        return (V, kb.sem["V"].n)

    kv_done = None
    for oc2 in range(2):
        kv_done = rope_proj(
            lambda oc2=oc2: [fwk_sb[:, ic, oc2 * 128:(oc2 + 1) * 128] for ic in range(DC)],
            kx[2 * oc2], kx[2 * oc2 + 1], ck_tok)

    vv_done = None
    for tt in range(TT):
        ps = pP[tt % 2]
        tok = self.mm_ff(ps,
                         [hT[:, ic, tt * 128:(tt + 1) * 128] for ic in range(DC)],
                         [fwv_sb[:, ic, :] for ic in range(DC)],
                         waits=consts_tok + [vA, vset])
        sv = None
        for kv in range(4):
            sv = kb.op("scalar", lambda e, ps=ps, tt=tt, kv=kv: e.copy(
                vkv[:, tt, kv, 0:64], ps.t[:, kv * 64:(kv + 1) * 64]),
                waits=([tok] if kv == 0 else []), inc=(Sc if kv == 3 else None))
        ps.rd = sv
        vv_done = sv

    sq_reader = None
    kn_done = None
    vkm_prev = None
    for kv in range(4):
        w = [kv_done] + ([sq_reader] if sq_reader else []) + ([vkm_prev] if vkm_prev else [])
        sqk = kb.op("scalar", lambda e, kv=kv: e.activation(
            sqh[0:64, :], kx[kv][0:64, :], AF.Square), waits=w, inc=Sc)
        for tw in range(TW):
            w = [sqk] + ([pX.rd] if pX.rd else [])
            mtok = kb.op("tensor", lambda e, tw=tw: e.matmul(
                pX.t[0:1, :], ones64[0:64, 0:1], sqh[0:64, tw * 512:(tw + 1) * 512],
                start=True, stop=True), waits=w, inc=T)
            kn_done = kb.op("scalar", lambda e, tw=tw: e.copy(
                kn_sb[0:1, tw * 512:(tw + 1) * 512], pX.t[0:1, :]), waits=[mtok], inc=Sc)
            pX.rd = kn_done
        sq_reader = mtok
        vkm = kb.op("vector", lambda e, kv=kv: e.reduce_max(
            kmaxs[0:1, kv:kv + 1], kn_sb[0:1, :], axis=AX.X), waits=[kn_done], inc=V)
        vkm_prev = vkm
        kones = kb.op("vector", lambda e, kv=kv: e.memset(kx[kv][64:65, :], 1.0),
                      waits=[kv_done], inc=V)
    stab_k = (V, V.n)
    kones_all = (V, V.n)

    wcol_tok = [None, None]
    wcol_use = [None, None]
    esb_free = None
    exp_tok = [None] * TT
    mask_tok = [None] * TT
    bank_exp = {}
    btok_prev = [None]
    vn_prev = [None]
    attn_done = None
    for oc in range(8):
        buf = oc % 2
        w = [wcol_use[buf]] if wcol_use[buf] else []
        wt = kb.op("sync", lambda e, oc=oc, buf=buf: e.dma_start(
            out=wqcol[:, buf, :, :],
            in_=self.wbf["fwq"].rearrange("(ic p) d -> p ic d", p=128)[:, :, oc * 128:(oc + 1) * 128]),
            waits=w + self.prep["fwq"], inc=dwc[buf], inc_by=16)
        qdone = rope_proj(
            lambda buf=buf: [wqcol[:, buf, ic, :] for ic in range(DC)],
            qx[0], qx[1], wt)
        wcol_use[buf] = qdone

        for hh in range(2):
            h = 2 * oc + hh
            kv = h // 4
            qh = qx[hh]
            kh = kx[kv]
            w = [qdone] + ([sq_reader] if sq_reader else [])
            sqq = kb.op("scalar", lambda e, qh=qh: e.activation(
                sqh[0:64, :], qh[0:64, :], AF.Square), waits=w, inc=Sc)
            qn_done = None
            for tw in range(TW):
                w = [sqq] + ([pX.rd] if pX.rd else [])
                mtok = kb.op("tensor", lambda e, tw=tw: e.matmul(
                    pX.t[0:1, :], ones64[0:64, 0:1], sqh[0:64, tw * 512:(tw + 1) * 512],
                    start=True, stop=True), waits=w, inc=T)
                qn_done = kb.op("scalar", lambda e, tw=tw: e.copy(
                    qn_sb[0:1, tw * 512:(tw + 1) * 512], pX.t[0:1, :]),
                    waits=[mtok], inc=Sc)
                pX.rd = qn_done
            sq_reader = mtok
            stab = kb.op("vector", lambda e, qh=qh, kv=kv: e.tensor_scalar(
                qh[64:65, :], qn_sb[0:1, :], kmaxs[0:1, kv:kv + 1], -0.5,
                ALU.add, ALU.mult), waits=[qn_done, stab_k], inc=V)

            for tw in range(TW):
                nkc = 4 * (tw + 1)
                for kc in range(nkc):
                    pb_ = pS[kc % 2]
                    w = [stab, kones_all]
                    bk = bank_exp.get(kc % 2)
                    if bk:
                        w.append(bk)
                    if esb_free:
                        w.append(esb_free)
                    mtok = kb.op("tensor", lambda e, pb_=pb_, kh=kh, qh=qh, kc=kc, tw=tw: e.matmul(
                        pb_.t[:, :], kh[:, kc * 128:(kc + 1) * 128], qh[:, tw * 512:(tw + 1) * 512],
                        start=True, stop=True), waits=w, inc=T)
                    etok = kb.op("scalar", lambda e, pb_=pb_, kc=kc: e.activation(
                        esb[:, kc, :], pb_.t[:, :], AF.Exp, scale=0.125),
                        waits=[mtok], inc=Sc)
                    bank_exp[kc % 2] = etok
                    exp_tok[kc] = etok
                    mask_tok[kc] = None
                    jj = kc - 4 * tw
                    if jj >= 0:
                        mtk = kb.op("vector", lambda e, kc=kc, jj=jj: e.tensor_tensor(
                            esb[:, kc, :], esb[:, kc, :], cmask_sb[:, jj, :], ALU.mult),
                            waits=[etok] + consts_tok, inc=V)
                        kb.drain("vector")
                        mask_tok[kc] = mtk
                pcx = pC if tw % 2 == 0 else pB
                ctx_tok = None
                for kc in range(nkc):
                    w = [mask_tok[kc] or exp_tok[kc], vv_done]
                    if kc == 0 and pcx.rd:
                        w.append(pcx.rd)
                    ctx_tok = kb.op("tensor", lambda e, kc=kc, kv=kv, nkc=nkc, pcx=pcx: e.matmul(
                        pcx.t[0:65, :], vkv[:, kc, kv, :], esb[:, kc, :],
                        start=(kc == 0), stop=(kc == nkc - 1)),
                        waits=w, inc=(T if kc == nkc - 1 else None))
                esb_free = ctx_tok
                w = [ctx_tok] + ([btok_prev[0]] if btok_prev[0] else [])
                kb.op("scalar", lambda e, pcx=pcx: e.activation(rzl[0:1, :], pcx.t[64:65, :], AF.Ln),
                      waits=w)
                kb.drain("scalar")
                vrz = kb.op("scalar", lambda e: e.activation(rz[0:1, :], rzl[0:1, :], AF.Exp,
                                                             scale=-1.0), inc=Sc)
                w = [vrz] + ([pX.rd] if pX.rd else [])
                btok = kb.op("tensor", lambda e: e.matmul(
                    pX.t[0:64, :], self.sb["ones_f"][0:1, 0:64], rz[0:1, :],
                    start=True, stop=True), waits=w, inc=T)
                btok_prev[0] = btok
                vcp = kb.op("vector", lambda e: e.tensor_copy(rzB[:, :], pX.t[0:64, :]),
                            waits=[btok] + ([vn_prev[0]] if vn_prev[0] else []), inc=V)
                pX.rd = vcp
                vn = kb.op("vector", lambda e, h=h, tw=tw, pcx=pcx: e.tensor_tensor(
                    ctxT[64 * (h % 2):64 * (h % 2) + 64, h // 2, tw * 512:(tw + 1) * 512],
                    pcx.t[0:64, :], rzB[:, :], ALU.mult), waits=[vcp], inc=V)
                vn_prev[0] = vn
                pcx.rd = vn
            attn_done = vn
    ctxT_done = attn_done

    fwo_tok = kb.op("sync", lambda e: e.dma_start(
        out=fwo_sb[:], in_=self.wbf["fwo"].rearrange("(ic p) d -> p ic d", p=128)),
        waits=[attn_done, (T, T.n)] + self.prep["fwo"], inc=dwo, inc_by=16)
    osb_use = None
    out_use = None
    for tt in range(TT):
        sl = tt % 2
        w = [x_use[sl]] if x_use[sl] else []
        x_tok[sl] = kb.op("sync", lambda e, b=b, tt=tt, sl=sl: e.dma_start(
            out=x1t[:, sl, :], in_=x1_ap[b, tt * 128:(tt + 1) * 128, :]),
            waits=w, inc=dx[sl], inc_by=16)
        vlnf = kb.op("vector", lambda e, tt=tt, sl=sl: e.tensor_scalar(
            lnf[:], x1t[:, sl, :], negm[:, tt:tt + 1], rstd[:, tt:tt + 1], ALU.add, ALU.mult),
            waits=[x_tok[sl]], inc=V)
        kb.drain("vector")
        ev = None
        for js in range(2):
            ps = pP[js]
            tok = self.mm_ff(ps,
                             [ctxT[:, dc, tt * 128:(tt + 1) * 128] for dc in range(DC)],
                             [fwo_sb[:, dc, js * 512:(js + 1) * 512] for dc in range(DC)],
                             waits=[ctxT_done, fwo_tok] + ([osb_use] if osb_use else []))
            ev = kb.op("vector", lambda e, ps=ps, js=js: e.tensor_tensor(
                osb[:, js * 512:(js + 1) * 512], ps.t[:, :], lnf[:, js * 512:(js + 1) * 512],
                ALU.add), waits=[tok, vlnf], inc=V)
            ps.rd = ev
        kb.drain("vector")
        w = [ev] + ([out_use] if out_use else [])
        vcl = kb.op("vector", lambda e: e.tensor_scalar(
            a2[:], osb[:], 100.0, -100.0, ALU.min, ALU.max), waits=w, inc=V)
        x_use[sl] = vcl
        osb_use = vcl
        kb.drain("vector")
        otok = kb.op("sync", lambda e, b=b, tt=tt: e.dma_start(
            out=rh2_ap[b, tt * 128:(tt + 1) * 128, :], in_=a2[:]),
            waits=[vcl], inc=dO, inc_by=16)
        out_use = otok


Builder.stage_s2 = _stage_s2


def _stage_s3(self, b):
    kb, al, nc = self.kb, self.al, self.nc
    T, V, Sc, dO = (kb.sem[k] for k in ("T", "V", "Sc", "dO"))
    al.begin_stage()
    h2_ap = self.dram["rh2"]
    x1_ap = self.dram["r1"]
    r2_ap = self.dram["r2"]
    dh = [kb.dstream("h0"), kb.dstream("h1")]
    dx = [kb.dstream("x0"), kb.dstream("x1")]
    dwd = kb.dstream("wd0")
    dgu = [kb.dstream("ga"), kb.dstream("gb")]

    GL = 1024
    GT = GL // 128

    hnT = al.stage("hnT", (128, DC, S), BF16)
    hnTf_off = al.stage_off
    hnTf = al.stage("hnTf", (128, DC, GL), F32)
    aT = al.stage_at("aT", (128, FC, GL), BF16, hnTf_off)
    acc = al.stage("acc3", (128, GT, D), F32)
    wd_sb = al.stage("wd_sb", (128, FC, D), BF16)
    wgu = al.stage("wgu", (128, 2, 2, DC, 256), BF16)
    wall = al.stage("wall", (128, TT, 8), F32)
    h2t = al.stage("h2t", (128, 2, D), F32)
    x1t = al.stage("x1t3", (128, 2, D), F32)
    hnf_off3 = al.stage_off
    hnf = al.stage("hnf", (128, D), F32)
    hnt = al.stage("hnt", (128, D), F32)
    silu_scr = al.stage("silu_scr", (128, 512), F32)
    osb = al.stage_at("osb3", (128, D), F32, hnf_off3)
    ln2wB = al.stage("ln2wB", (128, D), F32)
    rs2 = al.stage("rs2", (128, TT), F32)
    stA = al.stage("stA3", (128, 2 * TT), F32)
    sqscr_off3 = al.stage_off
    sqscr = al.stage("sqscr3", (128, D), F32)
    osb2 = al.stage_at("osb3b", (128, D), F32, sqscr_off3)
    rt = al.stage("rt3", (128, 8, 8), F32)
    rt1 = al.stage("rt13", (128, 8), F32)

    ident_f = self.sb["ident_f"]
    pA, pB, pC, pS = self.pA, self.pB, self.pC, [self.pS0, self.pS1]

    bcast_rows(kb, pA, pB, [(ln2wB[:, :], self.sb["ln2_row"])], waits=[self.persist_tok])

    h_tok = [None, None]
    h_use = [None, None]
    hnt_use = None
    pt_free = [None, None]
    rt_done = None
    router_done = None
    vA = None
    sA = None
    for tp in range(0, TT, 2):
        tiles = []
        for gt in (tp, tp + 1):
            sl = gt % 2
            w = [h_use[sl]] if h_use[sl] else []
            h_tok[sl] = kb.op("sync", lambda e, b=b, gt=gt, sl=sl: e.dma_start(
                out=h2t[:, sl, :], in_=h2_ap[b, gt * 128:(gt + 1) * 128, :]),
                waits=w, inc=dh[sl], inc_by=16)
            tiles.append((h2t[:, sl, :], rs2[:, gt:gt + 1], stA[:, 2 * gt:2 * gt + 1],
                          stA[:, 2 * gt + 1:2 * gt + 2], [h_tok[sl]]))
        stoks = emit_rms_stats2(kb, tiles, sqscr[:], self.eps)
        for ti, gt in enumerate((tp, tp + 1)):
            sl = gt % 2
            rtok = stoks[ti]
            vr = kb.op("vector", lambda e, gt=gt, sl=sl: e.tensor_single_scalar(
                hnf[:], h2t[:, sl, :], rs2[:, gt:gt + 1], ALU.mult), waits=[rtok], inc=V)
            h_use[sl] = vr
            kb.drain("vector")
            w = [hnt_use] if hnt_use else []
            vh = kb.op("vector", lambda e: e.tensor_tensor(
                hnt[:], hnf[:], ln2wB[:, :], ALU.mult), waits=w, inc=V)
            kb.drain("vector")
            t8 = gt % GT
            for g in range(2):
                ps = pS[g]
                tw_ = [vh] + (pt_free[g] if pt_free[g] else [])
                ttok = None
                for j in range(4):
                    dc = g * 4 + j
                    ttok = kb.op("tensor", lambda e, ps=ps, j=j, dc=dc: e.transpose(
                        ps.t[:, j * 128:(j + 1) * 128], hnt[:, dc * 128:(dc + 1) * 128],
                        ident_f[:]), waits=(tw_ if j == 0 else []),
                        inc=(T if j == 3 else None))
                vev = None
                for j in range(4):
                    dc = g * 4 + j
                    vev = kb.op("vector", lambda e, ps=ps, j=j, dc=dc, t8=t8: e.tensor_copy(
                        hnTf[:, dc, t8 * 128:(t8 + 1) * 128], ps.t[:, j * 128:(j + 1) * 128]),
                        waits=([ttok] if j == 0 else []), inc=V)
                sev = None
                for j in range(4):
                    dc = g * 4 + j
                    sev = kb.op("scalar", lambda e, j=j, dc=dc, gt=gt, t8=t8: e.copy(
                        hnT[:, dc, gt * 128:(gt + 1) * 128], hnTf[:, dc, t8 * 128:(t8 + 1) * 128]),
                        waits=([(V, kb.sem["V"].n)] if j == 0 else []),
                        inc=(Sc if j == 3 else None))
                pt_free[g] = [vev, sev]
                hnt_use = vev
            vA = (V, V.n)
            sA = (Sc, Sc.n)
            ps = pS[gt % 2]
            w = [vA, sA] + ([ps.rd] if ps.rd else [])
            tok = None
            for ic in range(DC):
                tok = kb.op("tensor", lambda e, ps=ps, ic=ic, t8=t8: e.matmul(
                    ps.t[:, 0:8], hnTf[:, ic, t8 * 128:(t8 + 1) * 128],
                    self.sb["wr3"][:, ic, :], start=(ic == 0), stop=(ic == DC - 1)),
                    waits=(w if ic == 0 else []), inc=(T if ic == DC - 1 else None))
            lg = rt[:, 0, :]
            pexp = rt[:, 1, :]
            pm = rt[:, 2, :]
            keep = rt[:, 3, :]
            wr_ = rt[:, 4, :]
            m1 = rt1[:, 0:1]
            negm1 = rt1[:, 1:2]
            m2 = rt1[:, 2:3]
            den = rt1[:, 3:4]
            rden = rt1[:, 4:5]
            w = [tok] + ([rt_done] if rt_done else [])
            vlg = kb.op("vector", lambda e, ps=ps: e.tensor_copy(lg, ps.t[:, 0:8]),
                        waits=w, inc=V)
            ps.rd = vlg
            kb.drain("vector")
            kb.op("vector", lambda e: e.reduce_max(m1, lg, axis=AX.X))
            kb.drain("vector")
            v2 = kb.op("vector", lambda e: e.tensor_single_scalar(negm1, m1, -1.0, ALU.mult), inc=V)
            kb.drain("vector")
            se = kb.op("scalar", lambda e: e.activation(pexp, lg, AF.Exp, bias=negm1),
                       waits=[v2], inc=Sc)
            kb.op("vector", lambda e: e.tensor_single_scalar(keep, lg, m1, ALU.is_ge))
            kb.drain("vector")
            kb.op("vector", lambda e: e.scalar_tensor_tensor(pm, keep, -1e30, lg, ALU.mult, ALU.add))
            kb.drain("vector")
            kb.op("vector", lambda e: e.reduce_max(m2, pm, axis=AX.X))
            kb.drain("vector")
            kb.op("vector", lambda e: e.tensor_single_scalar(keep, lg, m2, ALU.is_ge))
            kb.drain("vector")
            kb.op("vector", lambda e: e.tensor_tensor(wr_, pexp, keep, ALU.mult), waits=[se])
            kb.drain("vector")
            kb.op("vector", lambda e: e.reduce_sum(den, wr_, axis=AX.X))
            kb.drain("vector")
            kb.op("vector", lambda e: e.reciprocal(rden, den))
            kb.drain("vector")
            rt_done = kb.op("vector", lambda e, gt=gt: e.tensor_single_scalar(
                wall[:, gt, :], wr_, rden, ALU.mult), inc=V)
            kb.drain("vector")
    router_done = rt_done

    wd_tok = None
    wd_free = None
    silu_free = [None]
    gu_tok = [None, None]
    gu_free = [None, None]
    aT_free = None
    accD_done = None
    for gr in range(2):
        go = gr * GL
        acc_free = [[None, None] for _ in range(GT)]
        acc_done = None
        for e_i in range(8):
            w = [wd_free] if wd_free else []
            wd_tok = kb.op("sync", lambda e, e_i=e_i: e.dma_start(
                out=wd_sb[:], in_=self.wbf["wd"][e_i].rearrange("(fc p) d -> p fc d", p=128)),
                waits=w + self.prep["wd"], inc=dwd, inc_by=16)
            a_done = None
            for fc in range(FC):
                buf = (fc // 2) % 2
                if fc % 2 == 0:
                    w = [gu_free[buf]] if gu_free[buf] else []
                    for im, wn in ((0, "wg"), (1, "wu")):
                        gu_tok[buf] = kb.op("sync", lambda e, e_i=e_i, fc=fc, buf=buf, im=im, wn=wn: e.dma_start(
                            out=wgu[:, buf, im, :, :],
                            in_=self.wbf[wn][e_i].rearrange("(ic p) f -> p ic f", p=128)[:, :, fc * 128:(fc + 2) * 128]),
                            waits=(w + self.prep[wn] if im == 0 else []), inc=dgu[buf], inc_by=16)
                hw = slice((fc % 2) * 128, (fc % 2) * 128 + 128)
                for tw2 in range(2):
                    gtok = self.mm_ff(pA,
                                      [wgu[:, buf, 0, ic, hw] for ic in range(DC)],
                                      [hnT[:, ic, go + tw2 * 512:go + (tw2 + 1) * 512] for ic in range(DC)],
                                      waits=[gu_tok[buf], sA])
                    utok = self.mm_ff(pB,
                                      [wgu[:, buf, 1, ic, hw] for ic in range(DC)],
                                      [hnT[:, ic, go + tw2 * 512:go + (tw2 + 1) * 512] for ic in range(DC)],
                                      waits=[gu_tok[buf]])
                    w3 = [gtok] + ([silu_free[0]] if silu_free[0] else [])
                    stok = kb.op("scalar", lambda e: e.activation(
                        silu_scr[:], pA.t[:, :], AF.Silu), waits=w3, inc=Sc)
                    pA.rd = stok
                    w2 = [stok, utok] + ([aT_free] if aT_free else [])
                    vat = kb.op("vector", lambda e, fc=fc, tw2=tw2: e.tensor_tensor(
                        aT[:, fc, tw2 * 512:(tw2 + 1) * 512], silu_scr[:], pB.t[:, :],
                        ALU.mult), waits=w2, inc=V)
                    pB.rd = vat
                    silu_free[0] = vat
                    a_done = vat
                if fc % 2 == 1:
                    gu_free[buf] = utok
            dtok = None
            for ts in range(GT):
                for ds in range(2):
                    pD = pC if (ts * 2 + ds) % 2 == 0 else pS[0]
                    dtok = self.mm_ff(pD,
                                      [aT[:, fc, ts * 128:(ts + 1) * 128] for fc in range(FC)],
                                      waits=[a_done, wd_tok],
                                      rhs_slices=[wd_sb[:, fc, ds * 512:(ds + 1) * 512] for fc in range(FC)])
                    gts = gr * GT + ts
                    if e_i == 0:
                        va = kb.op("vector", lambda e, ts=ts, ds=ds, e_i=e_i, pD=pD, gts=gts: e.tensor_scalar(
                            acc[:, ts, ds * 512:(ds + 1) * 512], pD.t[:, :],
                            wall[:, gts, e_i:e_i + 1], None, ALU.mult),
                            waits=[dtok, router_done] + ([accD_done] if accD_done and gr == 1 else []), inc=V)
                    else:
                        va = kb.op("vector", lambda e, ts=ts, ds=ds, e_i=e_i, pD=pD, gts=gts: e.scalar_tensor_tensor(
                            acc[:, ts, ds * 512:(ds + 1) * 512], pD.t[:, :],
                            wall[:, gts, e_i:e_i + 1], acc[:, ts, ds * 512:(ds + 1) * 512],
                            ALU.mult, ALU.add), waits=[dtok, acc_free[ts][ds]], inc=V)
                    kb.drain("vector")
                    pD.rd = va
                    acc_free[ts][ds] = va
                    acc_done = va
            aT_free = dtok
            wd_free = dtok

        if gr == 0:
            out_use = None
            xv_use = [None, None]
        for t8 in range(GT):
            gt = GT * gr + t8
            sl = t8 % 2
            w = [h_use[sl]] if h_use[sl] else []
            h_tok[sl] = kb.op("sync", lambda e, b=b, gt=gt, sl=sl: e.dma_start(
                out=h2t[:, sl, :], in_=h2_ap[b, gt * 128:(gt + 1) * 128, :]),
                waits=w, inc=dh[sl], inc_by=16)
            x_use_w = [] if xv_use[sl] is None else [xv_use[sl]]
            x_tok = kb.op("sync", lambda e, b=b, gt=gt, sl=sl: e.dma_start(
                out=x1t[:, sl, :], in_=x1_ap[b, gt * 128:(gt + 1) * 128, :]),
                waits=x_use_w, inc=dx[sl], inc_by=16)
            w = [h_tok[sl], acc_done] + ([out_use] if out_use else [])
            v1 = kb.op("vector", lambda e, t8=t8, sl=sl: e.tensor_tensor(
                osb[:], h2t[:, sl, :], acc[:, t8, :], ALU.add), waits=w, inc=V)
            h_use[sl] = v1
            kb.drain("vector")
            kb.op("vector", lambda e: e.tensor_scalar(
                osb2[:], osb[:], 100.0, -100.0, ALU.min, ALU.max))
            kb.drain("vector")
            vout = kb.op("vector", lambda e, sl=sl: e.tensor_tensor(
                osb[:], osb2[:], x1t[:, sl, :], ALU.add), waits=[x_tok], inc=V)
            xv_use[sl] = vout
            accD_done = vout
            kb.drain("vector")
            otok = kb.op("sync", lambda e, b=b, gt=gt: e.dma_start(
                out=r2_ap[b, gt * 128:(gt + 1) * 128, :], in_=osb[:]),
                waits=[vout], inc=dO, inc_by=16)
            out_use = otok


Builder.stage_s3 = _stage_s3


FCH = [(0, 128), (128, 128), (256, 128), (384, 128), (512, 1)]


def _stage_s4(self, b):
    kb, al, nc = self.kb, self.al, self.nc
    T, V, Sc, dO = (kb.sem[k] for k in ("T", "V", "Sc", "dO"))
    al.begin_stage()
    x2_ap = self.dram["r2"]
    r3_ap = self.dram["r3"]
    dx = [kb.dstream("x0"), kb.dstream("x1")]
    dwc = [kb.dstream("wa"), kb.dstream("wb")]
    dbig = kb.dstream("wbig")
    dirf = kb.dstream("wbig2")

    lxT_off = al.stage_off
    lxT = al.stage("lxT", (128, DC, S), BF16)
    r1_off = al.stage_off
    strm = al.stage("strm", (128, DC, S), BF16)
    fre = al.stage("fre", (128, 5, S), BF16)
    fim = al.stage("fim", (128, 5, S), BF16)
    fqh = al.stage("fqh", (128, 2, S), BF16)
    w0 = al.stage("w0s4", (128, S), F32)
    w1 = al.stage("w1s4", (128, S), F32)
    cg0 = al.stage("cg0s4", (128, S), F32)
    cg1 = al.stage("cg1s4", (128, S), F32)

    wcol = al.stage("wcol4", (128, 2, DC, 128), BF16)
    dftc = al.stage("dftc", (128, DC, 513), BF16)
    dfts = al.stage("dfts", (128, DC, 513), BF16)
    irfa = al.stage_at("irfa", (128, 5, D), BF16, lxT_off)
    irfb = al.stage_at("irfb", (128, 5, D), BF16, r1_off)
    x2t_off = al.stage_off
    x2t = al.stage("x2t", (128, 2, D), F32)
    lntile_off4 = al.stage_off
    nw0 = al.stage_at("nw0s4", (128, 512), F32, x2t_off)
    nw1 = al.stage_at("nw1s4", (128, 512), F32, x2t_off + 2048)
    nw2 = al.stage_at("nw2s4", (128, 512), F32, x2t_off + 4096)
    lntile = al.stage("lntile4", (128, 2, D), BF16)
    osb_off4 = al.stage_off
    osb = al.stage("osb4", (128, D), F32)
    lnf = al.stage_at("lnf4", (128, D), F32, lntile_off4)
    negm = al.stage("negm4", (128, TT), F32)
    rstd = al.stage("rstd4", (128, TT), F32)
    stA = al.stage("stA4", (128, 2 * TT), F32)
    sqscr = al.stage_at("sqscr4", (128, D), F32, osb_off4)

    ident_bf = self.sb["ident_bf"]
    pA, pB, pC, pS, pT = self.pA, self.pB, self.pC, [self.pS0, self.pS1], [self.pT0, self.pT1]

    ct1 = kb.op("sync", lambda e: e.dma_start(
        out=dftc[:], in_=self.dram["dft_c"].rearrange("(ic p) f -> p ic f", p=128)),
        waits=[(dbig, dbig.n)], inc=dbig, inc_by=16)
    ct2 = kb.op("sync", lambda e: e.dma_start(
        out=dfts[:], in_=self.dram["dft_s"].rearrange("(ic p) f -> p ic f", p=128)),
        waits=[(dbig, dbig.n)], inc=dbig, inc_by=16)
    dft_tok = ct2

    x_tok = [None, None]
    x_use = [None, None]
    ln_use = [None, None]
    ptA_free = [None, None]
    vA = None
    for tp in range(0, TT, 2):
        tiles = []
        for tt in (tp, tp + 1):
            sl = tt % 2
            w = [x_use[sl]] if x_use[sl] else []
            x_tok[sl] = kb.op("sync", lambda e, b=b, tt=tt, sl=sl: e.dma_start(
                out=x2t[:, sl, :], in_=x2_ap[b, tt * 128:(tt + 1) * 128, :]),
                waits=w, inc=dx[sl], inc_by=16)
            tiles.append((x2t[:, sl, :], negm[:, tt:tt + 1], rstd[:, tt:tt + 1],
                          stA[:, 2 * tt:2 * tt + 1], stA[:, 2 * tt + 1:2 * tt + 2],
                          [x_tok[sl]]))
        stoks = emit_ln_stats2(kb, tiles, sqscr[:], self.eps)
        for ti, tt in enumerate((tp, tp + 1)):
            sl = tt % 2
            w = ([ln_use[sl]] if ln_use[sl] else []) + [stoks[ti]]
            vln = kb.op("vector", lambda e, tt=tt, sl=sl: e.tensor_scalar(
                lntile[:, sl, :], x2t[:, sl, :], negm[:, tt:tt + 1], rstd[:, tt:tt + 1],
                ALU.add, ALU.mult), waits=w, inc=V)
            x_use[sl] = vln
            for g in range(2):
                tw_ = [vln] + ([ptA_free[g]] if ptA_free[g] else [])
                ttok = None
                for j in range(4):
                    dc = g * 4 + j
                    ttok = kb.op("tensor", lambda e, g=g, j=j, dc=dc, sl=sl: e.transpose(
                        pT[g].t[:, j * 128:(j + 1) * 128], lntile[:, sl, dc * 128:(dc + 1) * 128],
                        ident_bf[:]), waits=(tw_ if j == 0 else []),
                        inc=(T if j == 3 else None))
                vev = None
                for j in range(4):
                    dc = g * 4 + j
                    vev = kb.op("vector", lambda e, g=g, j=j, dc=dc, tt=tt: e.tensor_copy(
                        lxT[:, dc, tt * 128:(tt + 1) * 128], pT[g].t[:, j * 128:(j + 1) * 128]),
                        waits=([ttok] if j == 0 else []), inc=(V if j == 3 else None))
                ptA_free[g] = vev
                ln_use[sl] = vev
        vA = ln_use[sl]

    wcol_tok = [None, None]
    wcol_use = [None, None]

    def proj_stream(wname, extra_waits):
        sdone = None
        for oc in range(8):
            buf = oc % 2
            w = [wcol_use[buf]] if wcol_use[buf] else []
            wt = kb.op("sync", lambda e, wname=wname, oc=oc, buf=buf: e.dma_start(
                out=wcol[:, buf, :, :],
                in_=self.wbf[wname].rearrange("(ic p) d -> p ic d", p=128)[:, :, oc * 128:(oc + 1) * 128]),
                waits=w + self.prep[wname], inc=dwc[buf], inc_by=16)
            last_mm = None
            for tw in range(TW):
                ps = pS[tw % 2]
                tok = self.mm_ff(ps,
                                 [wcol[:, buf, ic, :] for ic in range(DC)],
                                 [lxT[:, ic, tw * 512:(tw + 1) * 512] for ic in range(DC)],
                                 waits=[wt, vA] + list(extra_waits))
                last_mm = tok
                sdone = kb.op("scalar", lambda e, ps=ps, oc=oc, tw=tw: e.copy(
                    strm[:, oc, tw * 512:(tw + 1) * 512], ps.t[:, :]), waits=[tok], inc=Sc)
                ps.rd = sdone
            wcol_use[buf] = last_mm
        return sdone

    def dft_group(ps, mat, fc, tw, fw, f0, waits):
        return self.mm_ff(ps,
                          [mat[:, ic, f0:f0 + fw] for ic in range(DC)],
                          [strm[:, ic, tw * 512:(tw + 1) * 512] for ic in range(DC)],
                          waits=waits)

    def normalize_pair(fc, tw, fw, dst_re, dst_im, re_tok, im_tok, s1_extra=()):
        sl5 = slice(tw * 512, (tw + 1) * 512)
        s1 = kb.op("scalar", lambda e, fw=fw: e.activation(
            nw0[0:fw, :], pA.t[0:fw, :], AF.Square),
            waits=[re_tok] + list(s1_extra), inc=Sc)
        s2 = kb.op("scalar", lambda e, fw=fw: e.activation(
            nw1[0:fw, :], pB.t[0:fw, :], AF.Square), waits=[im_tok, s1], inc=Sc)
        v1 = kb.op("vector", lambda e, fw=fw: e.tensor_tensor(
            nw2[0:fw, :], nw0[0:fw, :], nw1[0:fw, :], ALU.add), waits=[s2], inc=V)
        kb.op("scalar", lambda e, fw=fw: e.activation(
            nw0[0:fw, :], nw2[0:fw, :], AF.Ln, bias=self.eps30[0][0:fw, 0:1]),
            waits=[v1, self.eps30[1]])
        kb.drain("scalar")
        s3 = kb.op("scalar", lambda e, fw=fw: e.activation(
            nw2[0:fw, :], nw0[0:fw, :], AF.Exp, scale=-0.5), inc=Sc)
        vr = kb.op("vector", lambda e, fw=fw, dst_re=dst_re, sl5=sl5: e.tensor_tensor(
            dst_re[0:fw, sl5], pA.t[0:fw, :], nw2[0:fw, :], ALU.mult), waits=[s3], inc=V)
        vi = kb.op("vector", lambda e, fw=fw, dst_im=dst_im, sl5=sl5: e.tensor_tensor(
            dst_im[0:fw, sl5], pB.t[0:fw, :], nw2[0:fw, :], ALU.mult), inc=V)
        kb.drain("vector")
        pA.rd = vr
        pB.rd = vi
        return vi

    kproj = proj_stream("awk", [])
    fk_done = None
    for fc, (f0, fw) in enumerate(FCH):
        for tw in range(TW):
            wre = [kproj, dft_tok] + ([pA.rd] if pA.rd else [])
            rt_ = dft_group(pA, dftc, fc, tw, fw, f0, wre)
            wim = [pB.rd] if pB.rd else []
            it_ = dft_group(pB, dfts, fc, tw, fw, f0, wim)
            fk_done = normalize_pair(fc, tw, fw, fre[:, fc, :], fim[:, fc, :], rt_, it_)

    vproj = proj_stream("awv", [fk_done])
    z_done = None
    for fc, (f0, fw) in enumerate(FCH):
        for tw in range(TW):
            sl5 = slice(tw * 512, (tw + 1) * 512)
            wre = [vproj] + ([pA.rd] if pA.rd else [])
            rt_ = dft_group(pA, dftc, fc, tw, fw, f0, wre)
            wim = [pB.rd] if pB.rd else []
            it_ = dft_group(pB, dfts, fc, tw, fw, f0, wim)
            va = kb.op("vector", lambda e, fw=fw, fc=fc, sl5=sl5: e.tensor_tensor(
                nw0[0:fw, :], fre[0:fw, fc, sl5], pA.t[0:fw, :], ALU.mult),
                waits=[rt_, fk_done], inc=V)
            vb = kb.op("vector", lambda e, fw=fw, fc=fc, sl5=sl5: e.tensor_tensor(
                nw1[0:fw, :], fim[0:fw, fc, sl5], pB.t[0:fw, :], ALU.mult),
                waits=[it_], inc=V)
            vc = kb.op("vector", lambda e, fw=fw, fc=fc, sl5=sl5: e.tensor_tensor(
                nw2[0:fw, :], fre[0:fw, fc, sl5], pB.t[0:fw, :], ALU.mult), inc=V)
            pB.rd = vc
            kb.drain("vector")
            vz1 = kb.op("vector", lambda e, fw=fw, fc=fc, sl5=sl5: e.tensor_tensor(
                fre[0:fw, fc, sl5], nw0[0:fw, :], nw1[0:fw, :], ALU.subtract), inc=V)
            kb.drain("vector")
            vd = kb.op("vector", lambda e, fw=fw, fc=fc, sl5=sl5: e.tensor_tensor(
                nw0[0:fw, :], fim[0:fw, fc, sl5], pA.t[0:fw, :], ALU.mult), inc=V)
            pA.rd = vd
            kb.drain("vector")
            z_done = kb.op("vector", lambda e, fw=fw, fc=fc, sl5=sl5: e.tensor_tensor(
                fim[0:fw, fc, sl5], nw2[0:fw, :], nw0[0:fw, :], ALU.add), inc=V)
            kb.drain("vector")

    qproj = proj_stream("awq", [z_done])
    irf_tok = None
    y_done = []
    for fc, (f0, fw) in enumerate(FCH):
        fq_done = None
        fqh_guard = y_done
        for tw in range(TW):
            wre = [qproj, (T, T.n)] if fc == 0 and tw == 0 else [qproj]
            wre = wre + ([pA.rd] if pA.rd else [])
            rt_ = dft_group(pA, dftc, fc, tw, fw, f0, wre)
            wim = [pB.rd] if pB.rd else []
            it_ = dft_group(pB, dfts, fc, tw, fw, f0, wim)
            fq_done = normalize_pair(fc, tw, fw, fqh[:, 0, :], fqh[:, 1, :], rt_, it_,
                                     s1_extra=(fqh_guard if fqh_guard and tw == 0 else []))
        G = kb.sem["G"]
        yprev = y_done
        csd = {}
        for comp, buf_, eng, b0, b1, sem in ((0, fre, "gpsimd", cg0, cg1, G),
                                             (1, fim, "vector", w0, w1, V)):
            w_ = [z_done, fq_done] + (yprev if yprev else [])
            kb.op(eng, lambda e, fw=fw, fc=fc, buf_=buf_, b0=b0: e.tensor_copy(
                b0[0:fw, :], buf_[0:fw, fc, :]), waits=w_)
            kb.drain(eng)
            src, dst = b0, b1
            sstep = 1
            while sstep < S:
                kb.op(eng, lambda e, fw=fw, src=src, dst=dst, sstep=sstep: e.tensor_copy(
                    dst[0:fw, 0:sstep], src[0:fw, 0:sstep]))
                kb.op(eng, lambda e, fw=fw, src=src, dst=dst, sstep=sstep: e.tensor_tensor(
                    dst[0:fw, sstep:S], src[0:fw, sstep:S], src[0:fw, 0:S - sstep], ALU.add))
                kb.drain(eng)
                src, dst = dst, src
                sstep *= 2
            csd[comp] = kb.op(eng, lambda e, fw=fw, fc=fc, buf_=buf_, src=src: e.tensor_copy(
                buf_[0:fw, fc, :], src[0:fw, :]), inc=sem)
            kb.drain(eng)
        t0v = kb.op("vector", lambda e, fw=fw, fc=fc: e.tensor_tensor(
            w0[0:fw, :], fre[0:fw, fc, :], fqh[0:fw, 0, :], ALU.mult),
            waits=[csd[0], csd[1]], inc=V)
        t1v = kb.op("vector", lambda e, fw=fw, fc=fc: e.tensor_tensor(
            w1[0:fw, :], fim[0:fw, fc, :], fqh[0:fw, 1, :], ALU.mult), inc=V)
        t2g = kb.op("gpsimd", lambda e, fw=fw, fc=fc: e.tensor_tensor(
            cg0[0:fw, :], fim[0:fw, fc, :], fqh[0:fw, 0, :], ALU.mult),
            waits=[csd[0], csd[1]], inc=G)
        t3g = kb.op("gpsimd", lambda e, fw=fw, fc=fc: e.tensor_tensor(
            cg1[0:fw, :], fre[0:fw, fc, :], fqh[0:fw, 1, :], ALU.mult), inc=G)
        kb.drain("vector")
        kb.drain("gpsimd")
        yre = kb.op("vector", lambda e, fw=fw, fc=fc: e.tensor_tensor(
            fre[0:fw, fc, :], w0[0:fw, :], w1[0:fw, :], ALU.add), waits=[t3g], inc=V)
        yim = kb.op("gpsimd", lambda e, fw=fw, fc=fc: e.tensor_tensor(
            fim[0:fw, fc, :], cg0[0:fw, :], cg1[0:fw, :], ALU.subtract), waits=[t1v], inc=G)
        kb.drain("vector")
        kb.drain("gpsimd")
        y_done = [yre, yim]

    irf_tok = kb.op("sync", lambda e: e.dma_start(
        out=irfa[:], in_=self.dram["irf_a"].rearrange("(fc p) d -> p fc d", p=128)),
        waits=[qproj, (T, T.n), (dirf, dirf.n)], inc=dirf, inc_by=16)
    irf_tok = kb.op("sync", lambda e: e.dma_start(
        out=irfb[:], in_=self.dram["irf_b"].rearrange("(fc p) d -> p fc d", p=128)),
        waits=[(dirf, dirf.n)], inc=dirf, inc_by=16)

    out_use = None
    osb_use = None
    for tt in range(TT):
        sl = tt % 2
        w = [x_use[sl]] if x_use[sl] else []
        if tt < 2:
            w = w + y_done
        x_tok[sl] = kb.op("sync", lambda e, b=b, tt=tt, sl=sl: e.dma_start(
            out=x2t[:, sl, :], in_=x2_ap[b, tt * 128:(tt + 1) * 128, :]),
            waits=w, inc=dx[sl], inc_by=16)
        vlnf = kb.op("vector", lambda e, tt=tt, sl=sl: e.tensor_scalar(
            lnf[:], x2t[:, sl, :], negm[:, tt:tt + 1], rstd[:, tt:tt + 1], ALU.add, ALU.mult),
            waits=[x_tok[sl]], inc=V)
        kb.drain("vector")
        ev = None
        for ds in range(2):
            ps = pS[ds]
            terms = []
            for fc, (f0, fw) in enumerate(FCH):
                terms.append((fre[0:fw, fc, tt * 128:(tt + 1) * 128],
                              irfa[0:fw, fc, ds * 512:(ds + 1) * 512]))
                terms.append((fim[0:fw, fc, tt * 128:(tt + 1) * 128],
                              irfb[0:fw, fc, ds * 512:(ds + 1) * 512]))
            tok = self.mm_ff(ps, [a_ for a_, b_ in terms], [b_ for a_, b_ in terms],
                             waits=y_done + [irf_tok] + ([osb_use] if osb_use else []))
            ev = kb.op("vector", lambda e, ps=ps, ds=ds: e.tensor_tensor(
                osb[:, ds * 512:(ds + 1) * 512], ps.t[:, :], lnf[:, ds * 512:(ds + 1) * 512],
                ALU.add), waits=[tok, vlnf], inc=V)
            ps.rd = ev
        kb.drain("vector")
        vout = kb.op("vector", lambda e, sl=sl: e.tensor_tensor(
            osb[:], osb[:], x2t[:, sl, :], ALU.add), waits=[ev], inc=V)
        x_use[sl] = vout
        osb_use = vout
        kb.drain("vector")
        otok = kb.op("sync", lambda e, b=b, tt=tt: e.dma_start(
            out=r3_ap[b, tt * 128:(tt + 1) * 128, :], in_=osb[:]),
            waits=[vout], inc=dO, inc_by=16)
        osb_use = otok
        out_use = otok


Builder.stage_s4 = _stage_s4


def _stage_s5(self, b):
    kb, al, nc = self.kb, self.al, self.nc
    T, V, Sc, dO = (kb.sem[k] for k in ("T", "V", "Sc", "dO"))
    al.begin_stage()
    x3_ap = self.dram["r3"]
    out_ap = self.dram["out"]
    dx = [kb.dstream("x0"), kb.dstream("x1")]
    dwd = kb.dstream("wd0")
    dgu = [kb.dstream("ga"), kb.dstream("gb")]

    GL = 1024
    GT = GL // 128

    hnT = al.stage("hnT5", (128, DC, S), BF16)
    hnTf_off = al.stage_off
    hnTf = al.stage("hnTf5", (128, DC, GL), F32)
    aT = al.stage_at("aT5", (128, FC, GL), BF16, hnTf_off)
    acc = al.stage("acc5", (128, GT, D), F32)
    wd_sb = al.stage("wd_sb5", (128, FC, D), BF16)
    wgu = al.stage("wgu5", (128, 2, 2, DC, 256), BF16)
    wall = al.stage("wall5", (128, TT, 8), F32)
    h2t = al.stage("x3t5", (128, 2, D), F32)
    hnt_off5 = al.stage_off
    hnt = al.stage("hnt5", (128, D), F32)
    silu_scr = al.stage("silu5", (128, 512), F32)
    osb = al.stage_at("osb5", (128, D), F32, hnt_off5)
    negm = al.stage("negm5", (128, TT), F32)
    rstd = al.stage("rstd5", (128, TT), F32)
    stA = al.stage("stA5", (128, 2 * TT), F32)
    sqscr = al.stage("sqscr5", (128, D), F32)
    rt = al.stage("rt5", (128, 8, 8), F32)
    rt1 = al.stage("rt15", (128, 8), F32)

    ident_f = self.sb["ident_f"]
    pA, pB, pC, pS = self.pA, self.pB, self.pC, [self.pS0, self.pS1]

    experts = [("sw1t", "sw3t", "sw2t", 0, "sh0"), ("sw1t", "sw3t", "sw2t", 1, "sh1")] + \
              [("w1t", "w3t", "w2t", e_i, "rt") for e_i in range(8)]

    h_tok = [None, None]
    h_use = [None, None]
    hnt_use = None
    pt_free = [None, None]
    rt_done = None
    vA = None
    sA = None
    for tp in range(0, TT, 2):
        tiles = []
        for gt in (tp, tp + 1):
            sl = gt % 2
            w = [h_use[sl]] if h_use[sl] else []
            h_tok[sl] = kb.op("sync", lambda e, b=b, gt=gt, sl=sl: e.dma_start(
                out=h2t[:, sl, :], in_=x3_ap[b, gt * 128:(gt + 1) * 128, :]),
                waits=w, inc=dx[sl], inc_by=16)
            tiles.append((h2t[:, sl, :], negm[:, gt:gt + 1], rstd[:, gt:gt + 1],
                          stA[:, 2 * gt:2 * gt + 1], stA[:, 2 * gt + 1:2 * gt + 2],
                          [h_tok[sl]]))
        stoks = emit_ln_stats2(kb, tiles, sqscr[:], self.eps)
        for ti, gt in enumerate((tp, tp + 1)):
            sl = gt % 2
            stok = stoks[ti]
            w = ([hnt_use] if hnt_use else []) + [stok]
            vh = kb.op("vector", lambda e, gt=gt, sl=sl: e.tensor_scalar(
                hnt[:], h2t[:, sl, :], negm[:, gt:gt + 1], rstd[:, gt:gt + 1],
                ALU.add, ALU.mult), waits=w, inc=V)
            h_use[sl] = vh
            kb.drain("vector")
            t8 = gt % GT
            for g in range(2):
                ps = pS[g]
                tw_ = [vh] + (pt_free[g] if pt_free[g] else [])
                ttok = None
                for j in range(4):
                    dc = g * 4 + j
                    ttok = kb.op("tensor", lambda e, ps=ps, j=j, dc=dc: e.transpose(
                        ps.t[:, j * 128:(j + 1) * 128], hnt[:, dc * 128:(dc + 1) * 128],
                        ident_f[:]), waits=(tw_ if j == 0 else []),
                        inc=(T if j == 3 else None))
                vev = None
                for j in range(4):
                    dc = g * 4 + j
                    vev = kb.op("vector", lambda e, ps=ps, j=j, dc=dc, t8=t8: e.tensor_copy(
                        hnTf[:, dc, t8 * 128:(t8 + 1) * 128], ps.t[:, j * 128:(j + 1) * 128]),
                        waits=([ttok] if j == 0 else []), inc=V)
                sev = None
                for j in range(4):
                    dc = g * 4 + j
                    sev = kb.op("scalar", lambda e, j=j, dc=dc, gt=gt, t8=t8: e.copy(
                        hnT[:, dc, gt * 128:(gt + 1) * 128], hnTf[:, dc, t8 * 128:(t8 + 1) * 128]),
                        waits=([(V, kb.sem["V"].n)] if j == 0 else []),
                        inc=(Sc if j == 3 else None))
                pt_free[g] = [vev, sev]
                hnt_use = vev
            vA = (V, V.n)
            sA = (Sc, Sc.n)
            ps = pS[gt % 2]
            w = [vA, sA] + ([ps.rd] if ps.rd else [])
            for ic in range(DC):
                kb.op("tensor", lambda e, ps=ps, ic=ic, t8=t8: e.matmul(
                    ps.t[:, 0:8], hnTf[:, ic, t8 * 128:(t8 + 1) * 128],
                    self.sb["rw3"][:, ic, :], start=(ic == 0), stop=False),
                    waits=(w if ic == 0 else []))
            tok = kb.op("tensor", lambda e, ps=ps: e.matmul(
                ps.t[:, 0:8], self.sb["ones_f"][0:1, :], self.sb["ebias_row"][0:1, :],
                start=False, stop=True), inc=T)
            lg = rt[:, 0, :]
            pexp = rt[:, 1, :]
            pm = rt[:, 2, :]
            keep = rt[:, 3, :]
            wr_ = rt[:, 4, :]
            m1 = rt1[:, 0:1]
            negm1 = rt1[:, 1:2]
            m2 = rt1[:, 2:3]
            den = rt1[:, 3:4]
            rden = rt1[:, 4:5]
            w = [tok] + ([rt_done] if rt_done else [])
            vlg = kb.op("vector", lambda e, ps=ps: e.tensor_copy(lg, ps.t[:, 0:8]),
                        waits=w, inc=V)
            ps.rd = vlg
            kb.drain("vector")
            kb.op("vector", lambda e: e.reduce_max(m1, lg, axis=AX.X))
            kb.drain("vector")
            v2 = kb.op("vector", lambda e: e.tensor_single_scalar(negm1, m1, -1.0, ALU.mult), inc=V)
            kb.drain("vector")
            se = kb.op("scalar", lambda e: e.activation(pexp, lg, AF.Exp, bias=negm1),
                       waits=[v2], inc=Sc)
            kb.op("vector", lambda e: e.tensor_single_scalar(keep, lg, m1, ALU.is_ge))
            kb.drain("vector")
            kb.op("vector", lambda e: e.scalar_tensor_tensor(pm, keep, -1e30, lg, ALU.mult, ALU.add))
            kb.drain("vector")
            kb.op("vector", lambda e: e.reduce_max(m2, pm, axis=AX.X))
            kb.drain("vector")
            kb.op("vector", lambda e: e.tensor_single_scalar(keep, lg, m2, ALU.is_ge))
            kb.drain("vector")
            kb.op("vector", lambda e: e.tensor_tensor(wr_, pexp, keep, ALU.mult), waits=[se])
            kb.drain("vector")
            kb.op("vector", lambda e: e.reduce_sum(den, wr_, axis=AX.X))
            kb.drain("vector")
            kb.op("vector", lambda e: e.reciprocal(rden, den))
            kb.drain("vector")
            rt_done = kb.op("vector", lambda e, gt=gt: e.tensor_single_scalar(
                wall[:, gt, :], wr_, rden, ALU.mult), inc=V)
            kb.drain("vector")
    router_done = rt_done

    wd_tok = None
    wd_free = None
    silu_free = [None]
    gu_tok = [None, None]
    gu_free = [None, None]
    aT_free = None
    accD_done = None
    for gr in range(2):
        go = gr * GL
        acc_free = [[None, None] for _ in range(GT)]
        acc_done = None
        for ei, (n1, n3, n2, e_i, kind) in enumerate(experts):
            w = [wd_free] if wd_free else []
            wd_tok = kb.op("sync", lambda e, n2=n2, e_i=e_i: e.dma_start(
                out=wd_sb[:], in_=self.wbf[n2][e_i].rearrange("(fc p) d -> p fc d", p=128)),
                waits=w + self.prep[n2], inc=dwd, inc_by=16)
            a_done = None
            for fc in range(FC):
                buf = (fc // 2) % 2
                if fc % 2 == 0:
                    w = [gu_free[buf]] if gu_free[buf] else []
                    for im, wn in ((0, n1), (1, n3)):
                        gu_tok[buf] = kb.op("sync", lambda e, e_i=e_i, fc=fc, buf=buf, im=im, wn=wn: e.dma_start(
                            out=wgu[:, buf, im, :, :],
                            in_=self.wbf[wn][e_i].rearrange("(ic p) f -> p ic f", p=128)[:, :, fc * 128:(fc + 2) * 128]),
                            waits=(w + self.prep[wn] if im == 0 else []), inc=dgu[buf], inc_by=16)
                hw = slice((fc % 2) * 128, (fc % 2) * 128 + 128)
                for tw2 in range(2):
                    gtok = self.mm_ff(pA,
                                      [wgu[:, buf, 0, ic, hw] for ic in range(DC)],
                                      [hnT[:, ic, go + tw2 * 512:go + (tw2 + 1) * 512] for ic in range(DC)],
                                      waits=[gu_tok[buf], sA])
                    utok = self.mm_ff(pB,
                                      [wgu[:, buf, 1, ic, hw] for ic in range(DC)],
                                      [hnT[:, ic, go + tw2 * 512:go + (tw2 + 1) * 512] for ic in range(DC)],
                                      waits=[gu_tok[buf]])
                    w3 = [gtok] + ([silu_free[0]] if silu_free[0] else [])
                    stok = kb.op("scalar", lambda e: e.activation(
                        silu_scr[:], pA.t[:, :], AF.Silu), waits=w3, inc=Sc)
                    pA.rd = stok
                    w2 = [stok, utok] + ([aT_free] if aT_free else [])
                    vat = kb.op("vector", lambda e, fc=fc, tw2=tw2: e.tensor_tensor(
                        aT[:, fc, tw2 * 512:(tw2 + 1) * 512], silu_scr[:], pB.t[:, :],
                        ALU.mult), waits=w2, inc=V)
                    pB.rd = vat
                    silu_free[0] = vat
                    a_done = vat
                if fc % 2 == 1:
                    gu_free[buf] = utok
            dtok = None
            for ts in range(GT):
                for ds in range(2):
                    pD = pC if (ts * 2 + ds) % 2 == 0 else pS[0]
                    dtok = self.mm_ff(pD,
                                      [aT[:, fc, ts * 128:(ts + 1) * 128] for fc in range(FC)],
                                      waits=[a_done, wd_tok],
                                      rhs_slices=[wd_sb[:, fc, ds * 512:(ds + 1) * 512] for fc in range(FC)])
                    gts = gr * GT + ts
                    if kind == "sh0":
                        va = kb.op("vector", lambda e, ts=ts, ds=ds, pD=pD: e.tensor_copy(
                            acc[:, ts, ds * 512:(ds + 1) * 512], pD.t[:, :]),
                            waits=[dtok] + ([accD_done] if accD_done and gr == 1 else []), inc=V)
                    elif kind == "sh1":
                        va = kb.op("vector", lambda e, ts=ts, ds=ds, pD=pD: e.tensor_tensor(
                            acc[:, ts, ds * 512:(ds + 1) * 512], pD.t[:, :],
                            acc[:, ts, ds * 512:(ds + 1) * 512], ALU.add),
                            waits=[dtok, acc_free[ts][ds]], inc=V)
                    else:
                        va = kb.op("vector", lambda e, ts=ts, ds=ds, e_i=e_i, pD=pD, gts=gts: e.scalar_tensor_tensor(
                            acc[:, ts, ds * 512:(ds + 1) * 512], pD.t[:, :],
                            wall[:, gts, e_i:e_i + 1], acc[:, ts, ds * 512:(ds + 1) * 512],
                            ALU.mult, ALU.add), waits=[dtok, acc_free[ts][ds], router_done], inc=V)
                    kb.drain("vector")
                    pD.rd = va
                    acc_free[ts][ds] = va
                    acc_done = va
            aT_free = dtok
            wd_free = dtok

        if gr == 0:
            out_use = None
        for t8 in range(GT):
            gt = GT * gr + t8
            sl = t8 % 2
            w = [h_use[sl]] if h_use[sl] else []
            h_tok[sl] = kb.op("sync", lambda e, b=b, gt=gt, sl=sl: e.dma_start(
                out=h2t[:, sl, :], in_=x3_ap[b, gt * 128:(gt + 1) * 128, :]),
                waits=w, inc=dx[sl], inc_by=16)
            w = [h_tok[sl], acc_done] + ([out_use] if out_use else [])
            vout = kb.op("vector", lambda e, t8=t8, sl=sl: e.tensor_tensor(
                osb[:], h2t[:, sl, :], acc[:, t8, :], ALU.add), waits=w, inc=V)
            h_use[sl] = vout
            accD_done = vout
            kb.drain("vector")
            otok = kb.op("sync", lambda e, b=b, gt=gt: e.dma_start(
                out=out_ap[b, gt * 128:(gt + 1) * 128, :], in_=osb[:]),
                waits=[vout], inc=dO, inc_by=16)
            out_use = otok


Builder.stage_s5 = _stage_s5


def _stage_s1_v2(self, b):
    kb, al, nc = self.kb, self.al, self.nc
    T, V, Sc, dO = (kb.sem[k] for k in ("T", "V", "Sc", "dO"))
    al.begin_stage()
    x_ap = self.dram["x_in"]
    r1_ap = self.dram["r1"]
    dx = [kb.dstream("x0"), kb.dstream("x1")]
    dwc = [kb.dstream("wa"), kb.dstream("wb")]
    dbig = kb.dstream("wbig")

    lnT = al.stage("lnT", (128, DC, S), BF16)
    ctxT = al.stage("ctxT1", (128, DC, S), BF16)
    qkv_off = al.stage_off
    qT = al.stage("qT1", (128, DC, 512), BF16)
    kT = al.stage("kT1", (128, DC, 512), BF16)
    vT = al.stage("vT1", (128, DC, 512), BF16)
    wo_sb = al.stage_at("wo_sb1", (128, DC, D), BF16, qkv_off)
    esb_off = al.stage_off
    esb = al.stage("esb1", (128, TT, 512), BF16)
    bB = al.stage_at("bB1", (128, 3, D), F32, esb_off)
    qp2 = [al.stage(f"qp1_{i}", (65, S), BF16) for i in range(2)]
    kp2 = [al.stage(f"kp1_{i}", (65, S), BF16) for i in range(2)]
    vpT2 = [al.stage(f"vpT1_{i}", (64, S), BF16) for i in range(2)]
    ctxp = al.stage("ctxp1", (64, S), BF16)
    vp2 = [al.stage(f"vp1_{i}", (128, TT, 65), BF16) for i in range(2)]
    wcol = al.stage("wcol1", (128, 2, 3, DC, 128), BF16)
    x0t = al.stage("x0t1", (128, 2, D), F32)
    lntile_off = al.stage_off
    lntile = al.stage("lntile1", (128, 2, D), BF16)
    osb_off1 = al.stage_off
    osb = al.stage("osb1", (128, D), F32)
    lnf_off1 = al.stage_off
    lnf = al.stage("lnf1", (128, D), F32)
    a2 = al.stage("a21", (128, D), F32)
    negm = al.stage("negm1", (128, TT), F32)
    rstd = al.stage("rstd1", (128, TT), F32)
    stA = al.stage("stA1", (128, 2 * TT), F32)
    sqscr_off = al.stage_off
    sqscr = al.stage("sqscr1", (128, D), F32)
    sqh = al.stage_at("sqh1", (64, S), BF16, lntile_off)
    qn_sb = al.stage("qn_sb1", (1, S), BF16)
    kn_sb = al.stage_at("kn_sb1", (1, S), BF16, sqscr_off)
    kmax = al.stage("kmax1", (1, 2), F32)
    rz = al.stage_at("rz1", (1, 512), F32, osb_off1)
    rzB = al.stage_at("rzB1", (64, 512), F32, osb_off1 + 2048)
    rzl = al.stage_at("rzl1", (1, 512), F32, lnf_off1)

    ident_bf = self.sb["ident_bf"]
    ones64 = self.sb["ones64"]
    pA, pB, pS, pC, pX, pT = self.pA, self.pB, [self.pS0, self.pS1], self.pC, self.pX, [self.pT0, self.pT1]

    x0_tok = [None, None]
    x0_use = [None, None]
    ln_use = [None, None]
    ptA_free = [None, None]
    vA = None
    for tp in range(0, TT, 2):
        tiles = []
        for tt in (tp, tp + 1):
            sl = tt % 2
            w = [x0_use[sl]] if x0_use[sl] else []
            x0_tok[sl] = kb.op("sync", lambda e, b=b, tt=tt, sl=sl: e.dma_start(
                out=x0t[:, sl, :], in_=x_ap[b, tt * 128:(tt + 1) * 128, :]),
                waits=w, inc=dx[sl], inc_by=16)
            tiles.append((x0t[:, sl, :], negm[:, tt:tt + 1], rstd[:, tt:tt + 1],
                          stA[:, 2 * tt:2 * tt + 1], stA[:, 2 * tt + 1:2 * tt + 2],
                          [x0_tok[sl]]))
        stoks = emit_ln_stats2(kb, tiles, sqscr[:], self.eps)
        for ti, tt in enumerate((tp, tp + 1)):
            sl = tt % 2
            w = ([ln_use[sl]] if ln_use[sl] else []) + [stoks[ti]]
            vln = kb.op("vector", lambda e, tt=tt, sl=sl: e.tensor_scalar(
                lntile[:, sl, :], x0t[:, sl, :], negm[:, tt:tt + 1], rstd[:, tt:tt + 1],
                ALU.add, ALU.mult), waits=w, inc=V)
            x0_use[sl] = vln
            for g in range(2):
                tw_ = [vln] + ([ptA_free[g]] if ptA_free[g] else [])
                ttok = None
                for j in range(4):
                    dc = g * 4 + j
                    ttok = kb.op("tensor", lambda e, g=g, j=j, dc=dc, sl=sl: e.transpose(
                        pT[g].t[:, j * 128:(j + 1) * 128], lntile[:, sl, dc * 128:(dc + 1) * 128],
                        ident_bf[:]), waits=(tw_ if j == 0 else []),
                        inc=(T if j == 3 else None))
                vev = None
                for j in range(4):
                    dc = g * 4 + j
                    vev = kb.op("vector", lambda e, g=g, j=j, dc=dc, tt=tt: e.tensor_copy(
                        lnT[:, dc, tt * 128:(tt + 1) * 128], pT[g].t[:, j * 128:(j + 1) * 128]),
                        waits=([ttok] if j == 0 else []), inc=(V if j == 3 else None))
                ptA_free[g] = vev
                ln_use[sl] = vev
        vA = ln_use[sl]

    vset = kb.op("vector", lambda e: (e.memset(vp2[0][:, :, 64:65], 1.0),
                                      e.memset(vp2[1][:, :, 64:65], 1.0),
                                      e.memset(kp2[0][64:65, :], 1.0),
                                      e.memset(kp2[1][64:65, :], 1.0))[-1], inc=V)

    wcol_tok = [None, None]
    wcol_use = [None, None]
    qkv_read = None
    vkm_prev = [None]
    vn_prev = [None]
    st = {"sq_reader": None}
    pp_read = [None, None]
    esb_free = None
    exp_tok = [None] * TT
    bank_exp = {}
    sq_reader = None
    attn_done = None
    sct = None
    vpt_rd = {}
    for qd in range(4):
        ev_tok = {}
        for oc in range(8):
            buf = (qd * 8 + oc) % 2
            w = [wcol_use[buf]] if wcol_use[buf] else []
            for m3 in range(3):
                wcol_tok[buf] = kb.op("sync", lambda e, oc=oc, buf=buf, m3=m3: e.dma_start(
                    out=wcol[:, buf, m3, :, :],
                    in_=self.wbf["wqkv"].rearrange("(ic p) d -> p ic d", p=128)[:, :, m3 * D + oc * 128: m3 * D + (oc + 1) * 128]),
                    waits=(w + self.prep["wqkv"] if m3 == 0 else []), inc=dwc[buf], inc_by=16)
            wready = wcol_tok[buf]
            for im, (dst, brow) in enumerate(((qT, "bq3"), (kT, "bk3"), (vT, "bv3"))):
                ps = pA if im % 2 == 0 else pB
                tok = self.mm_ff(
                    ps,
                    [wcol[:, buf, im, ic, :] for ic in range(DC)],
                    [lnT[:, ic, qd * 512:(qd + 1) * 512] for ic in range(DC)],
                    waits=[wready, vA] + (qkv_read if qkv_read and oc == 0 else []))
                ev_tok[(im, oc)] = kb.op("scalar", lambda e, ps=ps, dst=dst, oc=oc, brow=brow: e.activation(
                    dst[:, oc, :], ps.t[:, :], AF.Identity,
                    bias=self.sb[brow][:, oc:oc + 1]), waits=[tok], inc=Sc)
                ps.rd = ev_tok[(im, oc)]
            wcol_use[buf] = tok
        qkv_done = ev_tok[(2, 7)]

        def emit_prep(gw):
            par = gw % 2
            qp, kp, vpT, vp = qp2[par], kp2[par], vpT2[par], vp2[par]
            wl = (gw % 4) * 128
            bldv = None
            bldg = None
            for im, (src, dst) in enumerate(((qT, qp), (kT, kp), (vT, vpT))):
                for pc in range(2):
                    eng = "vector" if (im * 2 + pc) % 2 == 0 else "gpsimd"
                    sem = V if eng == "vector" else kb.sem["G"]
                    w = [qkv_done] + ([pp_read[par]] if pp_read[par] and im == 0 and pc == 0 else [])
                    tok = kb.op(eng, lambda e, src=src, dst=dst, pc=pc, wl=wl: e.tensor_copy(
                        dst.rearrange("p (sx c2 two) -> p two c2 sx", two=2, c2=8)[0:64, pc, :, :],
                        src[64 * pc:64 * pc + 64, :, wl:wl + 128]),
                        waits=w, inc=sem)
                    if eng == "vector":
                        bldv = tok
                    else:
                        bldg = tok
            kb.drain("vector")
            kb.drain("gpsimd")
            bld_done = [bldv, bldg]
            vv_done = None
            for g in range(2):
                w = bld_done + ([vpt_rd[(par, g)]] if vpt_rd.get((par, g)) else [])
                ttok = None
                for j in range(8):
                    kc = g * 8 + j
                    ttok = kb.op("tensor", lambda e, g=g, j=j, kc=kc, vpT=vpT: e.transpose(
                        pT[g].t[:, j * 64:(j + 1) * 64], vpT[0:64, kc * 128:(kc + 1) * 128],
                        ident_bf[0:64, 0:64]), waits=(w if j == 0 else []),
                        inc=(T if j == 7 else None))
                vv = None
                for j in range(8):
                    kc = g * 8 + j
                    vv = kb.op("vector", lambda e, g=g, j=j, kc=kc, vp=vp: e.tensor_copy(
                        vp[:, kc, 0:64], pT[g].t[:, j * 64:(j + 1) * 64]),
                        waits=([ttok, vset] if j == 0 else []), inc=(V if j == 7 else None))
                vpt_rd[(par, g)] = vv
                vv_done = vv
            w = bld_done + ([st["sq_reader"]] if st["sq_reader"] else [])
            sqq = kb.op("scalar", lambda e, qp=qp: e.activation(
                sqh[0:64, :], qp[0:64, :], AF.Square), waits=w, inc=Sc)
            qn_done = None
            mtok = None
            for tw in range(TW):
                w = [sqq] + ([pX.rd] if pX.rd else [])
                mtok = kb.op("tensor", lambda e, tw=tw: e.matmul(
                    pX.t[0:1, :], ones64[0:64, 0:1], sqh[0:64, tw * 512:(tw + 1) * 512],
                    start=True, stop=True), waits=w, inc=T)
                qn_done = kb.op("scalar", lambda e, tw=tw: e.copy(
                    qn_sb[0:1, tw * 512:(tw + 1) * 512], pX.t[0:1, :]),
                    waits=[mtok], inc=Sc)
                pX.rd = qn_done
            sqk = kb.op("scalar", lambda e, kp=kp: e.activation(
                sqh[0:64, :], kp[0:64, :], AF.Square),
                waits=[qn_done, mtok] + ([vkm_prev[0]] if vkm_prev[0] else []), inc=Sc)
            kn_done = None
            for tw in range(TW):
                w = [sqk] + ([pX.rd] if pX.rd else [])
                mtok = kb.op("tensor", lambda e, tw=tw: e.matmul(
                    pX.t[0:1, :], ones64[0:64, 0:1], sqh[0:64, tw * 512:(tw + 1) * 512],
                    start=True, stop=True), waits=w, inc=T)
                kn_done = kb.op("scalar", lambda e, tw=tw: e.copy(
                    kn_sb[0:1, tw * 512:(tw + 1) * 512], pX.t[0:1, :]), waits=[mtok], inc=Sc)
                pX.rd = kn_done
            st["sq_reader"] = mtok
            vkm = kb.op("vector", lambda e: e.reduce_max(kmax[0:1, 0:1], kn_sb[0:1, :], axis=AX.X),
                        waits=[kn_done], inc=V)
            vkm_prev[0] = vkm
            kb.drain("vector")
            stab = kb.op("vector", lambda e, qp=qp: e.tensor_scalar(
                qp[64:65, :], qn_sb[0:1, :], kmax[0:1, 0:1], -0.5, ALU.add, ALU.mult),
                waits=[vkm], inc=V)
            return {"stab": stab, "vv_done": vv_done, "par": par}

        def emit_attn(gw, prep, next_prep_gw):
            nonlocal esb_free, attn_done, sct
            par = prep["par"]
            qp, kp, vp = qp2[par], kp2[par], vp2[par]
            stab = prep["stab"]
            vv_done = prep["vv_done"]
            s0 = (gw // 4) * 512 + (gw % 4) * 128
            pSC = [pS[0], pS[1], pA]
            pCX = [pC, pB]
            btok_prev = [None]
            nprep = None
            sc_last = None
            for tw in range(TW):
                pcx = pCX[tw % 2]
                for kc in range(TT):
                    pb_ = pSC[kc % 3]
                    w = [stab, vset]
                    if bank_exp.get(kc % 3):
                        w.append(bank_exp[kc % 3])
                    if esb_free:
                        w.append(esb_free)
                    mtok = kb.op("tensor", lambda e, pb_=pb_, kc=kc, tw=tw, kp=kp, qp=qp: e.matmul(
                        pb_.t[:, :], kp[:, kc * 128:(kc + 1) * 128], qp[:, tw * 512:(tw + 1) * 512],
                        start=True, stop=True), waits=w, inc=T)
                    sc_last = mtok
                    etok = kb.op("scalar", lambda e, pb_=pb_, kc=kc: e.activation(
                        esb[:, kc, :], pb_.t[:, :], AF.Exp, scale=0.125),
                        waits=[mtok], inc=Sc)
                    bank_exp[kc % 3] = etok
                    exp_tok[kc] = etok
                ctx_tok = None
                for kc in range(TT):
                    w = [exp_tok[kc], vv_done]
                    if kc == 0 and pcx.rd:
                        w.append(pcx.rd)
                    ctx_tok = kb.op("tensor", lambda e, kc=kc, vp=vp, pcx=pcx: e.matmul(
                        pcx.t[0:65, :], vp[:, kc, :], esb[:, kc, :],
                        start=(kc == 0), stop=(kc == TT - 1)),
                        waits=w, inc=(T if kc == TT - 1 else None))
                esb_free = ctx_tok
                w = [ctx_tok] + ([btok_prev[0]] if btok_prev[0] else [])
                kb.op("scalar", lambda e, pcx=pcx: e.activation(rzl[0:1, :], pcx.t[64:65, :], AF.Ln),
                      waits=w)
                kb.drain("scalar")
                vrz = kb.op("scalar", lambda e: e.activation(rz[0:1, :], rzl[0:1, :], AF.Exp,
                                                             scale=-1.0), inc=Sc)
                w = [vrz] + ([pX.rd] if pX.rd else [])
                btok = kb.op("tensor", lambda e: e.matmul(
                    pX.t[0:64, :], self.sb["ones_f"][0:1, 0:64], rz[0:1, :],
                    start=True, stop=True), waits=w, inc=T)
                btok_prev[0] = btok
                vcp = kb.op("vector", lambda e: e.tensor_copy(rzB[:, :], pX.t[0:64, :]),
                            waits=[btok] + ([vn_prev[0]] if vn_prev[0] else []), inc=V)
                pX.rd = vcp
                w = [vcp] + (sct if sct and tw == 0 else [])
                vn = kb.op("vector", lambda e, tw=tw, pcx=pcx: e.tensor_tensor(
                    ctxp[0:64, tw * 512:(tw + 1) * 512], pcx.t[0:64, :], rzB[:, :], ALU.mult),
                    waits=w, inc=V)
                vn_prev[0] = vn
                pcx.rd = vn
                if tw == 0 and next_prep_gw is not None:
                    nprep = emit_prep(next_prep_gw)
            kb.drain("vector")
            sct_v = kb.op("vector", lambda e, s0=s0: e.tensor_copy(
                ctxT[0:64, :, s0:s0 + 128],
                ctxp.rearrange("p (sx c2 two) -> p two c2 sx", two=2, c2=8)[0:64, 0, :, :]),
                inc=V)
            sct_g = kb.op("gpsimd", lambda e, s0=s0: e.tensor_copy(
                ctxT[64:128, :, s0:s0 + 128],
                ctxp.rearrange("p (sx c2 two) -> p two c2 sx", two=2, c2=8)[0:64, 1, :, :]),
                waits=[vn_prev[0]], inc=kb.sem["G"])
            kb.drain("vector")
            kb.drain("gpsimd")
            sct = [sct_v, sct_g]
            pp_read[par] = sc_last
            attn_done = sct
            return nprep

        pending = emit_prep(qd * 4)
        for wi in range(4):
            gw = qd * 4 + wi
            nxt = gw + 1 if wi < 3 else None
            nprep = emit_attn(gw, pending, nxt)
            pending = nprep
        qkv_read = attn_done

    wo_tok = kb.op("sync", lambda e: e.dma_start(
        out=wo_sb[:], in_=self.wbf["wo"].rearrange("(ic p) d -> p ic d", p=128)),
        waits=attn_done + [(T, T.n), (Sc, Sc.n)] + self.prep["wo"], inc=dbig, inc_by=16)
    bcast_rows(kb, pA, pB,
               [(bB[:, 0, :], self.sb["bo_row"]),
                (bB[:, 1, :], self.sb["gamma_row"]),
                (bB[:, 2, :], self.sb["beta_row"])],
               waits=[self.persist_tok, (T, T.n), (Sc, Sc.n)] + attn_done)
    osb_use = None
    for tt in range(TT):
        sl = tt % 2
        w = [x0_use[sl]] if x0_use[sl] else []
        x0_tok[sl] = kb.op("sync", lambda e, b=b, tt=tt, sl=sl: e.dma_start(
            out=x0t[:, sl, :], in_=x_ap[b, tt * 128:(tt + 1) * 128, :]),
            waits=w, inc=dx[sl], inc_by=16)
        ev = {}
        for js in range(2):
            ps = pA if js == 0 else pB
            tok = self.mm_ff(
                ps,
                [ctxT[:, dc, tt * 128:(tt + 1) * 128] for dc in range(DC)],
                [wo_sb[:, dc, js * 512:(js + 1) * 512] for dc in range(DC)],
                waits=attn_done + [wo_tok] + ([osb_use] if osb_use else []))
            ev[js] = kb.op("vector", lambda e, ps=ps, js=js: e.tensor_tensor(
                osb[:, js * 512:(js + 1) * 512], ps.t[:, :], bB[:, 0, js * 512:(js + 1) * 512],
                ALU.add), waits=[tok], inc=V)
            ps.rd = ev[js]
        vlnf = kb.op("vector", lambda e, tt=tt, sl=sl: e.tensor_scalar(
            lnf[:], x0t[:, sl, :], negm[:, tt:tt + 1], rstd[:, tt:tt + 1], ALU.add, ALU.mult),
            waits=[x0_tok[sl], ev[1]], inc=V)
        kb.drain("vector")
        vsum = kb.op("vector", lambda e: e.tensor_tensor(osb[:], osb[:], lnf[:], ALU.add), inc=V)
        kb.drain("vector")
        emit_ln_stats(kb, osb[:], stA[:, 2 * tt:2 * tt + 1], stA[:, 2 * tt + 1:2 * tt + 2],
                      a2[:, 0:1], a2[:, 1:2], sqscr[:], self.eps, xwaits=[vsum])
        kb.op("vector", lambda e, tt=tt: e.tensor_scalar(
            a2[:], osb[:], stA[:, 2 * tt:2 * tt + 1], stA[:, 2 * tt + 1:2 * tt + 2],
            ALU.add, ALU.mult), waits=[(Sc, Sc.n)])
        kb.drain("vector")
        kb.op("vector", lambda e: e.tensor_tensor(lnf[:], a2[:], bB[:, 1, :], ALU.mult))
        kb.drain("vector")
        kb.op("vector", lambda e: e.tensor_tensor(a2[:], lnf[:], bB[:, 2, :], ALU.add))
        kb.drain("vector")
        vout = kb.op("vector", lambda e, sl=sl: e.tensor_tensor(
            osb[:], a2[:], x0t[:, sl, :], ALU.add), inc=V)
        x0_use[sl] = vout
        kb.drain("vector")
        otok = kb.op("sync", lambda e, b=b, tt=tt: e.dma_start(
            out=r1_ap[b, tt * 128:(tt + 1) * 128, :], in_=osb[:]),
            waits=[vout], inc=dO, inc_by=16)
        osb_use = otok


Builder.stage_s1 = _stage_s1_v2


# version 13 (final)
# speedup vs baseline: 1.0424x; 1.0424x over previous
import numpy as np
import ml_dtypes
import concourse.bass as bass
import concourse.mybir as mybir
from concourse.bass_utils import run_bass_kernel_spmd

F32 = mybir.dt.float32
BF16 = mybir.dt.bfloat16
AF = mybir.ActivationFunctionType
ALU = mybir.AluOpType
AX = mybir.AxisListType

NCORES = 8
BPC = 2
S = 2048
D = 1024
DC = D // 128
TT = S // 128
TW = S // 512
LN_EPS = 1e-5
RMS_EPS = 1e-5
FFN_F = 2048
FC = FFN_F // 128

bf16 = ml_dtypes.bfloat16


def host_constants():
    c = {}
    c["ident_bf"] = np.eye(128, dtype=np.float32).astype(bf16)
    c["ident_f"] = np.eye(128, dtype=np.float32)
    P = np.zeros((128, 128), np.float32)
    for r in range(128):
        P[r, r ^ 32] = 1.0
    c["perm32"] = P.astype(bf16)
    hd = 64
    inv_freq = 1.0 / (10000.0 ** (np.arange(0, hd, 2, dtype=np.float64) / hd))
    freqs = np.arange(S, dtype=np.float64)[:, None] * inv_freq[None, :]
    ridx = np.arange(128) % 32
    sign = np.where((np.arange(128) % 64) < 32, -1.0, 1.0)
    c["rope_c"] = np.ascontiguousarray(np.cos(freqs)[:, ridx].T).astype(np.float32)
    c["rope_s"] = np.ascontiguousarray((sign[:, None] * np.sin(freqs)[:, ridx].T)).astype(np.float32)
    r = np.arange(128)[:, None]
    cc = np.arange(512)[None, :]
    c["cmask"] = np.stack([(r + 128 * jj <= cc) for jj in range(4)]).astype(bf16)
    N, F = 1024, 513
    d = np.arange(N)[:, None].astype(np.float64)
    f = np.arange(F)[None, :].astype(np.float64)
    ang = 2.0 * np.pi * d * f / N
    c["dft_c"] = np.cos(ang).astype(bf16)
    c["dft_s"] = (-np.sin(ang)).astype(bf16)
    w = 2.0 - (np.arange(F) == 0) - (np.arange(F) == 512)
    irfa = np.zeros((640, 1024), np.float64)
    irfb = np.zeros((640, 1024), np.float64)
    irfa[:513] = w[:, None] * np.cos(ang.T) / N
    irfb[:513] = -w[:, None] * np.sin(ang.T) / N
    c["irf_a"] = irfa.astype(bf16)
    c["irf_b"] = irfb.astype(bf16)
    c["ones64"] = np.ones((64, 1), np.float32).astype(bf16)
    c["ones_f"] = np.ones((1, 128), np.float32)
    return c


class Sem:
    def __init__(self, nc, name):
        self.h = nc.alloc_semaphore(name)
        self.n = 0


class KB:
    ENGINES = ("sync", "scalar", "vector", "tensor", "gpsimd")

    def __init__(self, nc):
        self.nc = nc
        self.q = {e: [] for e in self.ENGINES}
        self.sem = {}
        for name in ("dW", "dA", "dO", "T", "V", "Sc", "G", "Bs"):
            self.sem[name] = Sem(nc, "sem_" + name)
        self.dsems = {}

    def op(self, eng, fn, waits=(), inc=None, inc_by=1):
        if inc is not None:
            inc.n += inc_by
        ws = tuple((s.h, v) for (s, v) in waits if s is not None)

        def th(e, fn=fn, ws=ws, inc=inc, inc_by=inc_by):
            for h, v in ws:
                e.wait_ge(h, v)
            ins = fn(e)
            if inc is not None:
                ins.then_inc(inc.h, inc_by)
        self.q[eng].append(th)
        return (inc, inc.n) if inc is not None else None

    def wait(self, eng, waits):
        ws = tuple((s.h, v) for (s, v) in waits if s is not None)

        def th(e, ws=ws):
            for h, v in ws:
                e.wait_ge(h, v)
        self.q[eng].append(th)

    def drain(self, eng):
        self.q[eng].append(lambda e: e.drain())

    def dstream(self, name):
        if name not in self.dsems:
            self.dsems[name] = Sem(self.nc, "d_" + name)
        return self.dsems[name]

    def barrier(self):
        toks = []
        pairs = (("sync", "Bs"), ("scalar", "Sc"), ("vector", "V"),
                 ("tensor", "T"), ("gpsimd", "G"))
        for eng, sname in pairs:
            if eng == "sync":
                toks.append(self.op(eng, lambda e: e.nop(), inc=self.sem[sname]))
            else:
                toks.append(self.op(eng, lambda e: e.drain(), inc=self.sem[sname]))
        toks.append((self.sem["dO"], self.sem["dO"].n))
        toks.append((self.sem["dW"], self.sem["dW"].n))
        for nm, ds in self.dsems.items():
            if not nm.startswith("prep"):
                toks.append((ds, ds.n))
        for eng, _ in pairs:
            self.wait(eng, toks)

    def finish(self, final_waits=()):
        self.wait("sync", list(final_waits))
        with self.nc.Block() as blk:
            for eng in self.ENGINES:
                thunks = self.q[eng]

                def body(e, thunks=thunks):
                    for th in thunks:
                        th(e)
                getattr(blk, eng)(body)


class Alloc:
    def __init__(self, nc):
        self.nc = nc
        self.base = ((nc.sbuf_base + 31) // 32) * 32
        self.PART_BYTES = nc.sbuf_top - 64
        self.per_off = self.base
        self.stage_off = None
        self.uid = 0

    def _alloc(self, name, shape, dtype, off):
        self.uid += 1
        nbytes = int(np.prod(shape[1:])) * (2 if dtype == BF16 else 4)
        t = self.nc.alloc_sbuf_tensor_at(f"{name}_{self.uid}", list(shape), dtype, offset=off)
        return t, off + ((nbytes + 31) // 32) * 32

    def perm(self, name, shape, dtype):
        t, self.per_off = self._alloc(name, shape, dtype, self.per_off)
        assert self.per_off <= self.PART_BYTES, f"persistent SBUF overflow {self.per_off}"
        return t

    def begin_stage(self):
        self.stage_off = self.per_off

    def stage(self, name, shape, dtype):
        t, self.stage_off = self._alloc(name, shape, dtype, self.stage_off)
        assert self.stage_off <= self.PART_BYTES, \
            f"stage SBUF overflow {self.stage_off/1024:.1f}KB at {name}"
        return t

    def stage_at(self, name, shape, dtype, offset):
        t, end = self._alloc(name, shape, dtype, offset)
        assert end <= self.PART_BYTES
        return t


class Psum:
    def __init__(self, nc, name, shape=(128, 512), dtype=F32):
        self.t = nc.alloc_psum_tensor(name, list(shape), dtype)
        self.rd = None
        self.wr = None


def emit_ln_stats(kb, x_ap, negmean_ap, rstd_ap, tmp_ap, tmp2_ap, sq_ap, eps, xwaits=(), width=D):
    V, Sc = kb.sem["V"], kb.sem["Sc"]
    sc1 = kb.op("scalar", lambda e: e.activation(sq_ap, x_ap, AF.Square, accum_out=tmp_ap),
                waits=xwaits, inc=Sc)
    kb.op("vector", lambda e: e.reduce_sum(negmean_ap, x_ap, axis=AX.X), waits=xwaits)
    kb.drain("vector")
    kb.op("vector", lambda e: e.tensor_single_scalar(negmean_ap, negmean_ap, -1.0 / width, ALU.mult))
    kb.drain("vector")
    kb.op("vector", lambda e: e.tensor_single_scalar(tmp2_ap, tmp_ap, 1.0 / width, ALU.mult),
          waits=[sc1])
    kb.drain("vector")
    v4 = kb.op("vector", lambda e: e.scalar_tensor_tensor(
        tmp_ap, negmean_ap, negmean_ap, tmp2_ap, ALU.mult, ALU.subtract), inc=V)
    kb.op("scalar", lambda e: e.activation(tmp2_ap, tmp_ap, AF.Ln, bias=eps[0], scale=-1.0),
          waits=[v4, eps[1]])
    kb.drain("scalar")
    sc2 = kb.op("scalar", lambda e: e.activation(rstd_ap, tmp2_ap, AF.Exp, scale=-0.5), inc=Sc)
    return sc2


def emit_ln_stats2(kb, tiles, sq_ap, eps, width=D):
    V, Sc = kb.sem["V"], kb.sem["Sc"]
    sc1 = []
    for x_ap, negm, rstd, tmp, tmp2, xw in tiles:
        w = list(xw) + ([sc1[-1]] if sc1 else [])
        sc1.append(kb.op("scalar", lambda e, x_ap=x_ap, sq_ap=sq_ap, tmp=tmp: e.activation(
            sq_ap, x_ap, AF.Square, accum_out=tmp), waits=w, inc=Sc))
    for x_ap, negm, rstd, tmp, tmp2, xw in tiles:
        kb.op("vector", lambda e, x_ap=x_ap, negm=negm: e.reduce_sum(negm, x_ap, axis=AX.X),
              waits=xw)
    kb.drain("vector")
    for x_ap, negm, rstd, tmp, tmp2, xw in tiles:
        kb.op("vector", lambda e, negm=negm: e.tensor_single_scalar(
            negm, negm, -1.0 / width, ALU.mult))
    kb.drain("vector")
    for i, (x_ap, negm, rstd, tmp, tmp2, xw) in enumerate(tiles):
        kb.op("vector", lambda e, tmp=tmp, tmp2=tmp2: e.tensor_single_scalar(
            tmp2, tmp, 1.0 / width, ALU.mult), waits=[sc1[i]])
    kb.drain("vector")
    v4 = None
    for x_ap, negm, rstd, tmp, tmp2, xw in tiles:
        v4 = kb.op("vector", lambda e, negm=negm, tmp=tmp, tmp2=tmp2: e.scalar_tensor_tensor(
            tmp, negm, negm, tmp2, ALU.mult, ALU.subtract), inc=V)
    toks = []
    for x_ap, negm, rstd, tmp, tmp2, xw in tiles:
        kb.op("scalar", lambda e, tmp=tmp, tmp2=tmp2: e.activation(
            tmp2, tmp, AF.Ln, bias=eps[0], scale=-1.0), waits=[v4, eps[1]])
    kb.drain("scalar")
    for x_ap, negm, rstd, tmp, tmp2, xw in tiles:
        toks.append(kb.op("scalar", lambda e, rstd=rstd, tmp2=tmp2: e.activation(
            rstd, tmp2, AF.Exp, scale=-0.5), inc=Sc))
    return toks



def emit_rms_stats2(kb, tiles, sq_ap, eps, width=D):
    V, Sc = kb.sem["V"], kb.sem["Sc"]
    sc1 = []
    for x_ap, rstd, tmp, tmp2, xw in tiles:
        w = list(xw) + ([sc1[-1]] if sc1 else [])
        sc1.append(kb.op("scalar", lambda e, x_ap=x_ap, sq_ap=sq_ap, tmp=tmp: e.activation(
            sq_ap, x_ap, AF.Square, accum_out=tmp), waits=w, inc=Sc))
    v1 = None
    for i, (x_ap, rstd, tmp, tmp2, xw) in enumerate(tiles):
        v1 = kb.op("vector", lambda e, tmp=tmp, tmp2=tmp2: e.tensor_single_scalar(
            tmp2, tmp, 1.0 / width, ALU.mult), waits=[sc1[i]], inc=V)
    for x_ap, rstd, tmp, tmp2, xw in tiles:
        kb.op("scalar", lambda e, tmp=tmp, tmp2=tmp2: e.activation(
            tmp, tmp2, AF.Ln, bias=eps[0]), waits=[v1, eps[1]])
    kb.drain("scalar")
    toks = []
    for x_ap, rstd, tmp, tmp2, xw in tiles:
        toks.append(kb.op("scalar", lambda e, rstd=rstd, tmp=tmp: e.activation(
            rstd, tmp, AF.Exp, scale=-0.5), inc=Sc))
    return toks


def emit_rms_stats(kb, x_ap, rstd_ap, tmp_ap, tmp2_ap, sq_ap, eps, xwaits=(), width=D):
    V, Sc = kb.sem["V"], kb.sem["Sc"]
    sc1 = kb.op("scalar", lambda e: e.activation(sq_ap, x_ap, AF.Square, accum_out=tmp_ap),
                waits=xwaits, inc=Sc)
    v1 = kb.op("vector", lambda e: e.tensor_single_scalar(tmp2_ap, tmp_ap, 1.0 / width, ALU.mult),
               waits=[sc1], inc=V)
    kb.op("scalar", lambda e: e.activation(tmp_ap, tmp2_ap, AF.Ln, bias=eps[0]),
          waits=[v1, eps[1]])
    kb.drain("scalar")
    sc2 = kb.op("scalar", lambda e: e.activation(rstd_ap, tmp_ap, AF.Exp, scale=-0.5), inc=Sc)
    return sc2


def bcast_rows(kb, psA, psB, dst_list, waits=()):
    T, V = kb.sem["T"], kb.sem["V"]
    ones_f = kb.ones_f
    banks = [psA, psB]
    i = 0
    for dst_ap, row_ap in dst_list:
        for j in range(D // 512):
            ps = banks[i % 2]
            i += 1
            sl = slice(j * 512, (j + 1) * 512)
            w = list(waits) + ([ps.rd] if ps.rd else [])
            t = kb.op("tensor", lambda e, sl=sl, ps=ps, row_ap=row_ap: e.matmul(
                ps.t[:, :], ones_f[0:1, :], row_ap[0:1, sl], start=True, stop=True),
                waits=w, inc=T)
            ps.wr = t
            v = kb.op("vector", lambda e, sl=sl, ps=ps, dst_ap=dst_ap: e.tensor_copy(
                dst_ap[:, sl], ps.t[:, :]), waits=[t], inc=V)
            ps.rd = v


STAGE_IO = {
    "s1": (("x_in",), "r1"),
    "s2": (("r1",), "rh2"),
    "s3": (("rh2", "r1"), "r2"),
    "s4": (("r2",), "r3"),
    "s5": (("r3",), "out"),
}
ALL_STAGES = ("s1", "s2", "s3", "s4", "s5")

WSPEC = {
    "wqkv": (D, 3 * D), "wo": (D, D),
    "bq3": (128, DC), "bk3": (128, DC), "bv3": (128, DC),
    "bv_row": (1, D), "bo_row": (1, D), "gamma_row": (1, D), "beta_row": (1, D),
    "ln1_row": (1, D), "ln2_row": (1, D),
    "fwq": (D, D), "fwk": (D, 256), "fwv": (D, 256), "fwo": (D, D),
    "wr3": (128, DC, 8),
    "wg": (8, D, FFN_F), "wu": (8, D, FFN_F), "wd": (8, FFN_F, D),
    "awq": (D, D), "awk": (D, D), "awv": (D, D),
    "rw3": (128, DC, 8), "ebias_row": (1, 8),
    "w1t": (8, D, FFN_F), "w3t": (8, D, FFN_F), "w2t": (8, FFN_F, D),
    "sw1t": (2, D, FFN_F), "sw3t": (2, D, FFN_F), "sw2t": (2, FFN_F, D),
}
CSPEC = {
    "ident_bf": ((128, 128), BF16), "ident_f": ((128, 128), F32),
    "perm32": ((128, 128), BF16),
    "rope_c": ((128, S), F32), "rope_s": ((128, S), F32),
    "cmask": ((4, 128, 512), BF16),
    "dft_c": ((D, 513), BF16), "dft_s": ((D, 513), BF16),
    "irf_a": ((640, D), BF16), "irf_b": ((640, D), BF16),
    "ones64": ((64, 1), BF16), "ones_f": ((1, 128), F32),
}


class Builder:
    def __init__(self, stages=ALL_STAGES, expose=False):
        self.stages = list(stages)
        nc = bass.Bass("TRN2", target_bir_lowering=False)
        self.nc = nc
        self.kb = KB(nc)
        self.al = Alloc(nc)
        kb = self.kb

        reads = set()
        writes = set()
        for st in self.stages:
            rr, w = STAGE_IO[st]
            reads.update(rr)
            writes.add(w)
        self.dram = {}
        for name in ("x_in", "r1", "rh2", "r2", "r3", "out"):
            if name == "out" and "out" in writes:
                kind = "ExternalOutput"
            elif name in writes:
                kind = "ExternalOutput" if expose else "Internal"
            elif name in reads:
                kind = "ExternalInput"
            else:
                continue
            self.dram[name] = nc.dram_tensor(name, [BPC, S, D], F32, kind=kind).ap()
        for name, shp in WSPEC.items():
            self.dram[name] = nc.dram_tensor(name, list(shp), F32, kind="ExternalInput").ap()
        for name, (shp, dt) in CSPEC.items():
            self.dram[name] = nc.dram_tensor(name, list(shp), dt, kind="ExternalInput").ap()

        self.pA = Psum(nc, "pA")
        self.pB = Psum(nc, "pB")
        self.pS0 = Psum(nc, "pS0")
        self.pS1 = Psum(nc, "pS1")
        self.pC = Psum(nc, "pC")
        self.pX = Psum(nc, "pX", (64, 512))
        self.pT0 = Psum(nc, "pT0", (128, 512), BF16)
        self.pT1 = Psum(nc, "pT1", (128, 512), BF16)

        al = self.al
        self.sb = {}
        pl = []
        for name, (shp, dt) in CSPEC.items():
            if name in ("rope_c", "rope_s", "cmask", "dft_c", "dft_s", "irf_a", "irf_b"):
                continue
            t = al.perm(name, shp, dt)
            self.sb[name] = t
            pl.append((t, self.dram[name]))
        for name in ("bq3", "bk3", "bv3", "bv_row", "bo_row", "gamma_row", "beta_row",
                     "ln1_row", "ln2_row", "ebias_row", "wr3", "rw3"):
            shp = WSPEC[name]
            t = al.perm(name, shp, F32)
            self.sb[name] = t
            pl.append((t, self.dram[name]))
        dW = kb.sem["dW"]
        for t, ap in pl:
            kb.op("sync", lambda e, t=t, ap=ap: e.dma_start(out=t[:], in_=ap[:]),
                  inc=dW, inc_by=16)
        self.persist_tok = (dW, dW.n)
        kb.ones_f = self.sb["ones_f"]
        self.wbf = {}
        self.prep = {}
        self.psem = [kb.dstream(f"prep{i}") for i in range(4)]
        self._pi = 0
        for name in ("wqkv", "wo", "fwq", "fwk", "fwv", "fwo", "awk", "awv", "awq",
                     "wg", "wu", "wd", "sw1t", "sw3t", "sw2t", "w1t", "w3t", "w2t"):
            self.wbf[name] = nc.dram_tensor("bf_" + name, list(WSPEC[name]), BF16,
                                            kind="Internal").ap()
        epst = al.perm("epsv", (128, 1), F32)
        etok = kb.op("vector", lambda e: e.memset(epst[:], LN_EPS), inc=kb.sem["V"])
        self.eps = (epst[:, 0:1], etok)
        ep30 = al.perm("eps30", (128, 1), F32)
        etok30 = kb.op("vector", lambda e: e.memset(ep30[:], 1e-30), inc=kb.sem["V"])
        self.eps30 = (ep30, etok30)
        self.record_prep(("wqkv", "wo", "fwq", "fwk", "fwv", "fwo", "awk", "awv", "awq"))

    def record_prep(self, names, gate=None):
        kb = self.kb
        for name in names:
            shp = WSPEC[name]
            nsl = shp[0] if len(shp) == 3 else 1
            for j in range(nsl):
                sm = self.psem[self._pi % 4]
                self._pi += 1
                src = self.dram[name][j] if nsl > 1 else self.dram[name][:]
                dst = self.wbf[name][j] if nsl > 1 else self.wbf[name][:]
                w = [(sm, sm.n)] + ([gate] if gate else [])
                gate = None
                kb.op("gpsimd", lambda e, src=src, dst=dst: e.dma_start(out=dst, in_=src),
                      waits=w, inc=sm, inc_by=16)
            self.prep[name] = [(sm_, sm_.n) for sm_ in self.psem]

    def mm_ff(self, ps, lhsT_slices, rhs_slices, waits):
        kb = self.kb
        T = kb.sem["T"]
        n = len(lhsT_slices)
        tok = None
        for i, (lt, rh) in enumerate(zip(lhsT_slices, rhs_slices)):
            w = list(waits) + ([ps.rd] if ps.rd else []) if i == 0 else []
            nw = rh.shape[-1]
            mw = lt.shape[-1]
            tok = kb.op("tensor", lambda e, lt=lt, rh=rh, i=i, n=n, nw=nw, mw=mw: e.matmul(
                ps.t[0:mw, 0:nw], lt, rh, start=(i == 0), stop=(i == n - 1)),
                waits=w, inc=(T if i == n - 1 else None))
        ps.wr = tok
        return tok

    def build(self):
        kb = self.kb
        for eng in ("tensor", "scalar", "vector", "gpsimd"):
            kb.wait(eng, [self.persist_tok, self.eps[1]])
        big = ("wg", "wu", "wd", "sw1t", "sw3t", "sw2t", "w1t", "w3t", "w2t")
        if self.stages[0] == "s1":
            first = True
            for b in range(BPC):
                for st in self.stages:
                    getattr(self, "stage_" + st)(b)
                    if first:
                        gate = (kb.sem["T"], max(1, kb.sem["T"].n * 1 // 2))
                        self.prep_gate = gate
                        self.record_prep(big, gate=gate)
                        first = False
                    kb.barrier()
        else:
            self.record_prep(big)
            for b in range(BPC):
                for st in self.stages:
                    getattr(self, "stage_" + st)(b)
                    kb.barrier()
        kb.finish(final_waits=[(kb.sem["dO"], kb.sem["dO"].n)])
        return self.nc

    def stage_s1(self, b):
        kb, al, nc = self.kb, self.al, self.nc
        T, V, Sc, dW, dA, dO, G = (kb.sem[k] for k in ("T", "V", "Sc", "dW", "dA", "dO", "G"))
        dx = [kb.dstream("x0"), kb.dstream("x1")]
        dwc = [kb.dstream("wa"), kb.dstream("wb")]
        dbig = kb.dstream("wbig")
        al.begin_stage()
        x_ap = self.dram["x_in"]
        r1_ap = self.dram["r1"]

        lnT = al.stage("lnT", (128, DC, S), BF16)
        ctxT = al.stage("ctxT", (128, DC, S), BF16)
        esb = al.stage("esb", (128, TT, 512), BF16)
        qx = [al.stage(f"qx{i}", (65, S), BF16) for i in range(2)]
        kx = [al.stage(f"kx{i}", (65, S), BF16) for i in range(2)]
        vpair = al.stage("vpair", (128, TT, 2, 65), BF16)
        wcol = al.stage("wcol", (128, 2, 3, DC, 128), BF16)
        wo_sb = al.stage("wo_sb", (128, DC, D), BF16)
        x0t = al.stage("x0t", (128, 2, D), F32)
        lntile_off = al.stage_off
        lntile = al.stage("lntile", (128, 2, D), BF16)
        osb = al.stage("osb", (128, D), F32)
        lnf = al.stage("lnf", (128, D), F32)
        a2 = al.stage("a2", (128, D), F32)
        bB = al.stage("bB", (128, 4, D), F32)
        negm = al.stage("negm", (128, TT), F32)
        rstd = al.stage("rstd", (128, TT), F32)
        stA = al.stage("stA", (128, 2 * TT), F32)
        negm2 = al.stage("negm2", (128, TT), F32)
        rstd2 = al.stage("rstd2", (128, TT), F32)
        stC = al.stage("stC", (128, 2 * TT), F32)
        sqscr_off = al.stage_off
        sqscr = al.stage("sqscr", (128, D), F32)
        sqh = al.stage_at("sqh", (64, S), BF16, lntile_off)
        qn_sb = al.stage("qn_sb", (1, S), BF16)
        kn_sb = al.stage_at("kn_sb", (1, S), BF16, sqscr_off)
        kmax = al.stage("kmax", (1, 2), F32)
        rz = al.stage("rz", (1, 512), F32)
        rzB = al.stage("rzB", (64, 512), F32)

        ident_bf = self.sb["ident_bf"]
        ones64 = self.sb["ones64"]
        pA, pB, pS, pC, pX, pT = self.pA, self.pB, [self.pS0, self.pS1], self.pC, self.pX, [self.pT0, self.pT1]

        bcast_rows(kb, pA, pB,
                   [(bB[:, 0, :], self.sb["bv_row"]),
                    (bB[:, 1, :], self.sb["bo_row"]),
                    (bB[:, 2, :], self.sb["gamma_row"]),
                    (bB[:, 3, :], self.sb["beta_row"])],
                   waits=[self.persist_tok])
        wo_tok = kb.op("gpsimd", lambda e: e.dma_start(
            out=wo_sb[:], in_=self.dram["wo"].rearrange("(ic p) d -> p ic d", p=128)),
            waits=[], inc=dbig, inc_by=16)
        vset = kb.op("vector", lambda e: e.memset(vpair[:, :, :, 64:65], 1.0), inc=V)

        x0_tok = [None, None]
        x0_use = [None, None]
        ln_use = [None, None]
        ptA_free = [None, None]
        vA = None
        for tt in range(TT):
            sl = tt % 2
            w = [x0_use[sl]] if x0_use[sl] else []
            x0_tok[sl] = kb.op("sync", lambda e, b=b, tt=tt, sl=sl: e.dma_start(
                out=x0t[:, sl, :], in_=x_ap[b, tt * 128:(tt + 1) * 128, :]),
                waits=w, inc=dx[sl], inc_by=16)
            emit_ln_stats(kb, x0t[:, sl, :], negm[:, tt:tt + 1], rstd[:, tt:tt + 1],
                          stA[:, 2 * tt:2 * tt + 1], stA[:, 2 * tt + 1:2 * tt + 2],
                          sqscr[:], self.eps, xwaits=[x0_tok[sl]])
            w = [ln_use[sl]] if ln_use[sl] else []
            vln = kb.op("vector", lambda e, tt=tt, sl=sl: e.tensor_scalar(
                lntile[:, sl, :], x0t[:, sl, :], negm[:, tt:tt + 1], rstd[:, tt:tt + 1],
                ALU.add, ALU.mult), waits=w, inc=V)
            x0_use[sl] = vln
            for g in range(2):
                tw_ = [vln] + ([ptA_free[g]] if ptA_free[g] else [])
                ttok = None
                for j in range(4):
                    dc = g * 4 + j
                    ttok = kb.op("tensor", lambda e, g=g, j=j, dc=dc, sl=sl: e.transpose(
                        pT[g].t[:, j * 128:(j + 1) * 128], lntile[:, sl, dc * 128:(dc + 1) * 128],
                        ident_bf[:]), waits=(tw_ if j == 0 else []),
                        inc=(T if j == 3 else None))
                vev = None
                for j in range(4):
                    dc = g * 4 + j
                    vev = kb.op("vector", lambda e, g=g, j=j, dc=dc, tt=tt: e.tensor_copy(
                        lnT[:, dc, tt * 128:(tt + 1) * 128], pT[g].t[:, j * 128:(j + 1) * 128]),
                        waits=([ttok] if j == 0 else []), inc=(V if j == 3 else None))
                ptA_free[g] = vev
                ln_use[sl] = vev
            vA = ln_use[sl]

        wcol_tok = [None, None]
        wcol_use = [None, None]
        esb_free = None
        exp_tok = [None] * TT
        bank_exp = [None, None]
        attn_done = None
        sq_reader = None
        for oc in range(8):
            buf = oc % 2
            w = [wcol_use[buf]] if wcol_use[buf] else []
            for im, (wn, wd_) in enumerate((("wq", D), ("wk", D), ("wv", D))):
                wcol_tok[buf] = kb.op("gpsimd", lambda e, wn=wn, oc=oc, buf=buf, im=im: e.dma_start(
                    out=wcol[:, buf, im, :, :],
                    in_=self.dram[wn].rearrange("(ic p) d -> p ic d", p=128)[:, :, oc * 128:(oc + 1) * 128]),
                    waits=(w if im == 0 else []), inc=dwc[buf], inc_by=16)
            wready = wcol_tok[buf]

            evq = {}
            for im, dstpair, brow in ((0, qx, "bq3"), (1, kx, "bk3")):
                for tw in range(TW):
                    ps = pA if (tw % 2 == 0) else pB
                    tok = self.mm_ff(
                        ps,
                        [wcol[:, buf, im, ic, :] for ic in range(DC)],
                        [lnT[:, ic, tw * 512:(tw + 1) * 512] for ic in range(DC)],
                        waits=[wready, vA])
                    s1 = kb.op("scalar", lambda e, ps=ps, dstpair=dstpair, tw=tw, oc=oc, brow=brow: e.activation(
                        dstpair[0][0:64, tw * 512:(tw + 1) * 512], ps.t[0:64, :], AF.Identity,
                        bias=self.sb[brow][0:64, oc:oc + 1]),
                        waits=[tok] + ([attn_done] if attn_done else []), inc=Sc)
                    s2 = kb.op("scalar", lambda e, ps=ps, dstpair=dstpair, tw=tw, oc=oc, brow=brow: e.activation(
                        dstpair[1][0:64, tw * 512:(tw + 1) * 512], ps.t[64:128, :], AF.Identity,
                        bias=self.sb[brow][64:128, oc:oc + 1]), inc=Sc)
                    ps.rd = s2
                    evq[(im, tw)] = s2
            qk_done = evq[(1, TW - 1)]

            vv_done = None
            for tt in range(TT):
                ps = pA if (tt % 2 == 0) else pB
                tok = self.mm_ff(
                    ps,
                    [lnT[:, ic, tt * 128:(tt + 1) * 128] for ic in range(DC)],
                    [wcol[:, buf, 2, ic, :] for ic in range(DC)],
                    waits=[wready, vA, vset])
                vv = None
                for h in range(2):
                    vv = kb.op("vector", lambda e, ps=ps, tt=tt, h=h, oc=oc: e.tensor_tensor(
                        vpair[:, tt, h, 0:64], ps.t[:, h * 64:(h + 1) * 64],
                        bB[:, 0, oc * 128 + h * 64: oc * 128 + (h + 1) * 64], ALU.add),
                        waits=(([tok] + ([attn_done] if attn_done else [])) if h == 0 else []),
                        inc=(V if h == 1 else None))
                ps.rd = vv
                vv_done = vv
            wcol_use[buf] = vv_done

            for h in range(2):
                qh, kh = qx[h], kx[h]
                sq_tok = kb.op("scalar", lambda e, qh=qh: e.activation(
                    sqh[0:64, :], qh[0:64, :], AF.Square),
                    waits=[qk_done] + ([sq_reader] if sq_reader else []), inc=Sc)
                qn_done = None
                for tw in range(TW):
                    w = [sq_tok] + ([pX.rd] if pX.rd else [])
                    mtok = kb.op("tensor", lambda e, tw=tw: e.matmul(
                        pX.t[0:1, :], ones64[0:64, 0:1], sqh[0:64, tw * 512:(tw + 1) * 512],
                        start=True, stop=True), waits=w, inc=T)
                    qn_done = kb.op("scalar", lambda e, tw=tw: e.activation(
                        qn_sb[0:1, tw * 512:(tw + 1) * 512], pX.t[0:1, :], AF.Sqrt),
                        waits=[mtok], inc=Sc)
                    pX.rd = qn_done
                sqk_tok = kb.op("scalar", lambda e, kh=kh: e.activation(
                    sqh[0:64, :], kh[0:64, :], AF.Square), waits=[qn_done, mtok], inc=Sc)
                kn_done = None
                for tw in range(TW):
                    w = [sqk_tok] + ([pX.rd] if pX.rd else [])
                    mtok = kb.op("tensor", lambda e, tw=tw: e.matmul(
                        pX.t[0:1, :], ones64[0:64, 0:1], sqh[0:64, tw * 512:(tw + 1) * 512],
                        start=True, stop=True), waits=w, inc=T)
                    kn_done = kb.op("scalar", lambda e, tw=tw: e.copy(
                        kn_sb[0:1, tw * 512:(tw + 1) * 512], pX.t[0:1, :]),
                        waits=[mtok], inc=Sc)
                    pX.rd = kn_done
                sq_reader = mtok
                vkm = kb.op("vector", lambda e: e.reduce_max(kmax[0:1, 0:1], kn_sb[0:1, :], axis=AX.X),
                            waits=[kn_done], inc=V)
                skm = kb.op("scalar", lambda e: e.activation(kmax[0:1, 1:2], kmax[0:1, 0:1], AF.Sqrt),
                            waits=[vkm], inc=Sc)
                stab = kb.op("vector", lambda e, qh=qh: e.tensor_scalar(
                    qh[64:65, :], qn_sb[0:1, :], kmax[0:1, 1:2], -1.0, ALU.mult, ALU.mult),
                    waits=[skm], inc=V)
                kones = kb.op("vector", lambda e, kh=kh: e.memset(kh[64:65, :], 1.0),
                              waits=[qk_done], inc=V)

                for tw in range(TW):
                    for kc in range(TT):
                        pb_ = pS[kc % 2]
                        w = [stab, kones]
                        if bank_exp[kc % 2]:
                            w.append(bank_exp[kc % 2])
                        if esb_free:
                            w.append(esb_free)
                        mtok = kb.op("tensor", lambda e, pb_=pb_, kh=kh, qh=qh, kc=kc, tw=tw: e.matmul(
                            pb_.t[:, :], kh[:, kc * 128:(kc + 1) * 128], qh[:, tw * 512:(tw + 1) * 512],
                            start=True, stop=True), waits=w, inc=T)
                        etok = kb.op("scalar", lambda e, pb_=pb_, kc=kc: e.activation(
                            esb[:, kc, :], pb_.t[:, :], AF.Exp, scale=0.125),
                            waits=[mtok], inc=Sc)
                        bank_exp[kc % 2] = etok
                        exp_tok[kc] = etok
                    ctx_tok = None
                    for kc in range(TT):
                        w = [exp_tok[kc], vv_done]
                        if kc == 0 and pC.rd:
                            w.append(pC.rd)
                        ctx_tok = kb.op("tensor", lambda e, kc=kc, h=h: e.matmul(
                            pC.t[0:65, :], vpair[:, kc, h, :], esb[:, kc, :],
                            start=(kc == 0), stop=(kc == TT - 1)),
                            waits=w, inc=(T if kc == TT - 1 else None))
                    esb_free = ctx_tok
                    vrz = kb.op("vector", lambda e: e.reciprocal(rz[0:1, :], pC.t[64:65, :]),
                                waits=[ctx_tok], inc=V)
                    w = [vrz] + ([pX.rd] if pX.rd else [])
                    btok = kb.op("tensor", lambda e: e.matmul(
                        pX.t[0:64, :], self.sb["ones_f"][0:1, 0:64], rz[0:1, :],
                        start=True, stop=True), waits=w, inc=T)
                    pX.wr = btok
                    vcp = kb.op("vector", lambda e: e.tensor_copy(rzB[:, :], pX.t[0:64, :]),
                                waits=[btok], inc=V)
                    pX.rd = vcp
                    g_head = 2 * oc + h
                    vn = kb.op("vector", lambda e, g_head=g_head, tw=tw: e.tensor_tensor(
                        ctxT[64 * (g_head % 2):64 * (g_head % 2) + 64, g_head // 2, tw * 512:(tw + 1) * 512],
                        pC.t[0:64, :], rzB[:, :], ALU.mult), waits=[vcp], inc=V)
                    pC.rd = vn
                attn_done = pC.rd
        ctxT_done = pC.rd

        osb_use = None
        x0_use = [x0_use[0], x0_use[1]]
        for tt in range(TT):
            sl = tt % 2
            w = [x0_use[sl]] if x0_use[sl] else []
            x0_tok[sl] = kb.op("sync", lambda e, b=b, tt=tt, sl=sl: e.dma_start(
                out=x0t[:, sl, :], in_=x_ap[b, tt * 128:(tt + 1) * 128, :]),
                waits=w, inc=dx[sl], inc_by=16)
            ev = {}
            for js in range(2):
                ps = pA if js == 0 else pB
                tok = self.mm_ff(
                    ps,
                    [ctxT[:, dc, tt * 128:(tt + 1) * 128] for dc in range(DC)],
                    [wo_sb[:, dc, js * 512:(js + 1) * 512] for dc in range(DC)],
                    waits=[ctxT_done, wo_tok] + ([osb_use] if osb_use else []))
                ev[js] = kb.op("vector", lambda e, ps=ps, js=js: e.tensor_tensor(
                    osb[:, js * 512:(js + 1) * 512], ps.t[:, :], bB[:, 1, js * 512:(js + 1) * 512],
                    ALU.add), waits=[tok], inc=V)
                ps.rd = ev[js]
            vlnf = kb.op("vector", lambda e, tt=tt, sl=sl: e.tensor_scalar(
                lnf[:], x0t[:, sl, :], negm[:, tt:tt + 1], rstd[:, tt:tt + 1], ALU.add, ALU.mult),
                waits=[x0_tok[sl], ev[1]], inc=V)
            kb.drain("vector")
            vsum = kb.op("vector", lambda e: e.tensor_tensor(osb[:], osb[:], lnf[:], ALU.add), inc=V)
            kb.drain("vector")
            emit_ln_stats(kb, osb[:], negm2[:, tt:tt + 1], rstd2[:, tt:tt + 1],
                          stC[:, 2 * tt:2 * tt + 1], stC[:, 2 * tt + 1:2 * tt + 2],
                          sqscr[:], self.eps, xwaits=[vsum])
            kb.op("vector", lambda e, tt=tt: e.tensor_scalar(
                a2[:], osb[:], negm2[:, tt:tt + 1], rstd2[:, tt:tt + 1], ALU.add, ALU.mult))
            kb.drain("vector")
            kb.op("vector", lambda e: e.tensor_tensor(lnf[:], a2[:], bB[:, 2, :], ALU.mult))
            kb.drain("vector")
            kb.op("vector", lambda e: e.tensor_tensor(a2[:], lnf[:], bB[:, 3, :], ALU.add))
            kb.drain("vector")
            vout = kb.op("vector", lambda e, sl=sl: e.tensor_tensor(
                osb[:], a2[:], x0t[:, sl, :], ALU.add), inc=V)
            x0_use[sl] = vout
            kb.drain("vector")
            otok = kb.op("sync", lambda e, b=b, tt=tt: e.dma_start(
                out=r1_ap[b, tt * 128:(tt + 1) * 128, :], in_=osb[:]),
                waits=[vout], inc=dO, inc_by=16)
            osb_use = otok


def prep_weights(i):
    f32 = np.float32
    ca = np.ascontiguousarray
    w = {}
    w["wqkv"] = ca(np.concatenate([np.asarray(i["L0_attn_wq"], f32),
                                   np.asarray(i["L0_attn_wk"], f32),
                                   np.asarray(i["L0_attn_wv"], f32)], axis=1))
    w["wo"] = ca(i["L0_attn_wo"], dtype=f32)
    w["bq3"] = ca(np.asarray(i["L0_attn_bq"], f32).reshape(DC, 128).T)
    w["bk3"] = ca(np.asarray(i["L0_attn_bk"], f32).reshape(DC, 128).T)
    w["bv3"] = ca(np.asarray(i["L0_attn_bv"], f32).reshape(DC, 128).T)
    w["bv_row"] = ca(np.asarray(i["L0_attn_bv"], f32).reshape(1, D))
    w["bo_row"] = ca(np.asarray(i["L0_attn_bo"], f32).reshape(1, D))
    w["gamma_row"] = ca(np.asarray(i["L0_attn_gamma"], f32).reshape(1, D))
    w["beta_row"] = ca(np.asarray(i["L0_attn_beta"], f32).reshape(1, D))
    w["ln1_row"] = ca(np.asarray(i["L0_ffn_ln1_w"], f32).reshape(1, D))
    w["ln2_row"] = ca(np.asarray(i["L0_ffn_ln2_w"], f32).reshape(1, D))
    w["fwq"] = ca(i["L0_ffn_Wq"], dtype=f32)
    w["fwk"] = ca(i["L0_ffn_Wk"], dtype=f32)
    w["fwv"] = ca(i["L0_ffn_Wv"], dtype=f32)
    w["fwo"] = ca(i["L0_ffn_Wo"], dtype=f32)
    w["wr3"] = ca(np.asarray(i["L0_ffn_Wr"], f32).reshape(DC, 128, 8).transpose(1, 0, 2))
    w["wg"] = ca(i["L0_ffn_Wg"], dtype=f32)
    w["wu"] = ca(i["L0_ffn_Wu"], dtype=f32)
    w["wd"] = ca(i["L0_ffn_Wd"], dtype=f32)
    w["awq"] = ca(np.asarray(i["L1_attn_Wq"], f32).T)
    w["awk"] = ca(np.asarray(i["L1_attn_Wk"], f32).T)
    w["awv"] = ca(np.asarray(i["L1_attn_Wv"], f32).T)
    rw = np.asarray(i["L1_ffn_router_w"], f32).T
    w["rw3"] = ca(rw.reshape(DC, 128, 8).transpose(1, 0, 2))
    w["ebias_row"] = ca(np.asarray(i["L1_ffn_expert_bias"], f32).reshape(1, 8))
    w["w1t"] = ca(np.asarray(i["L1_ffn_w1"], f32).transpose(0, 2, 1))
    w["w3t"] = ca(np.asarray(i["L1_ffn_w3"], f32).transpose(0, 2, 1))
    w["w2t"] = ca(np.asarray(i["L1_ffn_w2"], f32).transpose(0, 2, 1))
    w["sw1t"] = ca(np.asarray(i["L1_ffn_sw1"], f32).transpose(0, 2, 1))
    w["sw3t"] = ca(np.asarray(i["L1_ffn_sw3"], f32).transpose(0, 2, 1))
    w["sw2t"] = ca(np.asarray(i["L1_ffn_sw2"], f32).transpose(0, 2, 1))
    w.update(host_constants())
    return w


_PROG_CACHE = {}


def _get_prog(stages, expose):
    key = (tuple(stages), expose)
    if key not in _PROG_CACHE:
        b = Builder(stages, expose=expose)
        b.build()
        _PROG_CACHE[key] = b
    return _PROG_CACHE[key]


def kernel(**inputs):
    w = prep_weights(inputs)
    x = np.ascontiguousarray(np.asarray(inputs["L0_attn_x"], np.float32))
    b = _get_prog(ALL_STAGES, False)
    in_maps = []
    for c in range(NCORES):
        m = dict(w)
        m["x_in"] = x[c * BPC:(c + 1) * BPC]
        in_maps.append(m)
    res = run_bass_kernel_spmd(b.nc, in_maps, core_ids=list(range(NCORES)))
    return np.concatenate([r["out"] for r in res.results], axis=0)


def _stage_s2(self, b):
    kb, al, nc = self.kb, self.al, self.nc
    T, V, Sc, dO = (kb.sem[k] for k in ("T", "V", "Sc", "dO"))
    al.begin_stage()
    x1_ap = self.dram["r1"]
    rh2_ap = self.dram["rh2"]
    dx = [kb.dstream("x0"), kb.dstream("x1")]
    dwc = [kb.dstream("wa"), kb.dstream("wb")]
    dbig = kb.dstream("wbig")
    dwo = kb.dstream("wbig2")

    hT = al.stage("hT", (128, DC, S), BF16)
    ctxT = al.stage("ctxT2", (128, DC, S), BF16)
    esb = al.stage("esb2", (128, TT, 512), BF16)
    qx_off = al.stage_off
    qx = [al.stage(f"q2x{i}", (65, S), BF16) for i in range(2)]
    kx = [al.stage(f"k2x{i}", (65, S), BF16) for i in range(4)]
    fwo_sb = al.stage_at("fwo_sb", (128, DC, D), BF16, qx_off)
    vkv = al.stage("vkv", (128, TT, 4, 65), BF16)
    wqcol = al.stage("wqcol", (128, 2, DC, 128), BF16)
    fwk_sb = al.stage("fwk_sb", (128, DC, 256), BF16)
    fwv_sb = al.stage("fwv_sb", (128, DC, 256), BF16)
    x1t = al.stage("x1t", (128, 2, D), F32)
    htile_off = al.stage_off
    htile = al.stage("htile", (128, 2, D), BF16)
    hsbuf_off = al.stage_off
    hsbuf = al.stage("hsbuf", (128, D), F32)
    t1 = al.stage("t1", (128, 512), F32)
    t2 = al.stage("t2", (128, 512), F32)
    qraw = al.stage("qraw", (128, 512), BF16)
    osb2_off = al.stage_off
    osb = al.stage("osb2", (128, D), F32)
    lnf = al.stage("lnf2", (128, D), F32)
    a2 = al.stage("a2_2", (128, D), F32)
    ln1wB = al.stage("ln1wB", (128, D), F32)
    ropeC = al.stage("ropeC", (128, S), BF16)
    ropeS = al.stage("ropeS", (128, S), BF16)
    cmask_sb = al.stage("cmask_sb", (128, 4, 512), BF16)
    negm = al.stage("negmS2", (128, TT), F32)
    rstd = al.stage("rstdS2", (128, TT), F32)
    rs1 = al.stage("rs1", (128, TT), F32)
    stA = al.stage("stA2", (128, 2 * TT), F32)
    sqscr_off = al.stage_off
    sqscr = al.stage("sqscr2", (128, D), F32)
    sqh = al.stage_at("sqh2", (64, S), BF16, htile_off)
    kn_sb = al.stage_at("kn_sb2", (1, S), BF16, sqscr_off)
    qn_sb = al.stage("qn_sb2", (1, S), BF16)
    kmaxs = al.stage("kmaxs", (1, 8), F32)
    rz = al.stage_at("rz2", (1, 512), F32, hsbuf_off)
    rzB = al.stage_at("rzB2", (64, 512), F32, hsbuf_off + 2048)
    rzl = al.stage_at("rzl2", (1, 512), F32, osb2_off)

    ident_bf = self.sb["ident_bf"]
    perm32 = self.sb["perm32"]
    ones64 = self.sb["ones64"]
    pA, pB, pS, pC, pX, pT = self.pA, self.pB, [self.pS0, self.pS1], self.pC, self.pX, [self.pT0, self.pT1]

    bcast_rows(kb, pA, pB, [(ln1wB[:, :], self.sb["ln1_row"])], waits=[self.persist_tok])
    drp = kb.dstream("ropes")
    kb.op("gpsimd", lambda e: e.dma_start(out=ropeC[:], in_=self.dram["rope_c"][:]),
          waits=[(drp, drp.n)], inc=drp, inc_by=16)
    kb.op("gpsimd", lambda e: e.dma_start(out=ropeS[:], in_=self.dram["rope_s"][:]),
          waits=[(drp, drp.n)], inc=drp, inc_by=16)
    rc_tok = kb.op("gpsimd", lambda e: e.dma_start(
        out=cmask_sb[:], in_=self.dram["cmask"].rearrange("j p c -> p j c")),
        waits=[(drp, drp.n)], inc=drp, inc_by=16)
    dkv = kb.dstream("wkv")
    ck_tok = kb.op("sync", lambda e: e.dma_start(
        out=fwk_sb[:], in_=self.wbf["fwk"].rearrange("(ic p) d -> p ic d", p=128)),
        waits=self.prep["fwk"] + [(dkv, dkv.n)], inc=dkv, inc_by=16)
    ck_tok2 = kb.op("sync", lambda e: e.dma_start(
        out=fwv_sb[:], in_=self.wbf["fwv"].rearrange("(ic p) d -> p ic d", p=128)),
        waits=self.prep["fwv"] + [(dkv, dkv.n)], inc=dkv, inc_by=16)
    consts_tok = [rc_tok, ck_tok, ck_tok2]
    vset = kb.op("vector", lambda e: e.memset(vkv[:, :, :, 64:65], 1.0), inc=V)

    x_tok = [None, None]
    x_use = [None, None]
    h_use = [None, None]
    ptA_free = [None, None]
    vA = None
    for tt in range(TT):
        sl = tt % 2
        w = [x_use[sl]] if x_use[sl] else []
        x_tok[sl] = kb.op("sync", lambda e, b=b, tt=tt, sl=sl: e.dma_start(
            out=x1t[:, sl, :], in_=x1_ap[b, tt * 128:(tt + 1) * 128, :]),
            waits=w, inc=dx[sl], inc_by=16)
        emit_ln_stats(kb, x1t[:, sl, :], negm[:, tt:tt + 1], rstd[:, tt:tt + 1],
                      stA[:, 2 * tt:2 * tt + 1], stA[:, 2 * tt + 1:2 * tt + 2],
                      sqscr[:], self.eps, xwaits=[x_tok[sl]])
        vhs = kb.op("vector", lambda e, tt=tt, sl=sl: e.tensor_scalar(
            hsbuf[:], x1t[:, sl, :], negm[:, tt:tt + 1], rstd[:, tt:tt + 1],
            ALU.add, ALU.mult), waits=[(Sc, Sc.n)], inc=V)
        x_use[sl] = vhs
        kb.drain("vector")
        emit_rms_stats(kb, hsbuf[:], rs1[:, tt:tt + 1], stA[:, 2 * tt:2 * tt + 1],
                       stA[:, 2 * tt + 1:2 * tt + 2], sqscr[:], self.eps, xwaits=[vhs])
        kb.op("vector", lambda e, tt=tt: e.tensor_single_scalar(
            lnf[:], hsbuf[:], rs1[:, tt:tt + 1], ALU.mult), waits=[(Sc, Sc.n)])
        kb.drain("vector")
        w = [h_use[sl]] if h_use[sl] else []
        vln = kb.op("vector", lambda e, sl=sl: e.tensor_tensor(
            htile[:, sl, :], lnf[:], ln1wB[:, :], ALU.mult), waits=w, inc=V)
        kb.drain("vector")
        for g in range(2):
            tw_ = [vln] + ([ptA_free[g]] if ptA_free[g] else [])
            ttok = None
            for j in range(4):
                dc = g * 4 + j
                ttok = kb.op("tensor", lambda e, g=g, j=j, dc=dc, sl=sl: e.transpose(
                    pT[g].t[:, j * 128:(j + 1) * 128], htile[:, sl, dc * 128:(dc + 1) * 128],
                    ident_bf[:]), waits=(tw_ if j == 0 else []),
                    inc=(T if j == 3 else None))
            vev = None
            for j in range(4):
                dc = g * 4 + j
                vev = kb.op("vector", lambda e, g=g, j=j, dc=dc, tt=tt: e.tensor_copy(
                    hT[:, dc, tt * 128:(tt + 1) * 128], pT[g].t[:, j * 128:(j + 1) * 128]),
                    waits=([ttok] if j == 0 else []), inc=(V if j == 3 else None))
            ptA_free[g] = vev
            h_use[sl] = vev
        vA = h_use[sl]

    pP = [pA, pC]
    pW = [pB, pB]
    qraw_free = [None]

    def rope_proj(lhsT_slices_fn, dstA, dstB, wready):
        wready_l = wready if isinstance(wready, list) else [wready]
        for tw in range(TW):
            ps = pP[tw % 2]
            pw = pW[tw % 2]
            tok = self.mm_ff(ps, lhsT_slices_fn(),
                             [hT[:, ic, tw * 512:(tw + 1) * 512] for ic in range(DC)],
                             waits=wready_l + [vA])
            wts = [tok] + ([qraw_free[0]] if qraw_free[0] else [])
            ev = kb.op("scalar", lambda e, ps=ps: e.copy(qraw[:], ps.t[:, :]),
                       waits=wts, inc=Sc)
            ps.rd = ev
            w2 = [ev] + ([pw.rd] if pw.rd else [])
            sw = kb.op("tensor", lambda e, pw=pw: e.matmul(
                pw.t[:, :], perm32[:], qraw[:], start=True, stop=True), waits=w2, inc=T)
            pw.wr = sw
            m1 = kb.op("vector", lambda e, tw=tw: e.tensor_tensor(
                t1[:], qraw[:], ropeC[:, tw * 512:(tw + 1) * 512], ALU.mult),
                waits=[ev] + consts_tok, inc=V)
            m2 = kb.op("vector", lambda e, pw=pw, tw=tw: e.tensor_tensor(
                t2[:], pw.t[:, :], ropeS[:, tw * 512:(tw + 1) * 512], ALU.mult),
                waits=[sw], inc=V)
            pw.rd = m2
            qraw_free[0] = m2
            kb.drain("vector")
            va = kb.op("vector", lambda e, dstA=dstA, tw=tw: e.tensor_tensor(
                dstA[0:64, tw * 512:(tw + 1) * 512], t1[0:64, :], t2[0:64, :], ALU.add))
            vb = kb.op("vector", lambda e, dstB=dstB, tw=tw: e.tensor_tensor(
                dstB[0:64, tw * 512:(tw + 1) * 512], t1[64:128, :], t2[64:128, :], ALU.add),
                inc=V)
            kb.drain("vector")
        return (V, kb.sem["V"].n)

    kv_done = None
    for oc2 in range(2):
        kv_done = rope_proj(
            lambda oc2=oc2: [fwk_sb[:, ic, oc2 * 128:(oc2 + 1) * 128] for ic in range(DC)],
            kx[2 * oc2], kx[2 * oc2 + 1], ck_tok)

    vv_done = None
    for tt in range(TT):
        ps = pP[tt % 2]
        tok = self.mm_ff(ps,
                         [hT[:, ic, tt * 128:(tt + 1) * 128] for ic in range(DC)],
                         [fwv_sb[:, ic, :] for ic in range(DC)],
                         waits=consts_tok + [vA, vset])
        sv = None
        for kv in range(4):
            sv = kb.op("scalar", lambda e, ps=ps, tt=tt, kv=kv: e.copy(
                vkv[:, tt, kv, 0:64], ps.t[:, kv * 64:(kv + 1) * 64]),
                waits=([tok] if kv == 0 else []), inc=(Sc if kv == 3 else None))
        ps.rd = sv
        vv_done = sv

    sq_reader = None
    kn_done = None
    vkm_prev = None
    for kv in range(4):
        w = [kv_done] + ([sq_reader] if sq_reader else []) + ([vkm_prev] if vkm_prev else [])
        sqk = kb.op("scalar", lambda e, kv=kv: e.activation(
            sqh[0:64, :], kx[kv][0:64, :], AF.Square), waits=w, inc=Sc)
        for tw in range(TW):
            w = [sqk] + ([pX.rd] if pX.rd else [])
            mtok = kb.op("tensor", lambda e, tw=tw: e.matmul(
                pX.t[0:1, :], ones64[0:64, 0:1], sqh[0:64, tw * 512:(tw + 1) * 512],
                start=True, stop=True), waits=w, inc=T)
            kn_done = kb.op("scalar", lambda e, tw=tw: e.copy(
                kn_sb[0:1, tw * 512:(tw + 1) * 512], pX.t[0:1, :]), waits=[mtok], inc=Sc)
            pX.rd = kn_done
        sq_reader = mtok
        vkm = kb.op("vector", lambda e, kv=kv: e.reduce_max(
            kmaxs[0:1, kv:kv + 1], kn_sb[0:1, :], axis=AX.X), waits=[kn_done], inc=V)
        vkm_prev = vkm
        kones = kb.op("vector", lambda e, kv=kv: e.memset(kx[kv][64:65, :], 1.0),
                      waits=[kv_done], inc=V)
    stab_k = (V, V.n)
    kones_all = (V, V.n)

    wcol_tok = [None, None]
    wcol_use = [None, None]
    esb_free = None
    exp_tok = [None] * TT
    mask_tok = [None] * TT
    bank_exp = {}
    btok_prev = [None]
    vn_prev = [None]
    attn_done = None
    for oc in range(8):
        buf = oc % 2
        w = [wcol_use[buf]] if wcol_use[buf] else []
        wt = kb.op("sync", lambda e, oc=oc, buf=buf: e.dma_start(
            out=wqcol[:, buf, :, :],
            in_=self.wbf["fwq"].rearrange("(ic p) d -> p ic d", p=128)[:, :, oc * 128:(oc + 1) * 128]),
            waits=w + self.prep["fwq"], inc=dwc[buf], inc_by=16)
        qdone = rope_proj(
            lambda buf=buf: [wqcol[:, buf, ic, :] for ic in range(DC)],
            qx[0], qx[1], wt)
        wcol_use[buf] = qdone

        for hh in range(2):
            h = 2 * oc + hh
            kv = h // 4
            qh = qx[hh]
            kh = kx[kv]
            w = [qdone] + ([sq_reader] if sq_reader else [])
            sqq = kb.op("scalar", lambda e, qh=qh: e.activation(
                sqh[0:64, :], qh[0:64, :], AF.Square), waits=w, inc=Sc)
            qn_done = None
            for tw in range(TW):
                w = [sqq] + ([pX.rd] if pX.rd else [])
                mtok = kb.op("tensor", lambda e, tw=tw: e.matmul(
                    pX.t[0:1, :], ones64[0:64, 0:1], sqh[0:64, tw * 512:(tw + 1) * 512],
                    start=True, stop=True), waits=w, inc=T)
                qn_done = kb.op("scalar", lambda e, tw=tw: e.copy(
                    qn_sb[0:1, tw * 512:(tw + 1) * 512], pX.t[0:1, :]),
                    waits=[mtok], inc=Sc)
                pX.rd = qn_done
            sq_reader = mtok
            stab = kb.op("vector", lambda e, qh=qh, kv=kv: e.tensor_scalar(
                qh[64:65, :], qn_sb[0:1, :], kmaxs[0:1, kv:kv + 1], -0.5,
                ALU.add, ALU.mult), waits=[qn_done, stab_k], inc=V)

            for tw in range(TW):
                nkc = 4 * (tw + 1)
                for kc in range(nkc):
                    pb_ = pS[kc % 2]
                    w = [stab, kones_all]
                    bk = bank_exp.get(kc % 2)
                    if bk:
                        w.append(bk)
                    if esb_free:
                        w.append(esb_free)
                    mtok = kb.op("tensor", lambda e, pb_=pb_, kh=kh, qh=qh, kc=kc, tw=tw: e.matmul(
                        pb_.t[:, :], kh[:, kc * 128:(kc + 1) * 128], qh[:, tw * 512:(tw + 1) * 512],
                        start=True, stop=True), waits=w, inc=T)
                    etok = kb.op("scalar", lambda e, pb_=pb_, kc=kc: e.activation(
                        esb[:, kc, :], pb_.t[:, :], AF.Exp, scale=0.125),
                        waits=[mtok], inc=Sc)
                    bank_exp[kc % 2] = etok
                    exp_tok[kc] = etok
                    mask_tok[kc] = None
                    jj = kc - 4 * tw
                    if jj >= 0:
                        mtk = kb.op("vector", lambda e, kc=kc, jj=jj: e.tensor_tensor(
                            esb[:, kc, :], esb[:, kc, :], cmask_sb[:, jj, :], ALU.mult),
                            waits=[etok] + consts_tok, inc=V)
                        kb.drain("vector")
                        mask_tok[kc] = mtk
                pcx = pC if tw % 2 == 0 else pB
                ctx_tok = None
                for kc in range(nkc):
                    w = [mask_tok[kc] or exp_tok[kc], vv_done]
                    if kc == 0 and pcx.rd:
                        w.append(pcx.rd)
                    ctx_tok = kb.op("tensor", lambda e, kc=kc, kv=kv, nkc=nkc, pcx=pcx: e.matmul(
                        pcx.t[0:65, :], vkv[:, kc, kv, :], esb[:, kc, :],
                        start=(kc == 0), stop=(kc == nkc - 1)),
                        waits=w, inc=(T if kc == nkc - 1 else None))
                esb_free = ctx_tok
                w = [ctx_tok] + ([btok_prev[0]] if btok_prev[0] else [])
                kb.op("scalar", lambda e, pcx=pcx: e.activation(rzl[0:1, :], pcx.t[64:65, :], AF.Ln),
                      waits=w)
                kb.drain("scalar")
                vrz = kb.op("scalar", lambda e: e.activation(rz[0:1, :], rzl[0:1, :], AF.Exp,
                                                             scale=-1.0), inc=Sc)
                w = [vrz] + ([pX.rd] if pX.rd else [])
                btok = kb.op("tensor", lambda e: e.matmul(
                    pX.t[0:64, :], self.sb["ones_f"][0:1, 0:64], rz[0:1, :],
                    start=True, stop=True), waits=w, inc=T)
                btok_prev[0] = btok
                vcp = kb.op("vector", lambda e: e.tensor_copy(rzB[:, :], pX.t[0:64, :]),
                            waits=[btok] + ([vn_prev[0]] if vn_prev[0] else []), inc=V)
                pX.rd = vcp
                vn = kb.op("vector", lambda e, h=h, tw=tw, pcx=pcx: e.tensor_tensor(
                    ctxT[64 * (h % 2):64 * (h % 2) + 64, h // 2, tw * 512:(tw + 1) * 512],
                    pcx.t[0:64, :], rzB[:, :], ALU.mult), waits=[vcp], inc=V)
                vn_prev[0] = vn
                pcx.rd = vn
            attn_done = vn
    ctxT_done = attn_done

    fwo_tok = kb.op("sync", lambda e: e.dma_start(
        out=fwo_sb[:], in_=self.wbf["fwo"].rearrange("(ic p) d -> p ic d", p=128)),
        waits=[attn_done, (T, T.n)] + self.prep["fwo"], inc=dwo, inc_by=16)
    osb_use = None
    out_use = None
    for tt in range(TT):
        sl = tt % 2
        w = [x_use[sl]] if x_use[sl] else []
        x_tok[sl] = kb.op("sync", lambda e, b=b, tt=tt, sl=sl: e.dma_start(
            out=x1t[:, sl, :], in_=x1_ap[b, tt * 128:(tt + 1) * 128, :]),
            waits=w, inc=dx[sl], inc_by=16)
        vlnf = kb.op("vector", lambda e, tt=tt, sl=sl: e.tensor_scalar(
            lnf[:], x1t[:, sl, :], negm[:, tt:tt + 1], rstd[:, tt:tt + 1], ALU.add, ALU.mult),
            waits=[x_tok[sl]], inc=V)
        kb.drain("vector")
        ev = None
        for js in range(2):
            ps = pP[js]
            tok = self.mm_ff(ps,
                             [ctxT[:, dc, tt * 128:(tt + 1) * 128] for dc in range(DC)],
                             [fwo_sb[:, dc, js * 512:(js + 1) * 512] for dc in range(DC)],
                             waits=[ctxT_done, fwo_tok] + ([osb_use] if osb_use else []))
            ev = kb.op("vector", lambda e, ps=ps, js=js: e.tensor_tensor(
                osb[:, js * 512:(js + 1) * 512], ps.t[:, :], lnf[:, js * 512:(js + 1) * 512],
                ALU.add), waits=[tok, vlnf], inc=V)
            ps.rd = ev
        kb.drain("vector")
        w = [ev] + ([out_use] if out_use else [])
        vcl = kb.op("vector", lambda e: e.tensor_scalar(
            a2[:], osb[:], 100.0, -100.0, ALU.min, ALU.max), waits=w, inc=V)
        x_use[sl] = vcl
        osb_use = vcl
        kb.drain("vector")
        otok = kb.op("sync", lambda e, b=b, tt=tt: e.dma_start(
            out=rh2_ap[b, tt * 128:(tt + 1) * 128, :], in_=a2[:]),
            waits=[vcl], inc=dO, inc_by=16)
        out_use = otok


Builder.stage_s2 = _stage_s2


def _stage_s3(self, b):
    kb, al, nc = self.kb, self.al, self.nc
    T, V, Sc, dO = (kb.sem[k] for k in ("T", "V", "Sc", "dO"))
    al.begin_stage()
    h2_ap = self.dram["rh2"]
    x1_ap = self.dram["r1"]
    r2_ap = self.dram["r2"]
    dh = [kb.dstream("h0"), kb.dstream("h1")]
    dx = [kb.dstream("x0"), kb.dstream("x1")]
    dwd = kb.dstream("wd0")
    dgu = [kb.dstream("ga"), kb.dstream("gb")]

    GL = 1024
    GT = GL // 128

    hnT = al.stage("hnT", (128, DC, S), BF16)
    hnTf_off = al.stage_off
    hnTf = al.stage("hnTf", (128, DC, GL), F32)
    aT = al.stage_at("aT", (128, FC, GL), BF16, hnTf_off)
    acc = al.stage("acc3", (128, GT, D), F32)
    wd_sb = al.stage("wd_sb", (128, FC, D), BF16)
    wgu = al.stage("wgu", (128, 2, 2, DC, 256), BF16)
    wall = al.stage("wall", (128, TT, 8), F32)
    h2t = al.stage("h2t", (128, 2, D), F32)
    x1t = al.stage("x1t3", (128, 2, D), F32)
    hnf_off3 = al.stage_off
    hnf = al.stage("hnf", (128, D), F32)
    hnt = al.stage("hnt", (128, D), F32)
    silu_scr = al.stage("silu_scr", (128, 512), F32)
    osb = al.stage_at("osb3", (128, D), F32, hnf_off3)
    ln2wB = al.stage("ln2wB", (128, D), F32)
    rs2 = al.stage("rs2", (128, TT), F32)
    stA = al.stage("stA3", (128, 2 * TT), F32)
    sqscr_off3 = al.stage_off
    sqscr = al.stage("sqscr3", (128, D), F32)
    osb2 = al.stage_at("osb3b", (128, D), F32, sqscr_off3)
    rt = al.stage("rt3", (128, 8, 8), F32)
    rt1 = al.stage("rt13", (128, 8), F32)

    ident_f = self.sb["ident_f"]
    pA, pB, pC, pS = self.pA, self.pB, self.pC, [self.pS0, self.pS1]

    bcast_rows(kb, pA, pB, [(ln2wB[:, :], self.sb["ln2_row"])], waits=[self.persist_tok])

    h_tok = [None, None]
    h_use = [None, None]
    hnt_use = None
    pt_free = [None, None]
    rt_done = None
    router_done = None
    vA = None
    sA = None
    for tp in range(0, TT, 2):
        tiles = []
        for gt in (tp, tp + 1):
            sl = gt % 2
            w = [h_use[sl]] if h_use[sl] else []
            h_tok[sl] = kb.op("sync", lambda e, b=b, gt=gt, sl=sl: e.dma_start(
                out=h2t[:, sl, :], in_=h2_ap[b, gt * 128:(gt + 1) * 128, :]),
                waits=w, inc=dh[sl], inc_by=16)
            tiles.append((h2t[:, sl, :], rs2[:, gt:gt + 1], stA[:, 2 * gt:2 * gt + 1],
                          stA[:, 2 * gt + 1:2 * gt + 2], [h_tok[sl]]))
        stoks = emit_rms_stats2(kb, tiles, sqscr[:], self.eps)
        for ti, gt in enumerate((tp, tp + 1)):
            sl = gt % 2
            rtok = stoks[ti]
            vr = kb.op("vector", lambda e, gt=gt, sl=sl: e.tensor_single_scalar(
                hnf[:], h2t[:, sl, :], rs2[:, gt:gt + 1], ALU.mult), waits=[rtok], inc=V)
            h_use[sl] = vr
            kb.drain("vector")
            w = [hnt_use] if hnt_use else []
            vh = kb.op("vector", lambda e: e.tensor_tensor(
                hnt[:], hnf[:], ln2wB[:, :], ALU.mult), waits=w, inc=V)
            kb.drain("vector")
            t8 = gt % GT
            for g in range(2):
                ps = pS[g]
                tw_ = [vh] + (pt_free[g] if pt_free[g] else [])
                ttok = None
                for j in range(4):
                    dc = g * 4 + j
                    ttok = kb.op("tensor", lambda e, ps=ps, j=j, dc=dc: e.transpose(
                        ps.t[:, j * 128:(j + 1) * 128], hnt[:, dc * 128:(dc + 1) * 128],
                        ident_f[:]), waits=(tw_ if j == 0 else []),
                        inc=(T if j == 3 else None))
                vev = None
                for j in range(4):
                    dc = g * 4 + j
                    vev = kb.op("vector", lambda e, ps=ps, j=j, dc=dc, t8=t8: e.tensor_copy(
                        hnTf[:, dc, t8 * 128:(t8 + 1) * 128], ps.t[:, j * 128:(j + 1) * 128]),
                        waits=([ttok] if j == 0 else []), inc=V)
                sev = None
                for j in range(4):
                    dc = g * 4 + j
                    sev = kb.op("scalar", lambda e, j=j, dc=dc, gt=gt, t8=t8: e.copy(
                        hnT[:, dc, gt * 128:(gt + 1) * 128], hnTf[:, dc, t8 * 128:(t8 + 1) * 128]),
                        waits=([(V, kb.sem["V"].n)] if j == 0 else []),
                        inc=(Sc if j == 3 else None))
                pt_free[g] = [vev, sev]
                hnt_use = vev
            vA = (V, V.n)
            sA = (Sc, Sc.n)
            ps = pS[gt % 2]
            w = [vA, sA] + ([ps.rd] if ps.rd else [])
            tok = None
            for ic in range(DC):
                tok = kb.op("tensor", lambda e, ps=ps, ic=ic, t8=t8: e.matmul(
                    ps.t[:, 0:8], hnTf[:, ic, t8 * 128:(t8 + 1) * 128],
                    self.sb["wr3"][:, ic, :], start=(ic == 0), stop=(ic == DC - 1)),
                    waits=(w if ic == 0 else []), inc=(T if ic == DC - 1 else None))
            lg = rt[:, 0, :]
            pexp = rt[:, 1, :]
            pm = rt[:, 2, :]
            keep = rt[:, 3, :]
            wr_ = rt[:, 4, :]
            m1 = rt1[:, 0:1]
            negm1 = rt1[:, 1:2]
            m2 = rt1[:, 2:3]
            den = rt1[:, 3:4]
            rden = rt1[:, 4:5]
            w = [tok] + ([rt_done] if rt_done else [])
            vlg = kb.op("vector", lambda e, ps=ps: e.tensor_copy(lg, ps.t[:, 0:8]),
                        waits=w, inc=V)
            ps.rd = vlg
            kb.drain("vector")
            kb.op("vector", lambda e: e.reduce_max(m1, lg, axis=AX.X))
            kb.drain("vector")
            v2 = kb.op("vector", lambda e: e.tensor_single_scalar(negm1, m1, -1.0, ALU.mult), inc=V)
            kb.drain("vector")
            se = kb.op("scalar", lambda e: e.activation(pexp, lg, AF.Exp, bias=negm1),
                       waits=[v2], inc=Sc)
            kb.op("vector", lambda e: e.tensor_single_scalar(keep, lg, m1, ALU.is_ge))
            kb.drain("vector")
            kb.op("vector", lambda e: e.scalar_tensor_tensor(pm, keep, -1e30, lg, ALU.mult, ALU.add))
            kb.drain("vector")
            kb.op("vector", lambda e: e.reduce_max(m2, pm, axis=AX.X))
            kb.drain("vector")
            kb.op("vector", lambda e: e.tensor_single_scalar(keep, lg, m2, ALU.is_ge))
            kb.drain("vector")
            kb.op("vector", lambda e: e.tensor_tensor(wr_, pexp, keep, ALU.mult), waits=[se])
            kb.drain("vector")
            kb.op("vector", lambda e: e.reduce_sum(den, wr_, axis=AX.X))
            kb.drain("vector")
            kb.op("vector", lambda e: e.reciprocal(rden, den))
            kb.drain("vector")
            rt_done = kb.op("vector", lambda e, gt=gt: e.tensor_single_scalar(
                wall[:, gt, :], wr_, rden, ALU.mult), inc=V)
            kb.drain("vector")
    router_done = rt_done

    wd_tok = None
    wd_free = None
    silu_free = [None]
    gu_tok = [None, None]
    gu_free = [None, None]
    aT_free = None
    accD_done = None
    for gr in range(2):
        go = gr * GL
        acc_free = [[None, None] for _ in range(GT)]
        acc_done = None
        for e_i in range(8):
            w = [wd_free] if wd_free else []
            wd_tok = kb.op("sync", lambda e, e_i=e_i: e.dma_start(
                out=wd_sb[:], in_=self.wbf["wd"][e_i].rearrange("(fc p) d -> p fc d", p=128)),
                waits=w + self.prep["wd"], inc=dwd, inc_by=16)
            a_done = None
            for fc in range(FC):
                buf = (fc // 2) % 2
                if fc % 2 == 0:
                    w = [gu_free[buf]] if gu_free[buf] else []
                    for im, wn in ((0, "wg"), (1, "wu")):
                        gu_tok[buf] = kb.op("sync", lambda e, e_i=e_i, fc=fc, buf=buf, im=im, wn=wn: e.dma_start(
                            out=wgu[:, buf, im, :, :],
                            in_=self.wbf[wn][e_i].rearrange("(ic p) f -> p ic f", p=128)[:, :, fc * 128:(fc + 2) * 128]),
                            waits=(w + self.prep[wn] if im == 0 else []), inc=dgu[buf], inc_by=16)
                hw = slice((fc % 2) * 128, (fc % 2) * 128 + 128)
                for tw2 in range(2):
                    gtok = self.mm_ff(pA,
                                      [wgu[:, buf, 0, ic, hw] for ic in range(DC)],
                                      [hnT[:, ic, go + tw2 * 512:go + (tw2 + 1) * 512] for ic in range(DC)],
                                      waits=[gu_tok[buf], sA])
                    utok = self.mm_ff(pB,
                                      [wgu[:, buf, 1, ic, hw] for ic in range(DC)],
                                      [hnT[:, ic, go + tw2 * 512:go + (tw2 + 1) * 512] for ic in range(DC)],
                                      waits=[gu_tok[buf]])
                    w3 = [gtok] + ([silu_free[0]] if silu_free[0] else [])
                    stok = kb.op("scalar", lambda e: e.activation(
                        silu_scr[:], pA.t[:, :], AF.Silu), waits=w3, inc=Sc)
                    pA.rd = stok
                    w2 = [stok, utok] + ([aT_free] if aT_free else [])
                    vat = kb.op("vector", lambda e, fc=fc, tw2=tw2: e.tensor_tensor(
                        aT[:, fc, tw2 * 512:(tw2 + 1) * 512], silu_scr[:], pB.t[:, :],
                        ALU.mult), waits=w2, inc=V)
                    pB.rd = vat
                    silu_free[0] = vat
                    a_done = vat
                if fc % 2 == 1:
                    gu_free[buf] = utok
            dtok = None
            for ts in range(GT):
                for ds in range(2):
                    pD = pC if (ts * 2 + ds) % 2 == 0 else pS[0]
                    dtok = self.mm_ff(pD,
                                      [aT[:, fc, ts * 128:(ts + 1) * 128] for fc in range(FC)],
                                      waits=[a_done, wd_tok],
                                      rhs_slices=[wd_sb[:, fc, ds * 512:(ds + 1) * 512] for fc in range(FC)])
                    gts = gr * GT + ts
                    if e_i == 0:
                        va = kb.op("vector", lambda e, ts=ts, ds=ds, e_i=e_i, pD=pD, gts=gts: e.tensor_scalar(
                            acc[:, ts, ds * 512:(ds + 1) * 512], pD.t[:, :],
                            wall[:, gts, e_i:e_i + 1], None, ALU.mult),
                            waits=[dtok, router_done] + ([accD_done] if accD_done and gr == 1 else []), inc=V)
                    else:
                        va = kb.op("vector", lambda e, ts=ts, ds=ds, e_i=e_i, pD=pD, gts=gts: e.scalar_tensor_tensor(
                            acc[:, ts, ds * 512:(ds + 1) * 512], pD.t[:, :],
                            wall[:, gts, e_i:e_i + 1], acc[:, ts, ds * 512:(ds + 1) * 512],
                            ALU.mult, ALU.add), waits=[dtok, acc_free[ts][ds]], inc=V)
                    kb.drain("vector")
                    pD.rd = va
                    acc_free[ts][ds] = va
                    acc_done = va
            aT_free = dtok
            wd_free = dtok

        if gr == 0:
            out_use = None
            xv_use = [None, None]
        for t8 in range(GT):
            gt = GT * gr + t8
            sl = t8 % 2
            w = [h_use[sl]] if h_use[sl] else []
            h_tok[sl] = kb.op("sync", lambda e, b=b, gt=gt, sl=sl: e.dma_start(
                out=h2t[:, sl, :], in_=h2_ap[b, gt * 128:(gt + 1) * 128, :]),
                waits=w, inc=dh[sl], inc_by=16)
            x_use_w = [] if xv_use[sl] is None else [xv_use[sl]]
            x_tok = kb.op("sync", lambda e, b=b, gt=gt, sl=sl: e.dma_start(
                out=x1t[:, sl, :], in_=x1_ap[b, gt * 128:(gt + 1) * 128, :]),
                waits=x_use_w, inc=dx[sl], inc_by=16)
            w = [h_tok[sl], acc_done] + ([out_use] if out_use else [])
            v1 = kb.op("vector", lambda e, t8=t8, sl=sl: e.tensor_tensor(
                osb[:], h2t[:, sl, :], acc[:, t8, :], ALU.add), waits=w, inc=V)
            h_use[sl] = v1
            kb.drain("vector")
            kb.op("vector", lambda e: e.tensor_scalar(
                osb2[:], osb[:], 100.0, -100.0, ALU.min, ALU.max))
            kb.drain("vector")
            vout = kb.op("vector", lambda e, sl=sl: e.tensor_tensor(
                osb[:], osb2[:], x1t[:, sl, :], ALU.add), waits=[x_tok], inc=V)
            xv_use[sl] = vout
            accD_done = vout
            kb.drain("vector")
            otok = kb.op("sync", lambda e, b=b, gt=gt: e.dma_start(
                out=r2_ap[b, gt * 128:(gt + 1) * 128, :], in_=osb[:]),
                waits=[vout], inc=dO, inc_by=16)
            out_use = otok


Builder.stage_s3 = _stage_s3


FCH = [(0, 128), (128, 128), (256, 128), (384, 128), (512, 1)]


def _stage_s4(self, b):
    kb, al, nc = self.kb, self.al, self.nc
    T, V, Sc, dO = (kb.sem[k] for k in ("T", "V", "Sc", "dO"))
    al.begin_stage()
    x2_ap = self.dram["r2"]
    r3_ap = self.dram["r3"]
    dx = [kb.dstream("x0"), kb.dstream("x1")]
    dwc = [kb.dstream("wa"), kb.dstream("wb")]
    dbig = kb.dstream("wbig")
    dirf = kb.dstream("wbig2")

    lxT_off = al.stage_off
    lxT = al.stage("lxT", (128, DC, S), BF16)
    r1_off = al.stage_off
    strm = al.stage("strm", (128, DC, S), BF16)
    fre = al.stage("fre", (128, 5, S), BF16)
    fim = al.stage("fim", (128, 5, S), BF16)
    fqh = al.stage("fqh", (128, 2, S), BF16)
    w0 = al.stage("w0s4", (128, S), F32)
    w1 = al.stage("w1s4", (128, S), F32)
    cg0 = al.stage("cg0s4", (128, S), F32)
    cg1 = al.stage("cg1s4", (128, S), F32)

    wcol = al.stage("wcol4", (128, 2, DC, 128), BF16)
    dftc = al.stage("dftc", (128, DC, 513), BF16)
    dfts = al.stage("dfts", (128, DC, 513), BF16)
    irfa = al.stage_at("irfa", (128, 5, D), BF16, lxT_off)
    irfb = al.stage_at("irfb", (128, 5, D), BF16, r1_off)
    x2t_off = al.stage_off
    x2t = al.stage("x2t", (128, 2, D), F32)
    lntile_off4 = al.stage_off
    nw0 = al.stage_at("nw0s4", (128, 512), F32, x2t_off)
    nw1 = al.stage_at("nw1s4", (128, 512), F32, x2t_off + 2048)
    nw2 = al.stage_at("nw2s4", (128, 512), F32, x2t_off + 4096)
    lntile = al.stage("lntile4", (128, 2, D), BF16)
    osb_off4 = al.stage_off
    osb = al.stage("osb4", (128, D), F32)
    lnf = al.stage_at("lnf4", (128, D), F32, lntile_off4)
    negm = al.stage("negm4", (128, TT), F32)
    rstd = al.stage("rstd4", (128, TT), F32)
    stA = al.stage("stA4", (128, 2 * TT), F32)
    sqscr = al.stage_at("sqscr4", (128, D), F32, osb_off4)

    ident_bf = self.sb["ident_bf"]
    pA, pB, pC, pS, pT = self.pA, self.pB, self.pC, [self.pS0, self.pS1], [self.pT0, self.pT1]

    ct1 = kb.op("sync", lambda e: e.dma_start(
        out=dftc[:], in_=self.dram["dft_c"].rearrange("(ic p) f -> p ic f", p=128)),
        waits=[(dbig, dbig.n)], inc=dbig, inc_by=16)
    ct2 = kb.op("sync", lambda e: e.dma_start(
        out=dfts[:], in_=self.dram["dft_s"].rearrange("(ic p) f -> p ic f", p=128)),
        waits=[(dbig, dbig.n)], inc=dbig, inc_by=16)
    dft_tok = ct2

    x_tok = [None, None]
    x_use = [None, None]
    ln_use = [None, None]
    ptA_free = [None, None]
    vA = None
    for tp in range(0, TT, 2):
        tiles = []
        for tt in (tp, tp + 1):
            sl = tt % 2
            w = [x_use[sl]] if x_use[sl] else []
            x_tok[sl] = kb.op("sync", lambda e, b=b, tt=tt, sl=sl: e.dma_start(
                out=x2t[:, sl, :], in_=x2_ap[b, tt * 128:(tt + 1) * 128, :]),
                waits=w, inc=dx[sl], inc_by=16)
            tiles.append((x2t[:, sl, :], negm[:, tt:tt + 1], rstd[:, tt:tt + 1],
                          stA[:, 2 * tt:2 * tt + 1], stA[:, 2 * tt + 1:2 * tt + 2],
                          [x_tok[sl]]))
        stoks = emit_ln_stats2(kb, tiles, sqscr[:], self.eps)
        for ti, tt in enumerate((tp, tp + 1)):
            sl = tt % 2
            w = ([ln_use[sl]] if ln_use[sl] else []) + [stoks[ti]]
            vln = kb.op("vector", lambda e, tt=tt, sl=sl: e.tensor_scalar(
                lntile[:, sl, :], x2t[:, sl, :], negm[:, tt:tt + 1], rstd[:, tt:tt + 1],
                ALU.add, ALU.mult), waits=w, inc=V)
            x_use[sl] = vln
            for g in range(2):
                tw_ = [vln] + ([ptA_free[g]] if ptA_free[g] else [])
                ttok = None
                for j in range(4):
                    dc = g * 4 + j
                    ttok = kb.op("tensor", lambda e, g=g, j=j, dc=dc, sl=sl: e.transpose(
                        pT[g].t[:, j * 128:(j + 1) * 128], lntile[:, sl, dc * 128:(dc + 1) * 128],
                        ident_bf[:]), waits=(tw_ if j == 0 else []),
                        inc=(T if j == 3 else None))
                vev = None
                for j in range(4):
                    dc = g * 4 + j
                    vev = kb.op("vector", lambda e, g=g, j=j, dc=dc, tt=tt: e.tensor_copy(
                        lxT[:, dc, tt * 128:(tt + 1) * 128], pT[g].t[:, j * 128:(j + 1) * 128]),
                        waits=([ttok] if j == 0 else []), inc=(V if j == 3 else None))
                ptA_free[g] = vev
                ln_use[sl] = vev
        vA = ln_use[sl]

    wcol_tok = [None, None]
    wcol_use = [None, None]

    def proj_stream(wname, extra_waits):
        sdone = None
        for oc in range(8):
            buf = oc % 2
            w = [wcol_use[buf]] if wcol_use[buf] else []
            wt = kb.op("sync", lambda e, wname=wname, oc=oc, buf=buf: e.dma_start(
                out=wcol[:, buf, :, :],
                in_=self.wbf[wname].rearrange("(ic p) d -> p ic d", p=128)[:, :, oc * 128:(oc + 1) * 128]),
                waits=w + self.prep[wname], inc=dwc[buf], inc_by=16)
            last_mm = None
            for tw in range(TW):
                ps = pS[tw % 2]
                tok = self.mm_ff(ps,
                                 [wcol[:, buf, ic, :] for ic in range(DC)],
                                 [lxT[:, ic, tw * 512:(tw + 1) * 512] for ic in range(DC)],
                                 waits=[wt, vA] + list(extra_waits))
                last_mm = tok
                sdone = kb.op("scalar", lambda e, ps=ps, oc=oc, tw=tw: e.copy(
                    strm[:, oc, tw * 512:(tw + 1) * 512], ps.t[:, :]), waits=[tok], inc=Sc)
                ps.rd = sdone
            wcol_use[buf] = last_mm
        return sdone

    def dft_group(ps, mat, fc, tw, fw, f0, waits):
        return self.mm_ff(ps,
                          [mat[:, ic, f0:f0 + fw] for ic in range(DC)],
                          [strm[:, ic, tw * 512:(tw + 1) * 512] for ic in range(DC)],
                          waits=waits)

    def normalize_pair(fc, tw, fw, dst_re, dst_im, re_tok, im_tok, s1_extra=()):
        sl5 = slice(tw * 512, (tw + 1) * 512)
        s1 = kb.op("scalar", lambda e, fw=fw: e.activation(
            nw0[0:fw, :], pA.t[0:fw, :], AF.Square),
            waits=[re_tok] + list(s1_extra), inc=Sc)
        s2 = kb.op("scalar", lambda e, fw=fw: e.activation(
            nw1[0:fw, :], pB.t[0:fw, :], AF.Square), waits=[im_tok, s1], inc=Sc)
        v1 = kb.op("vector", lambda e, fw=fw: e.tensor_tensor(
            nw2[0:fw, :], nw0[0:fw, :], nw1[0:fw, :], ALU.add), waits=[s2], inc=V)
        kb.op("scalar", lambda e, fw=fw: e.activation(
            nw0[0:fw, :], nw2[0:fw, :], AF.Ln, bias=self.eps30[0][0:fw, 0:1]),
            waits=[v1, self.eps30[1]])
        kb.drain("scalar")
        s3 = kb.op("scalar", lambda e, fw=fw: e.activation(
            nw2[0:fw, :], nw0[0:fw, :], AF.Exp, scale=-0.5), inc=Sc)
        vr = kb.op("vector", lambda e, fw=fw, dst_re=dst_re, sl5=sl5: e.tensor_tensor(
            dst_re[0:fw, sl5], pA.t[0:fw, :], nw2[0:fw, :], ALU.mult), waits=[s3], inc=V)
        vi = kb.op("vector", lambda e, fw=fw, dst_im=dst_im, sl5=sl5: e.tensor_tensor(
            dst_im[0:fw, sl5], pB.t[0:fw, :], nw2[0:fw, :], ALU.mult), inc=V)
        kb.drain("vector")
        pA.rd = vr
        pB.rd = vi
        return vi

    kproj = proj_stream("awk", [])
    fk_done = None
    for fc, (f0, fw) in enumerate(FCH):
        for tw in range(TW):
            wre = [kproj, dft_tok] + ([pA.rd] if pA.rd else [])
            rt_ = dft_group(pA, dftc, fc, tw, fw, f0, wre)
            wim = [pB.rd] if pB.rd else []
            it_ = dft_group(pB, dfts, fc, tw, fw, f0, wim)
            fk_done = normalize_pair(fc, tw, fw, fre[:, fc, :], fim[:, fc, :], rt_, it_)

    vproj = proj_stream("awv", [fk_done])
    z_done = None
    for fc, (f0, fw) in enumerate(FCH):
        for tw in range(TW):
            sl5 = slice(tw * 512, (tw + 1) * 512)
            wre = [vproj] + ([pA.rd] if pA.rd else [])
            rt_ = dft_group(pA, dftc, fc, tw, fw, f0, wre)
            wim = [pB.rd] if pB.rd else []
            it_ = dft_group(pB, dfts, fc, tw, fw, f0, wim)
            va = kb.op("vector", lambda e, fw=fw, fc=fc, sl5=sl5: e.tensor_tensor(
                nw0[0:fw, :], fre[0:fw, fc, sl5], pA.t[0:fw, :], ALU.mult),
                waits=[rt_, fk_done], inc=V)
            vb = kb.op("vector", lambda e, fw=fw, fc=fc, sl5=sl5: e.tensor_tensor(
                nw1[0:fw, :], fim[0:fw, fc, sl5], pB.t[0:fw, :], ALU.mult),
                waits=[it_], inc=V)
            vc = kb.op("vector", lambda e, fw=fw, fc=fc, sl5=sl5: e.tensor_tensor(
                nw2[0:fw, :], fre[0:fw, fc, sl5], pB.t[0:fw, :], ALU.mult), inc=V)
            pB.rd = vc
            kb.drain("vector")
            vz1 = kb.op("vector", lambda e, fw=fw, fc=fc, sl5=sl5: e.tensor_tensor(
                fre[0:fw, fc, sl5], nw0[0:fw, :], nw1[0:fw, :], ALU.subtract), inc=V)
            kb.drain("vector")
            vd = kb.op("vector", lambda e, fw=fw, fc=fc, sl5=sl5: e.tensor_tensor(
                nw0[0:fw, :], fim[0:fw, fc, sl5], pA.t[0:fw, :], ALU.mult), inc=V)
            pA.rd = vd
            kb.drain("vector")
            z_done = kb.op("vector", lambda e, fw=fw, fc=fc, sl5=sl5: e.tensor_tensor(
                fim[0:fw, fc, sl5], nw2[0:fw, :], nw0[0:fw, :], ALU.add), inc=V)
            kb.drain("vector")

    qproj = proj_stream("awq", [z_done])
    irf_tok = None
    y_done = []
    for fc, (f0, fw) in enumerate(FCH):
        fq_done = None
        fqh_guard = y_done
        for tw in range(TW):
            wre = [qproj, (T, T.n)] if fc == 0 and tw == 0 else [qproj]
            wre = wre + ([pA.rd] if pA.rd else [])
            rt_ = dft_group(pA, dftc, fc, tw, fw, f0, wre)
            wim = [pB.rd] if pB.rd else []
            it_ = dft_group(pB, dfts, fc, tw, fw, f0, wim)
            fq_done = normalize_pair(fc, tw, fw, fqh[:, 0, :], fqh[:, 1, :], rt_, it_,
                                     s1_extra=(fqh_guard if fqh_guard and tw == 0 else []))
        G = kb.sem["G"]
        yprev = y_done
        csd = {}
        for comp, buf_, eng, b0, b1, sem in ((0, fre, "gpsimd", cg0, cg1, G),
                                             (1, fim, "vector", w0, w1, V)):
            w_ = [z_done, fq_done] + (yprev if yprev else [])
            kb.op(eng, lambda e, fw=fw, fc=fc, buf_=buf_, b0=b0: e.tensor_copy(
                b0[0:fw, :], buf_[0:fw, fc, :]), waits=w_)
            kb.drain(eng)
            src, dst = b0, b1
            sstep = 1
            while sstep < S:
                kb.op(eng, lambda e, fw=fw, src=src, dst=dst, sstep=sstep: e.tensor_copy(
                    dst[0:fw, 0:sstep], src[0:fw, 0:sstep]))
                kb.op(eng, lambda e, fw=fw, src=src, dst=dst, sstep=sstep: e.tensor_tensor(
                    dst[0:fw, sstep:S], src[0:fw, sstep:S], src[0:fw, 0:S - sstep], ALU.add))
                kb.drain(eng)
                src, dst = dst, src
                sstep *= 2
            csd[comp] = kb.op(eng, lambda e, fw=fw, fc=fc, buf_=buf_, src=src: e.tensor_copy(
                buf_[0:fw, fc, :], src[0:fw, :]), inc=sem)
            kb.drain(eng)
        t0v = kb.op("vector", lambda e, fw=fw, fc=fc: e.tensor_tensor(
            w0[0:fw, :], fre[0:fw, fc, :], fqh[0:fw, 0, :], ALU.mult),
            waits=[csd[0], csd[1]], inc=V)
        t1v = kb.op("vector", lambda e, fw=fw, fc=fc: e.tensor_tensor(
            w1[0:fw, :], fim[0:fw, fc, :], fqh[0:fw, 1, :], ALU.mult), inc=V)
        t2g = kb.op("gpsimd", lambda e, fw=fw, fc=fc: e.tensor_tensor(
            cg0[0:fw, :], fim[0:fw, fc, :], fqh[0:fw, 0, :], ALU.mult),
            waits=[csd[0], csd[1]], inc=G)
        t3g = kb.op("gpsimd", lambda e, fw=fw, fc=fc: e.tensor_tensor(
            cg1[0:fw, :], fre[0:fw, fc, :], fqh[0:fw, 1, :], ALU.mult), inc=G)
        kb.drain("vector")
        kb.drain("gpsimd")
        yre = kb.op("vector", lambda e, fw=fw, fc=fc: e.tensor_tensor(
            fre[0:fw, fc, :], w0[0:fw, :], w1[0:fw, :], ALU.add), waits=[t3g], inc=V)
        yim = kb.op("gpsimd", lambda e, fw=fw, fc=fc: e.tensor_tensor(
            fim[0:fw, fc, :], cg0[0:fw, :], cg1[0:fw, :], ALU.subtract), waits=[t1v], inc=G)
        kb.drain("vector")
        kb.drain("gpsimd")
        y_done = [yre, yim]

    irf_tok = kb.op("sync", lambda e: e.dma_start(
        out=irfa[:], in_=self.dram["irf_a"].rearrange("(fc p) d -> p fc d", p=128)),
        waits=[qproj, (T, T.n), (dirf, dirf.n)], inc=dirf, inc_by=16)
    irf_tok = kb.op("sync", lambda e: e.dma_start(
        out=irfb[:], in_=self.dram["irf_b"].rearrange("(fc p) d -> p fc d", p=128)),
        waits=[(dirf, dirf.n)], inc=dirf, inc_by=16)

    out_use = None
    osb_use = None
    for tt in range(TT):
        sl = tt % 2
        w = [x_use[sl]] if x_use[sl] else []
        if tt < 2:
            w = w + y_done
        x_tok[sl] = kb.op("sync", lambda e, b=b, tt=tt, sl=sl: e.dma_start(
            out=x2t[:, sl, :], in_=x2_ap[b, tt * 128:(tt + 1) * 128, :]),
            waits=w, inc=dx[sl], inc_by=16)
        vlnf = kb.op("vector", lambda e, tt=tt, sl=sl: e.tensor_scalar(
            lnf[:], x2t[:, sl, :], negm[:, tt:tt + 1], rstd[:, tt:tt + 1], ALU.add, ALU.mult),
            waits=[x_tok[sl]], inc=V)
        kb.drain("vector")
        ev = None
        for ds in range(2):
            ps = pS[ds]
            terms = []
            for fc, (f0, fw) in enumerate(FCH):
                terms.append((fre[0:fw, fc, tt * 128:(tt + 1) * 128],
                              irfa[0:fw, fc, ds * 512:(ds + 1) * 512]))
                terms.append((fim[0:fw, fc, tt * 128:(tt + 1) * 128],
                              irfb[0:fw, fc, ds * 512:(ds + 1) * 512]))
            tok = self.mm_ff(ps, [a_ for a_, b_ in terms], [b_ for a_, b_ in terms],
                             waits=y_done + [irf_tok] + ([osb_use] if osb_use else []))
            ev = kb.op("vector", lambda e, ps=ps, ds=ds: e.tensor_tensor(
                osb[:, ds * 512:(ds + 1) * 512], ps.t[:, :], lnf[:, ds * 512:(ds + 1) * 512],
                ALU.add), waits=[tok, vlnf], inc=V)
            ps.rd = ev
        kb.drain("vector")
        vout = kb.op("vector", lambda e, sl=sl: e.tensor_tensor(
            osb[:], osb[:], x2t[:, sl, :], ALU.add), waits=[ev], inc=V)
        x_use[sl] = vout
        osb_use = vout
        kb.drain("vector")
        otok = kb.op("sync", lambda e, b=b, tt=tt: e.dma_start(
            out=r3_ap[b, tt * 128:(tt + 1) * 128, :], in_=osb[:]),
            waits=[vout], inc=dO, inc_by=16)
        osb_use = otok
        out_use = otok


Builder.stage_s4 = _stage_s4


def _stage_s5(self, b):
    kb, al, nc = self.kb, self.al, self.nc
    T, V, Sc, dO = (kb.sem[k] for k in ("T", "V", "Sc", "dO"))
    al.begin_stage()
    x3_ap = self.dram["r3"]
    out_ap = self.dram["out"]
    dx = [kb.dstream("x0"), kb.dstream("x1")]
    dwd = kb.dstream("wd0")
    dgu = [kb.dstream("ga"), kb.dstream("gb")]

    GL = 1024
    GT = GL // 128

    hnT = al.stage("hnT5", (128, DC, S), BF16)
    hnTf_off = al.stage_off
    hnTf = al.stage("hnTf5", (128, DC, GL), F32)
    aT = al.stage_at("aT5", (128, FC, GL), BF16, hnTf_off)
    acc = al.stage("acc5", (128, GT, D), F32)
    wd_sb = al.stage("wd_sb5", (128, FC, D), BF16)
    wgu = al.stage("wgu5", (128, 2, 2, DC, 256), BF16)
    wall = al.stage("wall5", (128, TT, 8), F32)
    h2t = al.stage("x3t5", (128, 2, D), F32)
    hnt_off5 = al.stage_off
    hnt = al.stage("hnt5", (128, D), F32)
    silu_scr = al.stage("silu5", (128, 512), F32)
    osb = al.stage_at("osb5", (128, D), F32, hnt_off5)
    negm = al.stage("negm5", (128, TT), F32)
    rstd = al.stage("rstd5", (128, TT), F32)
    stA = al.stage("stA5", (128, 2 * TT), F32)
    sqscr = al.stage("sqscr5", (128, D), F32)
    rt = al.stage("rt5", (128, 8, 8), F32)
    rt1 = al.stage("rt15", (128, 8), F32)

    ident_f = self.sb["ident_f"]
    pA, pB, pC, pS = self.pA, self.pB, self.pC, [self.pS0, self.pS1]

    experts = [("sw1t", "sw3t", "sw2t", 0, "sh0"), ("sw1t", "sw3t", "sw2t", 1, "sh1")] + \
              [("w1t", "w3t", "w2t", e_i, "rt") for e_i in range(8)]

    h_tok = [None, None]
    h_use = [None, None]
    hnt_use = None
    pt_free = [None, None]
    rt_done = None
    vA = None
    sA = None
    for tp in range(0, TT, 2):
        tiles = []
        for gt in (tp, tp + 1):
            sl = gt % 2
            w = [h_use[sl]] if h_use[sl] else []
            h_tok[sl] = kb.op("sync", lambda e, b=b, gt=gt, sl=sl: e.dma_start(
                out=h2t[:, sl, :], in_=x3_ap[b, gt * 128:(gt + 1) * 128, :]),
                waits=w, inc=dx[sl], inc_by=16)
            tiles.append((h2t[:, sl, :], negm[:, gt:gt + 1], rstd[:, gt:gt + 1],
                          stA[:, 2 * gt:2 * gt + 1], stA[:, 2 * gt + 1:2 * gt + 2],
                          [h_tok[sl]]))
        stoks = emit_ln_stats2(kb, tiles, sqscr[:], self.eps)
        for ti, gt in enumerate((tp, tp + 1)):
            sl = gt % 2
            stok = stoks[ti]
            w = ([hnt_use] if hnt_use else []) + [stok]
            vh = kb.op("vector", lambda e, gt=gt, sl=sl: e.tensor_scalar(
                hnt[:], h2t[:, sl, :], negm[:, gt:gt + 1], rstd[:, gt:gt + 1],
                ALU.add, ALU.mult), waits=w, inc=V)
            h_use[sl] = vh
            kb.drain("vector")
            t8 = gt % GT
            for g in range(2):
                ps = pS[g]
                tw_ = [vh] + (pt_free[g] if pt_free[g] else [])
                ttok = None
                for j in range(4):
                    dc = g * 4 + j
                    ttok = kb.op("tensor", lambda e, ps=ps, j=j, dc=dc: e.transpose(
                        ps.t[:, j * 128:(j + 1) * 128], hnt[:, dc * 128:(dc + 1) * 128],
                        ident_f[:]), waits=(tw_ if j == 0 else []),
                        inc=(T if j == 3 else None))
                vev = None
                for j in range(4):
                    dc = g * 4 + j
                    vev = kb.op("vector", lambda e, ps=ps, j=j, dc=dc, t8=t8: e.tensor_copy(
                        hnTf[:, dc, t8 * 128:(t8 + 1) * 128], ps.t[:, j * 128:(j + 1) * 128]),
                        waits=([ttok] if j == 0 else []), inc=V)
                sev = None
                for j in range(4):
                    dc = g * 4 + j
                    sev = kb.op("scalar", lambda e, j=j, dc=dc, gt=gt, t8=t8: e.copy(
                        hnT[:, dc, gt * 128:(gt + 1) * 128], hnTf[:, dc, t8 * 128:(t8 + 1) * 128]),
                        waits=([(V, kb.sem["V"].n)] if j == 0 else []),
                        inc=(Sc if j == 3 else None))
                pt_free[g] = [vev, sev]
                hnt_use = vev
            vA = (V, V.n)
            sA = (Sc, Sc.n)
            ps = pS[gt % 2]
            w = [vA, sA] + ([ps.rd] if ps.rd else [])
            for ic in range(DC):
                kb.op("tensor", lambda e, ps=ps, ic=ic, t8=t8: e.matmul(
                    ps.t[:, 0:8], hnTf[:, ic, t8 * 128:(t8 + 1) * 128],
                    self.sb["rw3"][:, ic, :], start=(ic == 0), stop=False),
                    waits=(w if ic == 0 else []))
            tok = kb.op("tensor", lambda e, ps=ps: e.matmul(
                ps.t[:, 0:8], self.sb["ones_f"][0:1, :], self.sb["ebias_row"][0:1, :],
                start=False, stop=True), inc=T)
            lg = rt[:, 0, :]
            pexp = rt[:, 1, :]
            pm = rt[:, 2, :]
            keep = rt[:, 3, :]
            wr_ = rt[:, 4, :]
            m1 = rt1[:, 0:1]
            negm1 = rt1[:, 1:2]
            m2 = rt1[:, 2:3]
            den = rt1[:, 3:4]
            rden = rt1[:, 4:5]
            w = [tok] + ([rt_done] if rt_done else [])
            vlg = kb.op("vector", lambda e, ps=ps: e.tensor_copy(lg, ps.t[:, 0:8]),
                        waits=w, inc=V)
            ps.rd = vlg
            kb.drain("vector")
            kb.op("vector", lambda e: e.reduce_max(m1, lg, axis=AX.X))
            kb.drain("vector")
            v2 = kb.op("vector", lambda e: e.tensor_single_scalar(negm1, m1, -1.0, ALU.mult), inc=V)
            kb.drain("vector")
            se = kb.op("scalar", lambda e: e.activation(pexp, lg, AF.Exp, bias=negm1),
                       waits=[v2], inc=Sc)
            kb.op("vector", lambda e: e.tensor_single_scalar(keep, lg, m1, ALU.is_ge))
            kb.drain("vector")
            kb.op("vector", lambda e: e.scalar_tensor_tensor(pm, keep, -1e30, lg, ALU.mult, ALU.add))
            kb.drain("vector")
            kb.op("vector", lambda e: e.reduce_max(m2, pm, axis=AX.X))
            kb.drain("vector")
            kb.op("vector", lambda e: e.tensor_single_scalar(keep, lg, m2, ALU.is_ge))
            kb.drain("vector")
            kb.op("vector", lambda e: e.tensor_tensor(wr_, pexp, keep, ALU.mult), waits=[se])
            kb.drain("vector")
            kb.op("vector", lambda e: e.reduce_sum(den, wr_, axis=AX.X))
            kb.drain("vector")
            kb.op("vector", lambda e: e.reciprocal(rden, den))
            kb.drain("vector")
            rt_done = kb.op("vector", lambda e, gt=gt: e.tensor_single_scalar(
                wall[:, gt, :], wr_, rden, ALU.mult), inc=V)
            kb.drain("vector")
    router_done = rt_done

    wd_tok = None
    wd_free = None
    silu_free = [None]
    gu_tok = [None, None]
    gu_free = [None, None]
    aT_free = None
    accD_done = None
    for gr in range(2):
        go = gr * GL
        acc_free = [[None, None] for _ in range(GT)]
        acc_done = None
        for ei, (n1, n3, n2, e_i, kind) in enumerate(experts):
            w = [wd_free] if wd_free else []
            wd_tok = kb.op("sync", lambda e, n2=n2, e_i=e_i: e.dma_start(
                out=wd_sb[:], in_=self.wbf[n2][e_i].rearrange("(fc p) d -> p fc d", p=128)),
                waits=w + self.prep[n2], inc=dwd, inc_by=16)
            a_done = None
            for fc in range(FC):
                buf = (fc // 2) % 2
                if fc % 2 == 0:
                    w = [gu_free[buf]] if gu_free[buf] else []
                    for im, wn in ((0, n1), (1, n3)):
                        gu_tok[buf] = kb.op("sync", lambda e, e_i=e_i, fc=fc, buf=buf, im=im, wn=wn: e.dma_start(
                            out=wgu[:, buf, im, :, :],
                            in_=self.wbf[wn][e_i].rearrange("(ic p) f -> p ic f", p=128)[:, :, fc * 128:(fc + 2) * 128]),
                            waits=(w + self.prep[wn] if im == 0 else []), inc=dgu[buf], inc_by=16)
                hw = slice((fc % 2) * 128, (fc % 2) * 128 + 128)
                for tw2 in range(2):
                    gtok = self.mm_ff(pA,
                                      [wgu[:, buf, 0, ic, hw] for ic in range(DC)],
                                      [hnT[:, ic, go + tw2 * 512:go + (tw2 + 1) * 512] for ic in range(DC)],
                                      waits=[gu_tok[buf], sA])
                    utok = self.mm_ff(pB,
                                      [wgu[:, buf, 1, ic, hw] for ic in range(DC)],
                                      [hnT[:, ic, go + tw2 * 512:go + (tw2 + 1) * 512] for ic in range(DC)],
                                      waits=[gu_tok[buf]])
                    w3 = [gtok] + ([silu_free[0]] if silu_free[0] else [])
                    stok = kb.op("scalar", lambda e: e.activation(
                        silu_scr[:], pA.t[:, :], AF.Silu), waits=w3, inc=Sc)
                    pA.rd = stok
                    w2 = [stok, utok] + ([aT_free] if aT_free else [])
                    vat = kb.op("vector", lambda e, fc=fc, tw2=tw2: e.tensor_tensor(
                        aT[:, fc, tw2 * 512:(tw2 + 1) * 512], silu_scr[:], pB.t[:, :],
                        ALU.mult), waits=w2, inc=V)
                    pB.rd = vat
                    silu_free[0] = vat
                    a_done = vat
                if fc % 2 == 1:
                    gu_free[buf] = utok
            dtok = None
            for ts in range(GT):
                for ds in range(2):
                    pD = pC if (ts * 2 + ds) % 2 == 0 else pS[0]
                    dtok = self.mm_ff(pD,
                                      [aT[:, fc, ts * 128:(ts + 1) * 128] for fc in range(FC)],
                                      waits=[a_done, wd_tok],
                                      rhs_slices=[wd_sb[:, fc, ds * 512:(ds + 1) * 512] for fc in range(FC)])
                    gts = gr * GT + ts
                    if kind == "sh0":
                        va = kb.op("vector", lambda e, ts=ts, ds=ds, pD=pD: e.tensor_copy(
                            acc[:, ts, ds * 512:(ds + 1) * 512], pD.t[:, :]),
                            waits=[dtok] + ([accD_done] if accD_done and gr == 1 else []), inc=V)
                    elif kind == "sh1":
                        va = kb.op("vector", lambda e, ts=ts, ds=ds, pD=pD: e.tensor_tensor(
                            acc[:, ts, ds * 512:(ds + 1) * 512], pD.t[:, :],
                            acc[:, ts, ds * 512:(ds + 1) * 512], ALU.add),
                            waits=[dtok, acc_free[ts][ds]], inc=V)
                    else:
                        va = kb.op("vector", lambda e, ts=ts, ds=ds, e_i=e_i, pD=pD, gts=gts: e.scalar_tensor_tensor(
                            acc[:, ts, ds * 512:(ds + 1) * 512], pD.t[:, :],
                            wall[:, gts, e_i:e_i + 1], acc[:, ts, ds * 512:(ds + 1) * 512],
                            ALU.mult, ALU.add), waits=[dtok, acc_free[ts][ds], router_done], inc=V)
                    kb.drain("vector")
                    pD.rd = va
                    acc_free[ts][ds] = va
                    acc_done = va
            aT_free = dtok
            wd_free = dtok

        if gr == 0:
            out_use = None
        for t8 in range(GT):
            gt = GT * gr + t8
            sl = t8 % 2
            w = [h_use[sl]] if h_use[sl] else []
            h_tok[sl] = kb.op("sync", lambda e, b=b, gt=gt, sl=sl: e.dma_start(
                out=h2t[:, sl, :], in_=x3_ap[b, gt * 128:(gt + 1) * 128, :]),
                waits=w, inc=dx[sl], inc_by=16)
            w = [h_tok[sl], acc_done] + ([out_use] if out_use else [])
            vout = kb.op("vector", lambda e, t8=t8, sl=sl: e.tensor_tensor(
                osb[:], h2t[:, sl, :], acc[:, t8, :], ALU.add), waits=w, inc=V)
            h_use[sl] = vout
            accD_done = vout
            kb.drain("vector")
            otok = kb.op("sync", lambda e, b=b, gt=gt: e.dma_start(
                out=out_ap[b, gt * 128:(gt + 1) * 128, :], in_=osb[:]),
                waits=[vout], inc=dO, inc_by=16)
            out_use = otok


Builder.stage_s5 = _stage_s5


def _stage_s1_v2(self, b):
    kb, al, nc = self.kb, self.al, self.nc
    T, V, Sc, dO = (kb.sem[k] for k in ("T", "V", "Sc", "dO"))
    al.begin_stage()
    x_ap = self.dram["x_in"]
    r1_ap = self.dram["r1"]
    dx = [kb.dstream("x0"), kb.dstream("x1")]
    dwc = [kb.dstream("wa"), kb.dstream("wb")]
    dbig = kb.dstream("wbig")

    lnT = al.stage("lnT", (128, DC, S), BF16)
    ctxT = al.stage("ctxT1", (128, DC, S), BF16)
    qkv_off = al.stage_off
    qT = al.stage("qT1", (128, DC, 512), BF16)
    kT = al.stage("kT1", (128, DC, 512), BF16)
    vT = al.stage("vT1", (128, DC, 512), BF16)
    wo_sb = al.stage_at("wo_sb1", (128, DC, D), BF16, qkv_off)
    esb_off = al.stage_off
    esb = al.stage("esb1", (128, TT, 512), BF16)
    bB = al.stage_at("bB1", (128, 3, D), F32, esb_off)
    qp2 = [al.stage(f"qp1_{i}", (65, S), BF16) for i in range(2)]
    kp2 = [al.stage(f"kp1_{i}", (65, S), BF16) for i in range(2)]
    vpT2 = [al.stage(f"vpT1_{i}", (64, S), BF16) for i in range(2)]
    ctxp = al.stage("ctxp1", (64, S), BF16)
    vp2 = [al.stage(f"vp1_{i}", (128, TT, 65), BF16) for i in range(2)]
    wcol = al.stage("wcol1", (128, 2, 3, DC, 128), BF16)
    x0t = al.stage("x0t1", (128, 2, D), F32)
    lntile_off = al.stage_off
    lntile = al.stage("lntile1", (128, 2, D), BF16)
    osb_off1 = al.stage_off
    osb = al.stage("osb1", (128, D), F32)
    lnf_off1 = al.stage_off
    lnf = al.stage("lnf1", (128, D), F32)
    a2 = al.stage("a21", (128, D), F32)
    negm = al.stage("negm1", (128, TT), F32)
    rstd = al.stage("rstd1", (128, TT), F32)
    stA = al.stage("stA1", (128, 2 * TT), F32)
    sqscr_off = al.stage_off
    sqscr = al.stage("sqscr1", (128, D), F32)
    sqh = al.stage_at("sqh1", (64, S), BF16, lntile_off)
    qn_sb = al.stage("qn_sb1", (1, S), BF16)
    kn_sb = al.stage_at("kn_sb1", (1, S), BF16, sqscr_off)
    kmax = al.stage("kmax1", (1, 2), F32)
    rz = al.stage_at("rz1", (1, 512), F32, osb_off1)
    rzB = al.stage_at("rzB1", (64, 512), F32, osb_off1 + 2048)
    rzl = al.stage_at("rzl1", (1, 512), F32, lnf_off1)

    ident_bf = self.sb["ident_bf"]
    ones64 = self.sb["ones64"]
    pA, pB, pS, pC, pX, pT = self.pA, self.pB, [self.pS0, self.pS1], self.pC, self.pX, [self.pT0, self.pT1]

    x0_tok = [None, None]
    x0_use = [None, None]
    ln_use = [None, None]
    ptA_free = [None, None]
    vA = None
    for tp in range(0, TT, 2):
        tiles = []
        for tt in (tp, tp + 1):
            sl = tt % 2
            w = [x0_use[sl]] if x0_use[sl] else []
            x0_tok[sl] = kb.op("sync", lambda e, b=b, tt=tt, sl=sl: e.dma_start(
                out=x0t[:, sl, :], in_=x_ap[b, tt * 128:(tt + 1) * 128, :]),
                waits=w, inc=dx[sl], inc_by=16)
            tiles.append((x0t[:, sl, :], negm[:, tt:tt + 1], rstd[:, tt:tt + 1],
                          stA[:, 2 * tt:2 * tt + 1], stA[:, 2 * tt + 1:2 * tt + 2],
                          [x0_tok[sl]]))
        stoks = emit_ln_stats2(kb, tiles, sqscr[:], self.eps)
        for ti, tt in enumerate((tp, tp + 1)):
            sl = tt % 2
            w = ([ln_use[sl]] if ln_use[sl] else []) + [stoks[ti]]
            vln = kb.op("vector", lambda e, tt=tt, sl=sl: e.tensor_scalar(
                lntile[:, sl, :], x0t[:, sl, :], negm[:, tt:tt + 1], rstd[:, tt:tt + 1],
                ALU.add, ALU.mult), waits=w, inc=V)
            x0_use[sl] = vln
            for g in range(2):
                tw_ = [vln] + ([ptA_free[g]] if ptA_free[g] else [])
                ttok = None
                for j in range(4):
                    dc = g * 4 + j
                    ttok = kb.op("tensor", lambda e, g=g, j=j, dc=dc, sl=sl: e.transpose(
                        pT[g].t[:, j * 128:(j + 1) * 128], lntile[:, sl, dc * 128:(dc + 1) * 128],
                        ident_bf[:]), waits=(tw_ if j == 0 else []),
                        inc=(T if j == 3 else None))
                vev = None
                for j in range(4):
                    dc = g * 4 + j
                    vev = kb.op("vector", lambda e, g=g, j=j, dc=dc, tt=tt: e.tensor_copy(
                        lnT[:, dc, tt * 128:(tt + 1) * 128], pT[g].t[:, j * 128:(j + 1) * 128]),
                        waits=([ttok] if j == 0 else []), inc=(V if j == 3 else None))
                ptA_free[g] = vev
                ln_use[sl] = vev
        vA = ln_use[sl]

    vset = kb.op("vector", lambda e: (e.memset(vp2[0][:, :, 64:65], 1.0),
                                      e.memset(vp2[1][:, :, 64:65], 1.0),
                                      e.memset(kp2[0][64:65, :], 1.0),
                                      e.memset(kp2[1][64:65, :], 1.0))[-1], inc=V)

    wcol_tok = [None, None]
    wcol_use = [None, None]
    qkv_read = None
    vkm_prev = [None]
    vn_prev = [None]
    st = {"sq_reader": None}
    pp_read = [None, None]
    esb_free = None
    exp_tok = [None] * TT
    bank_exp = {}
    sq_reader = None
    attn_done = None
    sct = None
    vpt_rd = {}
    for qd in range(4):
        ev_tok = {}
        for oc in range(8):
            buf = (qd * 8 + oc) % 2
            w = [wcol_use[buf]] if wcol_use[buf] else []
            for m3 in range(3):
                wcol_tok[buf] = kb.op("sync", lambda e, oc=oc, buf=buf, m3=m3: e.dma_start(
                    out=wcol[:, buf, m3, :, :],
                    in_=self.wbf["wqkv"].rearrange("(ic p) d -> p ic d", p=128)[:, :, m3 * D + oc * 128: m3 * D + (oc + 1) * 128]),
                    waits=(w + self.prep["wqkv"] if m3 == 0 else []), inc=dwc[buf], inc_by=16)
            wready = wcol_tok[buf]
            for im, (dst, brow) in enumerate(((qT, "bq3"), (kT, "bk3"), (vT, "bv3"))):
                ps = pA if im % 2 == 0 else pB
                tok = self.mm_ff(
                    ps,
                    [wcol[:, buf, im, ic, :] for ic in range(DC)],
                    [lnT[:, ic, qd * 512:(qd + 1) * 512] for ic in range(DC)],
                    waits=[wready, vA] + (qkv_read if qkv_read and oc == 0 else []))
                ev_tok[(im, oc)] = kb.op("scalar", lambda e, ps=ps, dst=dst, oc=oc, brow=brow: e.activation(
                    dst[:, oc, :], ps.t[:, :], AF.Identity,
                    bias=self.sb[brow][:, oc:oc + 1]), waits=[tok], inc=Sc)
                ps.rd = ev_tok[(im, oc)]
            wcol_use[buf] = tok
        qkv_done = ev_tok[(2, 7)]

        def emit_prep(gw):
            par = gw % 2
            qp, kp, vpT, vp = qp2[par], kp2[par], vpT2[par], vp2[par]
            wl = (gw % 4) * 128
            bldv = None
            bldg = None
            for im, (src, dst) in enumerate(((qT, qp), (kT, kp), (vT, vpT))):
                for pc in range(2):
                    eng = "vector" if (im * 2 + pc) % 2 == 0 else "gpsimd"
                    sem = V if eng == "vector" else kb.sem["G"]
                    w = [qkv_done] + ([pp_read[par]] if pp_read[par] and im == 0 and pc == 0 else [])
                    tok = kb.op(eng, lambda e, src=src, dst=dst, pc=pc, wl=wl: e.tensor_copy(
                        dst.rearrange("p (sx c2 two) -> p two c2 sx", two=2, c2=8)[0:64, pc, :, :],
                        src[64 * pc:64 * pc + 64, :, wl:wl + 128]),
                        waits=w, inc=sem)
                    if eng == "vector":
                        bldv = tok
                    else:
                        bldg = tok
            kb.drain("vector")
            kb.drain("gpsimd")
            bld_done = [bldv, bldg]
            vv_done = None
            for g in range(2):
                w = bld_done + ([vpt_rd[(par, g)]] if vpt_rd.get((par, g)) else [])
                ttok = None
                for j in range(8):
                    kc = g * 8 + j
                    ttok = kb.op("tensor", lambda e, g=g, j=j, kc=kc, vpT=vpT: e.transpose(
                        pT[g].t[:, j * 64:(j + 1) * 64], vpT[0:64, kc * 128:(kc + 1) * 128],
                        ident_bf[0:64, 0:64]), waits=(w if j == 0 else []),
                        inc=(T if j == 7 else None))
                vv = None
                for j in range(8):
                    kc = g * 8 + j
                    vv = kb.op("vector", lambda e, g=g, j=j, kc=kc, vp=vp: e.tensor_copy(
                        vp[:, kc, 0:64], pT[g].t[:, j * 64:(j + 1) * 64]),
                        waits=([ttok, vset] if j == 0 else []), inc=(V if j == 7 else None))
                vpt_rd[(par, g)] = vv
                vv_done = vv
            w = bld_done + ([st["sq_reader"]] if st["sq_reader"] else [])
            sqq = kb.op("scalar", lambda e, qp=qp: e.activation(
                sqh[0:64, :], qp[0:64, :], AF.Square), waits=w, inc=Sc)
            qn_done = None
            mtok = None
            for tw in range(TW):
                w = [sqq] + ([pX.rd] if pX.rd else [])
                mtok = kb.op("tensor", lambda e, tw=tw: e.matmul(
                    pX.t[0:1, :], ones64[0:64, 0:1], sqh[0:64, tw * 512:(tw + 1) * 512],
                    start=True, stop=True), waits=w, inc=T)
                qn_done = kb.op("scalar", lambda e, tw=tw: e.copy(
                    qn_sb[0:1, tw * 512:(tw + 1) * 512], pX.t[0:1, :]),
                    waits=[mtok], inc=Sc)
                pX.rd = qn_done
            sqk = kb.op("scalar", lambda e, kp=kp: e.activation(
                sqh[0:64, :], kp[0:64, :], AF.Square),
                waits=[qn_done, mtok] + ([vkm_prev[0]] if vkm_prev[0] else []), inc=Sc)
            kn_done = None
            for tw in range(TW):
                w = [sqk] + ([pX.rd] if pX.rd else [])
                mtok = kb.op("tensor", lambda e, tw=tw: e.matmul(
                    pX.t[0:1, :], ones64[0:64, 0:1], sqh[0:64, tw * 512:(tw + 1) * 512],
                    start=True, stop=True), waits=w, inc=T)
                kn_done = kb.op("scalar", lambda e, tw=tw: e.copy(
                    kn_sb[0:1, tw * 512:(tw + 1) * 512], pX.t[0:1, :]), waits=[mtok], inc=Sc)
                pX.rd = kn_done
            st["sq_reader"] = mtok
            vkm = kb.op("vector", lambda e: e.reduce_max(kmax[0:1, 0:1], kn_sb[0:1, :], axis=AX.X),
                        waits=[kn_done], inc=V)
            vkm_prev[0] = vkm
            kb.drain("vector")
            stab = kb.op("vector", lambda e, qp=qp: e.tensor_scalar(
                qp[64:65, :], qn_sb[0:1, :], kmax[0:1, 0:1], -0.5, ALU.add, ALU.mult),
                waits=[vkm], inc=V)
            return {"stab": stab, "vv_done": vv_done, "par": par}

        def emit_attn(gw, prep, next_prep_gw):
            nonlocal esb_free, attn_done, sct
            par = prep["par"]
            qp, kp, vp = qp2[par], kp2[par], vp2[par]
            stab = prep["stab"]
            vv_done = prep["vv_done"]
            s0 = (gw // 4) * 512 + (gw % 4) * 128
            pSC = [pS[0], pS[1], pA]
            pCX = [pC, pB]
            btok_prev = [None]
            nprep = None
            sc_last = None
            for tw in range(TW):
                pcx = pCX[tw % 2]
                for kc in range(TT):
                    pb_ = pSC[kc % 3]
                    w = [stab, vset]
                    if bank_exp.get(kc % 3):
                        w.append(bank_exp[kc % 3])
                    if esb_free:
                        w.append(esb_free)
                    mtok = kb.op("tensor", lambda e, pb_=pb_, kc=kc, tw=tw, kp=kp, qp=qp: e.matmul(
                        pb_.t[:, :], kp[:, kc * 128:(kc + 1) * 128], qp[:, tw * 512:(tw + 1) * 512],
                        start=True, stop=True), waits=w, inc=T)
                    sc_last = mtok
                    etok = kb.op("scalar", lambda e, pb_=pb_, kc=kc: e.activation(
                        esb[:, kc, :], pb_.t[:, :], AF.Exp, scale=0.125),
                        waits=[mtok], inc=Sc)
                    bank_exp[kc % 3] = etok
                    exp_tok[kc] = etok
                ctx_tok = None
                for kc in range(TT):
                    w = [exp_tok[kc], vv_done]
                    if kc == 0 and pcx.rd:
                        w.append(pcx.rd)
                    ctx_tok = kb.op("tensor", lambda e, kc=kc, vp=vp, pcx=pcx: e.matmul(
                        pcx.t[0:65, :], vp[:, kc, :], esb[:, kc, :],
                        start=(kc == 0), stop=(kc == TT - 1)),
                        waits=w, inc=(T if kc == TT - 1 else None))
                esb_free = ctx_tok
                w = [ctx_tok] + ([btok_prev[0]] if btok_prev[0] else [])
                kb.op("scalar", lambda e, pcx=pcx: e.activation(rzl[0:1, :], pcx.t[64:65, :], AF.Ln),
                      waits=w)
                kb.drain("scalar")
                vrz = kb.op("scalar", lambda e: e.activation(rz[0:1, :], rzl[0:1, :], AF.Exp,
                                                             scale=-1.0), inc=Sc)
                w = [vrz] + ([pX.rd] if pX.rd else [])
                btok = kb.op("tensor", lambda e: e.matmul(
                    pX.t[0:64, :], self.sb["ones_f"][0:1, 0:64], rz[0:1, :],
                    start=True, stop=True), waits=w, inc=T)
                btok_prev[0] = btok
                vcp = kb.op("vector", lambda e: e.tensor_copy(rzB[:, :], pX.t[0:64, :]),
                            waits=[btok] + ([vn_prev[0]] if vn_prev[0] else []), inc=V)
                pX.rd = vcp
                w = [vcp] + (sct if sct and tw == 0 else [])
                vn = kb.op("vector", lambda e, tw=tw, pcx=pcx: e.tensor_tensor(
                    ctxp[0:64, tw * 512:(tw + 1) * 512], pcx.t[0:64, :], rzB[:, :], ALU.mult),
                    waits=w, inc=V)
                vn_prev[0] = vn
                pcx.rd = vn
                if tw == 0 and next_prep_gw is not None:
                    nprep = emit_prep(next_prep_gw)
            kb.drain("vector")
            sct_v = kb.op("vector", lambda e, s0=s0: e.tensor_copy(
                ctxT[0:64, :, s0:s0 + 128],
                ctxp.rearrange("p (sx c2 two) -> p two c2 sx", two=2, c2=8)[0:64, 0, :, :]),
                inc=V)
            sct_g = kb.op("gpsimd", lambda e, s0=s0: e.tensor_copy(
                ctxT[64:128, :, s0:s0 + 128],
                ctxp.rearrange("p (sx c2 two) -> p two c2 sx", two=2, c2=8)[0:64, 1, :, :]),
                waits=[vn_prev[0]], inc=kb.sem["G"])
            kb.drain("vector")
            kb.drain("gpsimd")
            sct = [sct_v, sct_g]
            pp_read[par] = sc_last
            attn_done = sct
            return nprep

        pending = emit_prep(qd * 4)
        for wi in range(4):
            gw = qd * 4 + wi
            nxt = gw + 1 if wi < 3 else None
            nprep = emit_attn(gw, pending, nxt)
            pending = nprep
        qkv_read = attn_done

    wo_tok = kb.op("sync", lambda e: e.dma_start(
        out=wo_sb[:], in_=self.wbf["wo"].rearrange("(ic p) d -> p ic d", p=128)),
        waits=attn_done + [(T, T.n), (Sc, Sc.n)] + self.prep["wo"], inc=dbig, inc_by=16)
    bcast_rows(kb, pA, pB,
               [(bB[:, 0, :], self.sb["bo_row"]),
                (bB[:, 1, :], self.sb["gamma_row"]),
                (bB[:, 2, :], self.sb["beta_row"])],
               waits=[self.persist_tok, (T, T.n), (Sc, Sc.n)] + attn_done)
    osb_use = None
    for tt in range(TT):
        sl = tt % 2
        w = [x0_use[sl]] if x0_use[sl] else []
        x0_tok[sl] = kb.op("sync", lambda e, b=b, tt=tt, sl=sl: e.dma_start(
            out=x0t[:, sl, :], in_=x_ap[b, tt * 128:(tt + 1) * 128, :]),
            waits=w, inc=dx[sl], inc_by=16)
        ev = {}
        for js in range(2):
            ps = pA if js == 0 else pB
            tok = self.mm_ff(
                ps,
                [ctxT[:, dc, tt * 128:(tt + 1) * 128] for dc in range(DC)],
                [wo_sb[:, dc, js * 512:(js + 1) * 512] for dc in range(DC)],
                waits=attn_done + [wo_tok] + ([osb_use] if osb_use else []))
            ev[js] = kb.op("vector", lambda e, ps=ps, js=js: e.tensor_tensor(
                osb[:, js * 512:(js + 1) * 512], ps.t[:, :], bB[:, 0, js * 512:(js + 1) * 512],
                ALU.add), waits=[tok], inc=V)
            ps.rd = ev[js]
        vlnf = kb.op("vector", lambda e, tt=tt, sl=sl: e.tensor_scalar(
            lnf[:], x0t[:, sl, :], negm[:, tt:tt + 1], rstd[:, tt:tt + 1], ALU.add, ALU.mult),
            waits=[x0_tok[sl], ev[1]], inc=V)
        kb.drain("vector")
        vsum = kb.op("vector", lambda e: e.tensor_tensor(osb[:], osb[:], lnf[:], ALU.add), inc=V)
        kb.drain("vector")
        emit_ln_stats(kb, osb[:], stA[:, 2 * tt:2 * tt + 1], stA[:, 2 * tt + 1:2 * tt + 2],
                      a2[:, 0:1], a2[:, 1:2], sqscr[:], self.eps, xwaits=[vsum])
        kb.op("vector", lambda e, tt=tt: e.tensor_scalar(
            a2[:], osb[:], stA[:, 2 * tt:2 * tt + 1], stA[:, 2 * tt + 1:2 * tt + 2],
            ALU.add, ALU.mult), waits=[(Sc, Sc.n)])
        kb.drain("vector")
        kb.op("vector", lambda e: e.tensor_tensor(lnf[:], a2[:], bB[:, 1, :], ALU.mult))
        kb.drain("vector")
        kb.op("vector", lambda e: e.tensor_tensor(a2[:], lnf[:], bB[:, 2, :], ALU.add))
        kb.drain("vector")
        vout = kb.op("vector", lambda e, sl=sl: e.tensor_tensor(
            osb[:], a2[:], x0t[:, sl, :], ALU.add), inc=V)
        x0_use[sl] = vout
        kb.drain("vector")
        otok = kb.op("sync", lambda e, b=b, tt=tt: e.dma_start(
            out=r1_ap[b, tt * 128:(tt + 1) * 128, :], in_=osb[:]),
            waits=[vout], inc=dO, inc_by=16)
        osb_use = otok


Builder.stage_s1 = _stage_s1_v2
